# version 16
# baseline (speedup 1.0000x reference)
import contextlib

import numpy as np
import concourse.bass as bass
import concourse.mybir as mybir
from concourse.bass_utils import run_bass_kernel_spmd

F32 = mybir.dt.float32
BF16 = mybir.dt.bfloat16
I8 = mybir.dt.int8
AF = mybir.ActivationFunctionType
ALU = mybir.AluOpType

D = 2048
T = 2048
H = 16
NK = D // 128
NC = 8192 // 128
SM_SCALE = float(1.0 / np.sqrt(2048.0))
EPS = 1e-5


class Ctr:
    def __init__(self, sem, step):
        self.sem = sem
        self.v = 0
        self.step = step


def build_nc(upto="full"):
    nc = bass.Bass("TRN2", target_bir_lowering=False)

    xT = nc.dram_tensor("xT", [D, T], F32, kind="ExternalInput").ap()
    wq_h = nc.dram_tensor("wq_h", [H, 128, D], F32, kind="ExternalInput").ap()
    wk_h = nc.dram_tensor("wk_h", [H, 128, D], F32, kind="ExternalInput").ap()
    wv = nc.dram_tensor("wv", [D, D], F32, kind="ExternalInput").ap()
    wo_m = nc.dram_tensor("wo_m", [NK, 128, D], F32, kind="ExternalInput").ap()
    wgu_m = nc.dram_tensor("wgu_m", [NC, 2, 128, D], I8,
                           kind="ExternalInput").ap()
    wd_m = nc.dram_tensor("wd_m", [NK, 128, 8192], I8, kind="ExternalInput").ap()
    sg2d = nc.dram_tensor("sg2d", [128, NC], F32, kind="ExternalInput").ap()
    su2d = nc.dram_tensor("su2d", [128, NC], F32, kind="ExternalInput").ap()
    sd2d = nc.dram_tensor("sd2d", [128, NK], F32, kind="ExternalInput").ap()
    tri_in = nc.dram_tensor("tri_in", [128, 128], F32, kind="ExternalInput").ap()

    out = nc.dram_tensor("out", [D, T], F32, kind="ExternalOutput").ap()

    x1_scr = nc.dram_tensor("x1_scr", [D, T], F32, kind="Internal").ap()
    wo_scr = nc.dram_tensor("wo_scr", [NK, 128, D], BF16, kind="Internal").ap()
    attn_scr = nc.dram_tensor("attn_scr", [H, 128, T], BF16, kind="Internal").ap()

    with contextlib.ExitStack() as es:
        tri = es.enter_context(nc.sbuf_tensor("tri", [128, 128], F32))
        ones_st = es.enter_context(nc.sbuf_tensor("ones_st", [128, 128], BF16))
        ones1 = es.enter_context(nc.sbuf_tensor("ones1", [128, 1], BF16))
        onesk1 = es.enter_context(nc.sbuf_tensor("onesk1", [1, 128], BF16))
        eps_t = es.enter_context(nc.sbuf_tensor("eps_t", [128, 1], F32))
        sg_sb = es.enter_context(nc.sbuf_tensor("sg_sb", [128, NC], F32))
        su_sb = es.enter_context(nc.sbuf_tensor("su_sb", [128, NC], F32))
        sd_sb = es.enter_context(nc.sbuf_tensor("sd_sb", [128, NK], F32))

        sGP = es.enter_context(nc.semaphore("sGP"))
        sSY = es.enter_context(nc.semaphore("sSY"))
        sPE = es.enter_context(nc.semaphore("sPE"))
        sDV = es.enter_context(nc.semaphore("sDV"))
        sAC = es.enter_context(nc.semaphore("sAC"))
        sAC16 = es.enter_context(nc.semaphore("sAC16"))
        sPL = es.enter_context(nc.semaphore("sPL"))

        cGP, cSY = Ctr(sGP, 16), Ctr(sSY, 16)
        cPE, cDV, cAC = Ctr(sPE, 1), Ctr(sDV, 1), Ctr(sAC, 1)
        cAC16, cPL = Ctr(sAC16, 16), Ctr(sPL, 1)
        ALLC = [cGP, cSY, cPE, cDV, cAC, cAC16, cPL]

        block = es.enter_context(nc.Block())

        ops = {"gp": [], "sy": [], "pe": [], "dv": [], "ac": []}

        def flush():
            if ops["gp"]:
                block.gpsimd(lambda e, fns=tuple(ops["gp"]): [f(e) for f in fns])
            if ops["sy"]:
                block.sync(lambda e, fns=tuple(ops["sy"]): [f(e) for f in fns])
            if ops["pe"]:
                block.tensor(lambda e, fns=tuple(ops["pe"]): [f(e) for f in fns])
            if ops["dv"]:
                block.vector(lambda e, fns=tuple(ops["dv"]): [f(e) for f in fns])
            if ops["ac"]:
                block.scalar(lambda e, fns=tuple(ops["ac"]): [f(e) for f in fns])
            for k in ops:
                ops[k] = []

        def _run(eng, waits, make, inc):
            for (sem, v) in waits:
                if v > 0:
                    eng.wait_ge(sem, v)
            ins = make(eng)
            if inc is not None:
                ins.then_inc(inc[0], inc[1])

        def em(engname, make, waits=(), ctr=None):
            inc = None
            if ctr is not None:
                ctr.v += ctr.step
                inc = (ctr.sem, ctr.step)
            ops[engname].append(
                lambda eng, w=tuple(waits), m=make, i=inc: _run(eng, w, m, i))

        def wt(engname, *waits):
            ops[engname].append(
                lambda eng, w=tuple(waits): [eng.wait_ge(s, v)
                                             for (s, v) in w if v > 0])

        def barrier():
            snap = [(c.sem, c.v) for c in ALLC]
            for e in ("gp", "sy", "pe", "dv", "ac"):
                wt(e, *snap)

        def dma(engname, ctr, dst, src, waits=()):
            em(engname, lambda e, d=dst, s=src: e.dma_start(out=d, in_=s),
               waits=waits, ctr=ctr)

        def warm(psum_t, rhs_ap, *waits):
            em("pe", lambda e, p=psum_t[0:64, 0:64], r=rhs_ap:
               e.matmul(p, ones_st[:, 0:64], r, start=True, stop=True,
                        skip_group_check=True), waits=list(waits))

        def mm(psum, lhsT, rhs, start, stop, waits=(), ctr=None):
            em("pe", lambda e, p=psum, l=lhsT, r=rhs, s0=start, s1=stop:
               e.matmul(p, l, r, start=s0, stop=s1, skip_group_check=True),
               waits=waits, ctr=ctr)

        def stats_finish(ps_sum, ps_sq, m_b, var_b, nmsq, w_each, pe_done):
            wt("dv", (sPE, pe_done))
            for tc in range(len(ps_sum)):
                sl = slice(tc * w_each, (tc + 1) * w_each)
                em("dv", lambda e, o=m_b[:, sl], i=ps_sum[tc][:]:
                   e.tensor_scalar(o, i, 1.0 / D, None, ALU.mult))
                em("dv", lambda e, o=var_b[:, sl], i=ps_sq[tc][:]:
                   e.tensor_scalar(o, i, 1.0 / D, None, ALU.mult))
            em("dv", lambda e, o=nmsq[:], a=m_b[:]:
               e.scalar_tensor_tensor(o, a, -1.0, a, ALU.mult, ALU.mult))
            em("dv", lambda e, o=var_b[:], a=nmsq[:]:
               e.tensor_tensor(o, o, a, ALU.add), ctr=cDV)
            wt("ac", (sDV, cDV.v))
            em("ac", lambda e, o=var_b[:]:
               e.activation(o, o, AF.Ln, bias=eps_t[:]))
            em("ac", lambda e, o=var_b[:]:
               e.activation(o, o, AF.Exp, scale=-0.5), ctr=cAC)
            wt("dv", (sAC, cAC.v))
            em("dv", lambda e, o=m_b[:], b=var_b[:]:
               e.tensor_tensor(o, o, b, ALU.mult), ctr=cDV)
            return cDV.v

        dma("sy", cSY, tri[:], tri_in)
        dma("sy", cSY, sg_sb[:], sg2d)
        dma("sy", cSY, su_sb[:], su2d)
        dma("sy", cSY, sd_sb[:], sd2d)
        consts_ld = cSY.v
        em("dv", lambda e: e.memset(ones_st[:], 1.0))
        em("dv", lambda e: e.memset(ones1[:], 1.0))
        em("dv", lambda e: e.memset(onesk1[:], 1.0))
        em("dv", lambda e: e.memset(eps_t[:], EPS), ctr=cDV)
        consts_dv = cDV.v

        p123 = es.enter_context(contextlib.ExitStack())
        xn1 = p123.enter_context(nc.sbuf_tensor("xn1", [128, NK, T], BF16))
        v_tm = p123.enter_context(nc.sbuf_tensor("v_tm", [128, NK, D], BF16))
        wqb = [p123.enter_context(nc.sbuf_tensor(f"wqb{i}", [128, D], BF16))
               for i in range(2)]
        wkb = [p123.enter_context(nc.sbuf_tensor(f"wkb{i}", [128, D], BF16))
               for i in range(2)]
        wostg = [p123.enter_context(
            nc.sbuf_tensor(f"wostg{i}", [128, D], BF16)) for i in range(2)]
        wv_ld = {}
        qk_ld = {}

        with contextlib.ExitStack() as p1:
            xsq = [p1.enter_context(nc.sbuf_tensor(f"xsq{i}", [128, T], BF16))
                   for i in range(2)]
            xf = [p1.enter_context(nc.sbuf_tensor(f"xf{i}", [128, T], F32))
                  for i in range(2)]
            m_b = p1.enter_context(nc.sbuf_tensor("m_b", [128, T], F32))
            var_b = p1.enter_context(nc.sbuf_tensor("var_b", [128, T], F32))
            ps_sum = [p1.enter_context(
                nc.psum_tensor(f"ps_sum{i}", [128, 512], F32)) for i in range(4)]
            ps_sq = [p1.enter_context(
                nc.psum_tensor(f"ps_sq{i}", [128, 512], F32)) for i in range(4)]

            cp_guard = {}
            ld_gp = {}
            for k in range(8):
                dma("gp", cGP, xn1[:, k, :], xT[k * 128:(k + 1) * 128, :])
                ld_gp[k] = cGP.v
            ld_sy = {}

            ready = {}
            for k in range(8):
                ready[k] = (sGP, ld_gp[k])
            for k in range(8, NK):
                lw = []
                if k >= 10:
                    lw.append((sDV, cp_guard[k - 2]))
                dma("ac", cAC16, xf[k % 2][:], xT[k * 128:(k + 1) * 128, :],
                    waits=lw)
                ld_sy[k] = cAC16.v
                em("dv", lambda e, o=xn1[:, k, :], i=xf[k % 2][:]:
                   e.tensor_copy(o, i), waits=[(sAC16, ld_sy[k])], ctr=cDV)
                cp_guard[k] = cDV.v
                ready[k] = (sDV, cDV.v)

            pe_k = []
            for k in range(NK):
                wts = [ready[k], (sDV, consts_dv)]
                if k >= 2:
                    wts.append((sPE, pe_k[k - 2]))
                em("ac", lambda e, o=xsq[k % 2][:], i=xn1[:, k, :]:
                   e.activation(o, i, AF.Square), waits=wts, ctr=cAC)
                wt("pe", ready[k], (sAC, cAC.v))
                for tc in range(4):
                    sl = slice(tc * 512, (tc + 1) * 512)
                    mm(ps_sum[tc][:], ones_st[:], xn1[:, k, sl],
                       start=(k == 0), stop=(k == NK - 1))
                    mm(ps_sq[tc][:], ones_st[:], xsq[k % 2][:, sl],
                       start=(k == 0), stop=(k == NK - 1),
                       ctr=(cPE if tc == 3 else None))
                pe_k.append(cPE.v)

            ln1_ready = stats_finish(ps_sum, ps_sq, m_b, var_b, xsq[0], 512,
                                     cPE.v)
            r_ready_ac = cAC.v

            ap_mark = []
            last_cp = cDV.v
            POOLK = {3, 6, 9, 12, 15}
            for k in range(NK):
                wts = []
                if k < 2:
                    wts.append((sDV, last_cp))
                if k >= 2:
                    wts.append(ap_mark[k - 2])
                dma("sy", cSY, xf[k % 2][:], xT[k * 128:(k + 1) * 128, :],
                    waits=wts)
                eng = "gp" if k in POOLK else "dv"
                ctr = cPL if k in POOLK else cDV
                wt(eng, (sSY, cSY.v), (sAC, r_ready_ac), (sDV, ln1_ready))
                em(eng, lambda e, b=xf[k % 2][:], r=var_b[:]:
                   e.tensor_tensor(b, b, r, ALU.mult))
                em(eng, lambda e, o=xn1[:, k, :], b=xf[k % 2][:], m=m_b[:]:
                   e.tensor_tensor(o, b, m, ALU.subtract), ctr=ctr)
                ap_mark.append((ctr.sem, ctr.v))
                warm(ps_sum[0], xsq[0][:, 0:64], ap_mark[-1])
            flush()
        barrier()

        if upto == "ln1":
            for k in range(NK):
                dma("gp", cGP, out[k * 128:(k + 1) * 128, :], xn1[:, k, :])
            _finalize(ops, wt, barrier, flush, ALLC)
            return nc

        with contextlib.ExitStack() as p2:
            wvb = [p2.enter_context(
                nc.sbuf_tensor(f"wvb{i}", [128, NK, 512], BF16))
                for i in range(2)]
            ps_v = [p2.enter_context(
                nc.psum_tensor(f"ps_v{i}", [128, 512], F32)) for i in range(2)]
            evac = []
            vq_mark = []
            for q in range(4):
                cols = slice(q * 512, (q + 1) * 512)
                src2 = wv[:, cols].rearrange("(k p) n -> p k n", p=128)
                lw = [] if q < 2 else [(sPE, vq_mark[q - 2])]
                dma("gp", cGP, wvb[q % 2][:], src2, waits=lw)
                wld = cGP.v
                for st in range(NK):
                    j = q * NK + st
                    par = j % 2
                    wts = [(sGP, wld)]
                    if j >= 2:
                        wts.append((sAC, evac[j - 2]))
                    wt("pe", *wts)
                    for k in range(NK):
                        mm(ps_v[par][:],
                           xn1[:, k, st * 128:(st + 1) * 128],
                           wvb[q % 2][:, k, :],
                           start=(k == 0), stop=(k == NK - 1),
                           ctr=(cPE if k == NK - 1 else None))
                    dst = v_tm[:, st, q * 512:(q + 1) * 512]
                    em("ac", lambda e, o=dst, i=ps_v[par][:]:
                       e.activation(o, i, AF.Copy),
                       waits=[(sPE, cPE.v)], ctr=cAC)
                    evac.append(cAC.v)
                vq_mark.append(cPE.v)
            for h2 in range(2):
                dma("gp", cGP, wqb[h2][:], wq_h[h2])
                dma("gp", cGP, wkb[h2][:], wk_h[h2])
                qk_ld[h2] = cGP.v
            flush()
        barrier()

        with contextlib.ExitStack() as p3:
            qh = p3.enter_context(nc.sbuf_tensor("qh", [128, T], BF16))
            kh = p3.enter_context(nc.sbuf_tensor("kh", [128, T], BF16))
            at = p3.enter_context(nc.sbuf_tensor("at", [128, 16, 512], BF16))
            attn_h = [p3.enter_context(
                nc.sbuf_tensor(f"attn_h{i}", [128, T], BF16)) for i in range(2)]
            zt = p3.enter_context(nc.sbuf_tensor("zt", [1, 512], F32))
            suf_sb = p3.enter_context(nc.sbuf_tensor("suf_sb", [128, 4], F32))
            zinv_bf = p3.enter_context(nc.sbuf_tensor("zinv_bf", [1, 512], BF16))
            zb_sb = p3.enter_context(nc.sbuf_tensor("zb_sb", [128, 512], BF16))
            ps_qk = [p3.enter_context(
                nc.psum_tensor(f"ps_qk{i}", [128, 512], F32)) for i in range(2)]
            ps_s = [p3.enter_context(
                nc.psum_tensor(f"ps_s{i}", [128, 512], F32)) for i in range(2)]
            ps_o = [p3.enter_context(
                nc.psum_tensor(f"ps_o{i}", [128, 512], F32)) for i in range(2)]
            ps_z = p3.enter_context(nc.psum_tensor("ps_z", [1, 512], F32))
            ps_zb = p3.enter_context(nc.psum_tensor("ps_zb", [128, 512], F32))

            qk_mm_done = {}
            scores_last = {-1: 0}
            exp_all = [0, 0]
            qk_evac = [0, 0]
            at_free = 0
            o_evac = [0, 0]
            z_ts_prev = 0
            ln_done_prev = 0
            zbmm_prev = 0
            attn_store = [0, 0]
            suf_read = 0
            suf_dv_prev = 0

            def emit_tail(pend):
                nonlocal zbmm_prev
                (o_par_p, q_sl_p, h_p, zinv_mark, qc_p, sufdv) = pend
                wt("pe", (sAC, zinv_mark), (sDV, o_evac[1 - o_par_p]))
                mm(ps_zb[:], onesk1[:], zinv_bf[:], start=True, stop=True,
                   ctr=cPE)
                zbmm_prev = cPE.v
                wts = [(sPE, zbmm_prev), (sSY, attn_store[h_p % 2])]
                if sufdv:
                    wts.append((sDV, sufdv))
                wt("dv", *wts)
                em("dv", lambda e, o=zb_sb[:], i=ps_zb[:]: e.tensor_copy(o, i))
                sufop = suf_sb[:, qc_p:qc_p + 1] if qc_p < 3 else 0.0
                em("dv", lambda e, o=attn_h[h_p % 2][:, q_sl_p],
                   i=ps_o[o_par_p][:], z2=zb_sb[:], s3=sufop:
                   e.scalar_tensor_tensor(o, i, s3, z2, ALU.add, ALU.mult),
                   ctr=cDV)
                o_evac[o_par_p] = cDV.v

            wo_stg_mark = [0, 0]
            for h in range(H):
                if h in qk_ld:
                    wld = qk_ld[h]
                else:
                    hw = [(sPE, qk_mm_done[h - 2])]
                    dma("gp", cGP, wqb[h % 2][:], wq_h[h], waits=hw)
                    dma("gp", cGP, wkb[h % 2][:], wk_h[h])
                    wld = cGP.v
                sw = [] if wo_stg_mark[h % 2] == 0 else                     [(sSY, wo_stg_mark[h % 2])]
                dma("gp", cGP, wostg[h % 2][:], wo_m[h], waits=sw)
                dma("sy", cSY, wo_scr[h], wostg[h % 2][:],
                    waits=[(sGP, cGP.v)])
                wo_stg_mark[h % 2] = cSY.v

                for which, (wbuf, dst) in enumerate(
                        ((wqb, qh), (wkb, kh))):
                    for tc in range(4):
                        j = which * 4 + tc
                        par = j % 2
                        wts = [(sGP, wld)]
                        if qk_evac[par]:
                            wts.append((sAC, qk_evac[par]))
                        if which == 0 and tc == 0 and suf_dv_prev:
                            wts.append((sDV, suf_dv_prev))
                        wt("pe", *wts)
                        for k in range(NK):
                            mm(ps_qk[par][:],
                               wqb[h % 2][:, k * 128:(k + 1) * 128] if which == 0
                               else wkb[h % 2][:, k * 128:(k + 1) * 128],
                               xn1[:, k, tc * 512:(tc + 1) * 512],
                               start=(k == 0), stop=(k == NK - 1),
                               ctr=(cPE if k == NK - 1 else None))
                        ewts = [(sPE, cPE.v), (sPE, scores_last[h - 1])]
                        em("ac", lambda e,
                           o=dst[:, tc * 512:(tc + 1) * 512],
                           i=ps_qk[par][:]: e.activation(o, i, AF.Copy),
                           waits=ewts, ctr=cAC)
                        qk_evac[par] = cAC.v
                qk_mm_done[h] = cPE.v
                qk_ready = cAC.v

                hs0 = slice(h * 128, (h + 1) * 128)
                wt("pe", (sAC, qk_evac[0]), (sAC, qk_evac[1]),
                   (sDV, suf_read))
                for c in range(4, 16):
                    mm(ps_qk[0][:, c - 4:c - 3], v_tm[:, c, hs0], ones1[:],
                       start=True, stop=True,
                       ctr=(cPE if c == 15 else None))
                wt("dv", (sPE, cPE.v))
                for j in range(3):
                    em("dv", lambda e, o=suf_sb[:, j:j + 1],
                       i=ps_qk[0][:, 4 * j:12]:
                       e.reduce_sum(o, i, axis=mybir.AxisListType.X),
                       ctr=(cDV if j == 2 else None))
                suf_dv = cDV.v
                suf_dv_prev = suf_dv

                pend = None
                for qc in range(4):
                    c_hi = 4 * (qc + 1)
                    q_sl = slice(qc * 512, (qc + 1) * 512)
                    o_par = qc % 2
                    hs = slice(h * 128, (h + 1) * 128)
                    expm = {}
                    msm = {}

                    def emit_av(cav):
                        avw = [(sAC, expm[cav])] if cav in expm else []
                        if cav in msm:
                            avw.append((sDV, msm[cav]))
                        if cav == 0:
                            avw.append((sDV, o_evac[o_par]))
                        if avw:
                            wt("pe", *avw)
                        mm(ps_o[o_par][:], v_tm[:, cav, hs], at[:, cav, :],
                           start=(cav == 0), stop=(cav == c_hi - 1),
                           ctr=(cPE if cav == c_hi - 1 else None))

                    def emit_z(cz):
                        if cz == 0:
                            wt("pe", (sDV, z_ts_prev))
                        mm(ps_z[:], ones1[:], at[:, cz, :], start=(cz == 0),
                           stop=(cz == c_hi - 1),
                           ctr=(cPE if cz == c_hi - 1 else None))

                    for c in range(c_hi):
                        par = c % 2
                        wts = [(sAC, exp_all[par])]
                        if qc == 0 and c == 0:
                            wts.append((sAC, qk_ready))
                        wt("pe", *wts)
                        mm(ps_s[par][:], kh[:, c * 128:(c + 1) * 128],
                           qh[:, q_sl], start=True, stop=True, ctr=cPE)
                        sc = cPE.v
                        if c == 1 and pend is not None:
                            emit_tail(pend)
                            pend = None
                        band = c >= 4 * qc
                        dmark = 0
                        j0 = 0
                        if band:
                            j0 = c * 128 - qc * 512
                            em("dv", lambda e,
                               p=ps_s[par][:, j0:j0 + 128], t2=tri[:]:
                               e.tensor_tensor(p, p, t2, ALU.mult),
                               waits=[(sPE, sc), (sSY, consts_ld)], ctr=cDV)
                            dmark = cDV.v
                            if j0 > 0:
                                em("dv", lambda e, o=at[:, c, 0:j0]:
                                   e.memset(o, 1.0),
                                   waits=[(sPE, at_free)], ctr=cDV)
                                msm[c] = cDV.v
                        ewts = [(sPE, sc), (sPE, at_free)]
                        if dmark:
                            ewts.append((sDV, dmark))
                        em("ac", lambda e, o=at[:, c, j0:512],
                           i=ps_s[par][:, j0:512]:
                           e.activation(o, i, AF.Exp, scale=SM_SCALE),
                           waits=ewts, ctr=cAC)
                        exp_all[par] = cAC.v
                        expm[c] = cAC.v
                        if c >= 2:
                            emit_av(c - 2)
                            emit_z(c - 2)
                    for cav in range(c_hi - 2, c_hi):
                        emit_av(cav)
                        emit_z(cav)
                    z_done = cPE.v
                    at_free = z_done
                    cnt = float(2048 - c_hi * 128)
                    wt("dv", (sPE, z_done), (sAC, ln_done_prev))
                    em("dv", lambda e, o=zt[:], i=ps_z[:], c2=cnt:
                       e.tensor_scalar(o, i, c2, None, ALU.add), ctr=cDV)
                    z_ts_prev = cDV.v
                    wt("ac", (sDV, z_ts_prev), (sPE, zbmm_prev))
                    em("ac", lambda e, o=zt[:]: e.activation(o, o, AF.Ln))
                    em("ac", lambda e, o=zinv_bf[:], i=zt[:]:
                       e.activation(o, i, AF.Exp, scale=-1.0), ctr=cAC)
                    ln_done_prev = cAC.v
                    pend = (o_par, q_sl, h, cAC.v, qc, suf_dv)
                emit_tail(pend)
                pend = None
                suf_read = o_evac[1]
                scores_last[h] = cPE.v
                dma("sy", cSY, attn_scr[h], attn_h[h % 2][:],
                    waits=[(sDV, cDV.v)])
                attn_store[h % 2] = cSY.v
            flush()
        barrier()

        if upto == "attn":
            for hh in range(H):
                dma("gp", cGP, out[hh * 128:(hh + 1) * 128, :], attn_scr[hh])
            _finalize(ops, wt, barrier, flush, ALLC)
            return nc

        p123.close()

        p45 = es.enter_context(contextlib.ExitStack())
        r5_all = p45.enter_context(nc.sbuf_tensor("r5_all", [128, T], BF16))
        mrb5_all = p45.enter_context(nc.sbuf_tensor("mrb5_all", [128, T], BF16))

        with contextlib.ExitStack() as p4:
            attn_sb = p4.enter_context(
                nc.sbuf_tensor("attn_sb", [128, NK, T], BF16))
            y_half = p4.enter_context(
                nc.sbuf_tensor("y_half", [128, NK, 1024], F32))
            wob = [p4.enter_context(nc.sbuf_tensor(f"wob{i}", [128, D], BF16))
                   for i in range(2)]
            yb = [p4.enter_context(nc.sbuf_tensor(f"yb{i}", [128, 512], BF16))
                  for i in range(2)]
            ysq = [p4.enter_context(nc.sbuf_tensor(f"ysq{i}", [128, 512], BF16))
                   for i in range(2)]
            xf2 = [p4.enter_context(nc.sbuf_tensor(f"xf2{i}", [128, 1024], F32))
                   for i in range(2)]
            x1b = [p4.enter_context(
                nc.sbuf_tensor(f"x1b{i}", [128, 1024], BF16)) for i in range(2)]
            x1sq = [p4.enter_context(
                nc.sbuf_tensor(f"x1sq{i}", [128, 1024], BF16)) for i in range(2)]
            x1t = [p4.enter_context(nc.sbuf_tensor(f"x1t{i}", [128, 1024], F32))
                   for i in range(2)]
            m_b4 = p4.enter_context(nc.sbuf_tensor("m_b4", [128, 1024], F32))
            var_b4 = p4.enter_context(nc.sbuf_tensor("var_b4", [128, 1024], F32))
            nmsq4 = p4.enter_context(nc.sbuf_tensor("nmsq4", [128, 1024], F32))
            ps_y = [p4.enter_context(
                nc.psum_tensor(f"ps_y{i}", [128, 512], F32)) for i in range(2)]
            ps_s1 = [p4.enter_context(
                nc.psum_tensor(f"ps_s1{i}", [128, 512], F32)) for i in range(2)]
            ps_q1 = [p4.enter_context(
                nc.psum_tensor(f"ps_q1{i}", [128, 512], F32)) for i in range(2)]


            half_src = attn_scr.rearrange("h p t -> p h t")
            ald_cb = []
            for cb in range(4):
                cs = slice(cb * 512, (cb + 1) * 512)
                ceng, cctr2 = (("sy", cSY), ("ac", cAC16), ("gp", cGP),
                               ("sy", cSY))[cb]
                dma(ceng, cctr2, attn_sb[:, :, cs], half_src[:, :, cs])
                ald_cb.append((cctr2.sem, cctr2.v))

            y_evac = [0, 0]
            stats_mark = [0, 0]
            wo_mark = {}
            apm_all = {}
            apm = [0, 0]
            x1dma = [(None, 0), (None, 0)]
            x1stats_mark = [0, 0]

            def flush_ystats(p):
                (ptc, pm2, dvm, acm, extra) = p
                wt("pe", (sDV, dvm), (sAC, acm), *extra)
                mm(ps_s1[ptc][:], ones_st[:], yb[ptc][:],
                   start=(pm2 == 0), stop=(pm2 == NK - 1))
                mm(ps_q1[ptc][:], ones_st[:], ysq[ptc][:],
                   start=(pm2 == 0), stop=(pm2 == NK - 1), ctr=cPE)
                stats_mark[ptc] = cPE.v

            for half in range(2):
                base = half * 1024
                prev_stats_fin = cDV.v
                pend_st = None
                for m2 in range(NK):
                    hw = []
                    if (half, m2) != (0, 0):
                        pm2 = (half * NK + m2) - 2
                        if pm2 >= 0:
                            hw.append((sPE, wo_mark.get(pm2, 0)))
                    weng, wctr = (("gp", cGP), ("sy", cSY), ("ac", cAC16))[m2 % 3]
                    dma(weng, wctr, wob[m2 % 2][:], wo_scr[m2], waits=hw)
                    wldt = (wctr.sem, wctr.v)
                    for tc in range(2):
                        j = m2 * 2 + tc
                        par = j % 2
                        wts = [wldt, ald_cb[half * 2 + tc]]
                        if y_evac[par]:
                            wts.append((sAC, y_evac[par]))
                        wt("pe", *wts)
                        for k in range(NK):
                            mm(ps_y[par][:],
                               wob[m2 % 2][:, k * 128:(k + 1) * 128],
                               attn_sb[:, k, base + tc * 512:
                                       base + (tc + 1) * 512],
                               start=(k == 0), stop=(k == NK - 1),
                               ctr=(cPE if k == NK - 1 else None))
                        y_sl = y_half[:, m2, tc * 512:(tc + 1) * 512]
                        ew = [(sPE, cPE.v)]
                        if half == 1:
                            ew.append(apm_all[m2])
                        em("ac", lambda e, o=y_sl, i=ps_y[par][:]:
                           e.activation(o, i, AF.Copy), waits=ew, ctr=cAC)
                        y_evac[par] = cAC.v
                        sw = [(sAC, y_evac[par])]
                        if stats_mark[par]:
                            sw.append((sPE, stats_mark[par]))
                        em("dv", lambda e, o=yb[par][:], i=y_sl:
                           e.tensor_copy(o, i), waits=sw, ctr=cDV)
                        em("ac", lambda e, o=ysq[par][:], i=y_sl:
                           e.activation(o, i, AF.Square),
                           waits=([(sPE, stats_mark[par])]
                                  if stats_mark[par] else ()), ctr=cAC)
                        if pend_st is not None:
                            flush_ystats(pend_st)
                        extra = ([(sDV, prev_stats_fin)]
                                 if half == 1 and m2 == 0 else [])
                        pend_st = (tc, m2, cDV.v, cAC.v, extra)
                    wo_mark[half * NK + m2] = cPE.v

                flush_ystats(pend_st)
                pend_st = None
                fin = stats_finish(ps_s1, ps_q1, m_b4, var_b4, nmsq4, 512,
                                   cPE.v)
                r4_ac = cAC.v
                pend_x1 = None

                def flush_x1stats(p):
                    (pm2, dvm, acm) = p
                    ppar = pm2 % 2
                    pw = [(sDV, dvm), (sAC, acm)]
                    if pm2 == 0:
                        pw.append((sDV, fin))
                    wt("pe", *pw)
                    for tc2 in range(2):
                        s2 = slice(tc2 * 512, (tc2 + 1) * 512)
                        mm(ps_s1[tc2][:], ones_st[:], x1b[ppar][:, s2],
                           start=(pm2 == 0), stop=(pm2 == NK - 1))
                        mm(ps_q1[tc2][:], ones_st[:], x1sq[ppar][:, s2],
                           start=(pm2 == 0), stop=(pm2 == NK - 1),
                           ctr=(cPE if tc2 == 1 else None))
                    x1stats_mark[ppar] = cPE.v

                for m2 in range(NK):
                    par = m2 % 2
                    ceng, cctr = (("dv", cDV), ("gp", cPL))[m2 % 2]
                    rows = slice(m2 * 128, (m2 + 1) * 128)
                    cols = slice(base, base + 1024)
                    leng, lctr = (("sy", cSY), ("gp", cGP), ("ac", cAC16))[m2 % 3]
                    lw = []
                    if apm[par]:
                        lw.append(apm[par])
                    dma(leng, lctr, xf2[par][:], xT[rows, cols], waits=lw)
                    dwts = [(lctr.sem, lctr.v), (sAC, r4_ac), (sDV, fin)]
                    if x1dma[par][0] is not None:
                        dwts.append(x1dma[par])
                    wt(ceng, *dwts)
                    em(ceng, lambda e, o=y_half[:, m2, :], r=var_b4[:]:
                       e.tensor_tensor(o, o, r, ALU.mult))
                    em(ceng, lambda e, o=xf2[par][:], m=m_b4[:]:
                       e.tensor_tensor(o, o, m, ALU.subtract))
                    em(ceng, lambda e, o=x1t[par][:], a=y_half[:, m2, :],
                       b=xf2[par][:]: e.tensor_tensor(o, a, b, ALU.add),
                       ctr=cctr)
                    apm[par] = (cctr.sem, cctr.v)
                    apm_all[m2] = apm[par]
                    seng, sctr = (("ac", cAC16), ("sy", cSY), ("gp", cGP))[m2 % 3]
                    dma(seng, sctr, x1_scr[rows, cols], x1t[par][:],
                        waits=[apm[par]])
                    x1dma[par] = (sctr.sem, sctr.v)
                    xsw = [apm[par]]
                    if x1stats_mark[par]:
                        xsw.append((sPE, x1stats_mark[par]))
                    em("dv", lambda e, o=x1b[par][:], i=x1t[par][:]:
                       e.tensor_copy(o, i), waits=xsw, ctr=cDV)
                    em("ac", lambda e, o=x1sq[par][:], i=x1t[par][:]:
                       e.activation(o, i, AF.Square), waits=xsw, ctr=cAC)
                    if pend_x1 is not None:
                        flush_x1stats(pend_x1)
                    pend_x1 = (m2, cDV.v, cAC.v)
                flush_x1stats(pend_x1)
                pend_x1 = None
                fin5p = stats_finish(ps_s1, ps_q1, m_b4, var_b4, x1sq[0], 512,
                                     cPE.v)
                wt("dv", (sAC, cAC.v), (sDV, fin5p))
                em("dv", lambda e, o=r5_all[:, base:base + 1024], i=var_b4[:]:
                   e.tensor_copy(o, i))
                em("dv", lambda e, o=mrb5_all[:, base:base + 1024], i=m_b4[:]:
                   e.tensor_copy(o, i), ctr=cDV)
            flush()
        barrier()

        if upto == "x1":
            dma("sy", cSY, out[:, :], x1_scr[:, :])
            _finalize(ops, wt, barrier, flush, ALLC)
            return nc

        with contextlib.ExitStack() as p5:
            x1sb = p5.enter_context(nc.sbuf_tensor("x1sb", [128, NK, 512], F32))
            hbuf = p5.enter_context(nc.sbuf_tensor("hbuf", [128, NC, 512], BF16))
            y2 = p5.enter_context(nc.sbuf_tensor("y2", [128, NK, 512], BF16))
            xn2 = p5.enter_context(nc.sbuf_tensor("xn2", [128, NK, 512], BF16))
            xapp = [p5.enter_context(
                nc.sbuf_tensor(f"xapp{i}", [128, 512], F32)) for i in range(2)]
            wgub = [p5.enter_context(
                nc.sbuf_tensor(f"wgub{i}", [128, 2, D], BF16))
                for i in range(2)]
            wdb = [p5.enter_context(nc.sbuf_tensor(f"wdb{i}", [128, 8192], BF16))
                   for i in range(2)]
            sg_t = [p5.enter_context(nc.sbuf_tensor(f"sg_t{i}", [128, 512], F32))
                    for i in range(2)]
            xq5 = [p5.enter_context(nc.sbuf_tensor(f"xq5{i}", [128, 512], BF16))
                   for i in range(2)]
            m_b5 = p5.enter_context(nc.sbuf_tensor("m_b5", [128, 512], F32))
            var_b5 = p5.enter_context(nc.sbuf_tensor("var_b5", [128, 512], F32))
            ps_a = p5.enter_context(nc.psum_tensor("ps_a", [128, 512], F32))
            ps_b = p5.enter_context(nc.psum_tensor("ps_b", [128, 512], F32))
            ps_g = [p5.enter_context(
                nc.psum_tensor(f"ps_g{i}", [128, 512], F32)) for i in range(2)]
            ps_u = [p5.enter_context(
                nc.psum_tensor(f"ps_u{i}", [128, 512], F32)) for i in range(2)]
            ps_d = [p5.enter_context(
                nc.psum_tensor(f"ps_d{i}", [128, 512], F32)) for i in range(2)]

            x2_all = {}
            gu_mark = [0, 0]
            h_mark = [0, 0]
            d_mark = [0, 0]
            y2_evac = [0, 0]
            st5_mark = [0, 0]
            apm5 = {}
            xld_g = []

            def emit_x1sb_load(dst_tt, gw):
                res = []
                cols2 = slice(dst_tt * 512, (dst_tt + 1) * 512)
                for g in range(4):
                    gsrc = x1_scr[g * 512:(g + 1) * 512, cols2].rearrange(
                        "(k p) t -> p k t", p=128)
                    geng, gctr = (("sy", cSY), ("ac", cAC16))[g % 2]
                    dma(geng, gctr, x1sb[:, 4 * g:4 * (g + 1), :], gsrc,
                        waits=gw)
                    res.append((gctr.sem, gctr.v))
                return res

            xld_g = emit_x1sb_load(0, [])
            ap5 = [0, 0]
            for k in range(NK):
                par = k % 2
                aw = [xld_g[k // 4]]
                if ap5[par]:
                    aw.append((sDV, ap5[par]))
                wt("dv", *aw)
                em("dv", lambda e, o=xapp[par][:], a=x1sb[:, k, :],
                   r=r5_all[:, 0:512]: e.tensor_tensor(o, a, r, ALU.mult))
                em("dv", lambda e, o=xn2[:, k, :], a=xapp[par][:],
                   m=mrb5_all[:, 0:512]: e.tensor_tensor(o, a, m, ALU.subtract),
                   ctr=cDV)
                ap5[par] = cDV.v
                apm5[k] = (sDV, cDV.v)
            xapp_free = (sDV, cDV.v)

            wgu_pre = {}
            gu_early = {}

            def emit_gu_group(m, wld, apm_map):
                par = m % 2
                pw = [wld]
                if m >= 2:
                    pw.append(apm_map[NK - 1])
                if h_mark[par]:
                    pw.append((sDV, h_mark[par]))
                wt("pe", *pw)
                for k in range(NK):
                    kw = [apm_map[k]] if m < 2 else ()
                    mm(ps_g[par][:], wgub[par][:, 0, k * 128:(k + 1) * 128],
                       xn2[:, k, :], start=(k == 0), stop=(k == NK - 1),
                       waits=kw, ctr=(cPE if k == NK - 1 else None))
                gmm = cPE.v
                for k in range(NK):
                    mm(ps_u[par][:], wgub[par][:, 1, k * 128:(k + 1) * 128],
                       xn2[:, k, :], start=(k == 0), stop=(k == NK - 1),
                       ctr=(cPE if k == NK - 1 else None))
                umm = cPE.v
                gu_mark[par] = cPE.v
                return gmm, umm

            def emit_gu_evac(m, gmm, umm, extra):
                par = m % 2
                em("ac", lambda e, o=sg_t[par][:], i=ps_g[par][:],
                   s=sg_sb[:, m:m + 1]:
                   e.activation(o, i, AF.Silu, scale=s),
                   waits=[(sPE, gmm), (sSY, consts_ld)] + list(extra),
                   ctr=cAC)
                em("dv", lambda e, o=hbuf[:, m, :], i=ps_u[par][:],
                   s=su_sb[:, m:m + 1], g=sg_t[par][:]:
                   e.scalar_tensor_tensor(o, i, s, g, ALU.mult, ALU.mult),
                   waits=[(sPE, umm), (sAC, cAC.v)] + list(extra), ctr=cDV)
                h_mark[par] = cDV.v

            for tt in range(4):
                cols = slice(tt * 512, (tt + 1) * 512)

                wd_pre = {}
                for m in range(NC):
                    par = m % 2
                    if m in gu_early:
                        gu_early.pop(m)
                        continue
                    if m in wgu_pre:
                        wld = wgu_pre.pop(m)
                    else:
                        lw = []
                        if gu_mark[par]:
                            lw.append((sPE, gu_mark[par]))
                        dma("gp", cGP, wgub[par][:],
                            wgu_m[m].rearrange("a p f -> p a f"), waits=lw)
                        wld = (sGP, cGP.v)
                    gmm, umm = emit_gu_group(m, wld, apm5)
                    if m in (48, 56):
                        m2p = (m - 48) // 8
                        dlw = []
                        if d_mark[m2p % 2]:
                            dlw.append((sPE, d_mark[m2p % 2]))
                        dma("gp", cGP, wdb[m2p % 2][:], wd_m[m2p], waits=dlw)
                        wd_pre[m2p] = cGP.v
                    emit_gu_evac(m, gmm, umm, ())
                h_all = cDV.v
                gu_last = cPE.v

                pend_d = None

                def flush_dstats(p):
                    (pm2, dvm, acm) = p
                    ppar = pm2 % 2
                    pw = [(sDV, dvm), (sAC, acm)]
                    if pm2 == 0 and tt > 0:
                        pw.append((sDV, prev_stats_fin5))
                    wt("pe", *pw)
                    mm(ps_a[:], ones_st[:], y2[:, pm2, :],
                       start=(pm2 == 0), stop=(pm2 == NK - 1))
                    mm(ps_b[:], ones_st[:], xq5[ppar][:],
                       start=(pm2 == 0), stop=(pm2 == NK - 1), ctr=cPE)
                    st5_mark[ppar] = cPE.v

                nap = apm5
                if tt < 3:
                    nap = {}
                    ncols = slice((tt + 1) * 512, (tt + 2) * 512)
                for m2 in range(NK):
                    par = m2 % 2
                    if tt < 3 and m2 < 8:
                        for k in (2 * m2, 2 * m2 + 1):
                            kp = k % 2
                            klw = [nap[k - 2]] if k >= 2 else [xapp_free]
                            dma(("sy", "ac")[kp],
                                (cSY, cAC16)[kp], xapp[kp][:],
                                x1_scr[k * 128:(k + 1) * 128, ncols],
                                waits=klw)
                            kld = ((sSY, cSY.v), (sAC16, cAC16.v))[kp]
                            wt("dv", kld, (sPE, gu_last))
                            em("dv", lambda e, o=xapp[kp][:],
                               r=r5_all[:, ncols]:
                               e.tensor_tensor(o, o, r, ALU.mult))
                            em("dv", lambda e, o=xn2[:, k, :], a=xapp[kp][:],
                               m=mrb5_all[:, ncols]:
                               e.tensor_tensor(o, a, m, ALU.subtract),
                               ctr=cDV)
                            nap[k] = (sDV, cDV.v)
                    if tt < 3 and m2 in (6, 7):
                        mpf = m2 - 6
                        dma("gp", cGP, wgub[mpf][:],
                            wgu_m[mpf].rearrange("a p f -> p a f"),
                            waits=[(sPE, gu_mark[mpf])])
                        wgu_pre[mpf] = (sGP, cGP.v)
                    if m2 in wd_pre:
                        wld = wd_pre[m2]
                    else:
                        lw = []
                        if d_mark[par]:
                            lw.append((sPE, d_mark[par]))
                        dma("gp", cGP, wdb[par][:], wd_m[m2], waits=lw)
                        wld = cGP.v
                    pw = [(sGP, wld), (sDV, h_all)]
                    if y2_evac[par]:
                        pw.append((sDV, y2_evac[par]))
                    wt("pe", *pw)
                    for c in range(NC):
                        mm(ps_d[par][:], wdb[par][:, c * 128:(c + 1) * 128],
                           hbuf[:, c, :], start=(c == 0), stop=(c == NC - 1),
                           ctr=(cPE if c == NC - 1 else None))
                    d_mark[par] = cPE.v
                    if tt < 3 and m2 in (9, 10):
                        mpf = m2 - 9
                        gu_early[mpf] = emit_gu_group(
                            mpf, wgu_pre.pop(mpf), nap)
                    if tt < 3 and m2 in (11, 12):
                        mpf = m2 - 9
                        dma("gp", cGP, wgub[mpf % 2][:],
                            wgu_m[mpf].rearrange("a p f -> p a f"),
                            waits=[(sPE, gu_mark[mpf % 2])])
                        wgu_pre[mpf] = (sGP, cGP.v)
                    em("dv", lambda e, o=y2[:, m2, :], i=ps_d[par][:],
                       s=sd_sb[:, m2:m2 + 1]:
                       e.tensor_scalar(o, i, s, None, ALU.mult),
                       waits=[(sPE, d_mark[par]), (sSY, consts_ld)], ctr=cDV)
                    y2_evac[par] = cDV.v
                    em("ac", lambda e, o=xq5[par][:], i=y2[:, m2, :]:
                       e.activation(o, i, AF.Square),
                       waits=[(sDV, y2_evac[par])] + ([(sPE, st5_mark[par])]
                                                      if st5_mark[par] else []),
                       ctr=cAC)
                    if pend_d is not None:
                        flush_dstats(pend_d)
                    pend_d = (m2, y2_evac[par], cAC.v)
                flush_dstats(pend_d)
                pend_d = None
                d_last = cPE.v
                for mpf in sorted(gu_early):
                    gmm, umm = gu_early[mpf]
                    emit_gu_evac(mpf, gmm, umm,
                                 [(sPE, d_last), (sDV, h_mark[mpf % 2])])
                prev_stats_fin5 = stats_finish([ps_a], [ps_b], m_b5, var_b5,
                                               xapp[0], 512, cPE.v)
                r5c_ac = cAC.v
                for m2 in range(NK):
                    par = m2 % 2
                    rows = slice(m2 * 128, (m2 + 1) * 128)
                    aw = [(sAC, r5c_ac), (sDV, prev_stats_fin5),
                          xld_g[m2 // 4]]
                    wt("dv", *aw)
                    em("dv", lambda e, o=xapp[par][:], a=y2[:, m2, :],
                       r=var_b5[:]: e.tensor_tensor(o, a, r, ALU.mult))
                    em("dv", lambda e, o=x1sb[:, m2, :], m=m_b5[:]:
                       e.tensor_tensor(o, o, m, ALU.subtract))
                    em("dv", lambda e, o=x1sb[:, m2, :], a=xapp[par][:]:
                       e.tensor_tensor(o, o, a, ALU.add), ctr=cDV)
                    oeng, octr = (("sy", cSY), ("ac", cAC16),
                                  ("gp", cGP))[m2 % 3]
                    dma(oeng, octr, out[rows, cols], x1sb[:, m2, :],
                        waits=[(sDV, cDV.v)])
                    x2_all[m2] = (octr.sem, octr.v)
                apm5 = nap
                xapp_free = (sDV, cDV.v)
                if tt < 3:
                    st_done = [(sSY, cSY.v), (sAC16, cAC16.v), (sGP, cGP.v)]
                    xld_g = emit_x1sb_load(tt + 1, st_done)
            flush()

        _finalize(ops, wt, barrier, flush, ALLC)
    return nc


def _finalize(ops, wt, barrier, flush, ALLC):
    barrier()
    flush()


def _prep_inputs(L0_attn_x, L0_attn_Wq, L0_attn_Wk, L0_attn_Wv, L0_attn_Wo,
                 L0_ffn_w_gate, L0_ffn_w_up, L0_ffn_w_down,
                 L0_ffn_s_gate, L0_ffn_s_up, L0_ffn_s_down):
    c = np.ascontiguousarray
    wq = c(np.asarray(L0_attn_Wq, np.float32)
           .reshape(16, 128, 16, 128).transpose(2, 1, 0, 3)
           .reshape(16, 128, 2048))
    wk = c(np.asarray(L0_attn_Wk, np.float32)
           .reshape(16, 128, 16, 128).transpose(2, 1, 0, 3)
           .reshape(16, 128, 2048))
    wo = c(np.asarray(L0_attn_Wo, np.float32)
           .reshape(16, 128, 16, 128).transpose(2, 1, 0, 3)
           .reshape(16, 128, 2048))
    wv = c(np.asarray(L0_attn_Wv, np.float32))
    wg = (np.asarray(L0_ffn_w_gate, np.int8)
          .reshape(16, 128, 64, 128).transpose(2, 1, 0, 3)
          .reshape(64, 128, 2048))
    wu = (np.asarray(L0_ffn_w_up, np.int8)
          .reshape(16, 128, 64, 128).transpose(2, 1, 0, 3)
          .reshape(64, 128, 2048))
    wgu = c(np.stack([wg, wu], axis=1))
    wd = c(np.asarray(L0_ffn_w_down, np.int8)
           .reshape(64, 128, 16, 128).transpose(2, 1, 0, 3)
           .reshape(16, 128, 8192))
    sg = c(np.asarray(L0_ffn_s_gate, np.float32).reshape(64, 128).T)
    su = c(np.asarray(L0_ffn_s_up, np.float32).reshape(64, 128).T)
    sd = c(np.asarray(L0_ffn_s_down, np.float32).reshape(16, 128).T)
    tri = np.triu(np.ones((128, 128), np.float32))
    x = np.asarray(L0_attn_x, np.float32)
    shared = dict(wq_h=wq, wk_h=wk, wv=wv, wo_m=wo, wgu_m=wgu,
                  wd_m=wd, sg2d=sg, su2d=su, sd2d=sd, tri_in=tri)
    in_maps = []
    for b in range(8):
        m = dict(shared)
        m["xT"] = c(x[b].T)
        in_maps.append(m)
    return in_maps


_NC_CACHE = {}


def kernel(**inputs):
    in_maps = _prep_inputs(**inputs)
    if "full" not in _NC_CACHE:
        _NC_CACHE["full"] = build_nc("full")
    nc = _NC_CACHE["full"]
    res = run_bass_kernel_spmd(nc, in_maps, core_ids=list(range(8)))
    outs = [np.ascontiguousarray(np.asarray(r["out"], np.float32).T)
            for r in res.results]
    return np.stack(outs, axis=0)


# version 20
# speedup vs baseline: 1.0932x; 1.0932x over previous
import contextlib

import numpy as np
import concourse.bass as bass
import concourse.mybir as mybir
from concourse.bass_utils import run_bass_kernel_spmd

F32 = mybir.dt.float32
BF16 = mybir.dt.bfloat16
I8 = mybir.dt.int8
AF = mybir.ActivationFunctionType
ALU = mybir.AluOpType

D = 2048
T = 2048
H = 16
NK = D // 128
NC = 8192 // 128
SM_SCALE = float(1.0 / np.sqrt(2048.0))
EPS = 1e-5


class Ctr:
    def __init__(self, sem, step):
        self.sem = sem
        self.v = 0
        self.step = step


def build_nc(upto="full"):
    nc = bass.Bass("TRN2", target_bir_lowering=False)

    xT = nc.dram_tensor("xT", [D, T], F32, kind="ExternalInput").ap()
    wq_h = nc.dram_tensor("wq_h", [H, 128, D], F32, kind="ExternalInput").ap()
    wk_h = nc.dram_tensor("wk_h", [H, 128, D], F32, kind="ExternalInput").ap()
    wv = nc.dram_tensor("wv", [D, D], F32, kind="ExternalInput").ap()
    wo_m = nc.dram_tensor("wo_m", [NK, 128, D], F32, kind="ExternalInput").ap()
    wgu_m = nc.dram_tensor("wgu_m", [NC, 2, 128, D], I8,
                           kind="ExternalInput").ap()
    wd_m = nc.dram_tensor("wd_m", [NK, 128, 8192], I8, kind="ExternalInput").ap()
    sg2d = nc.dram_tensor("sg2d", [128, NC], F32, kind="ExternalInput").ap()
    su2d = nc.dram_tensor("su2d", [128, NC], F32, kind="ExternalInput").ap()
    sd2d = nc.dram_tensor("sd2d", [128, NK], F32, kind="ExternalInput").ap()
    tri_in = nc.dram_tensor("tri_in", [128, 128], F32, kind="ExternalInput").ap()

    out = nc.dram_tensor("out", [D, T], F32, kind="ExternalOutput").ap()

    x1_scr = nc.dram_tensor("x1_scr", [D, T], F32, kind="Internal").ap()
    wo_scr = nc.dram_tensor("wo_scr", [NK, 128, D], BF16, kind="Internal").ap()
    attn_scr = nc.dram_tensor("attn_scr", [H, 128, T], BF16, kind="Internal").ap()

    with contextlib.ExitStack() as es:
        tri = es.enter_context(nc.sbuf_tensor("tri", [128, 128], F32))
        ones_st = es.enter_context(nc.sbuf_tensor("ones_st", [128, 128], BF16))
        ones1 = es.enter_context(nc.sbuf_tensor("ones1", [128, 1], BF16))
        onesk1 = es.enter_context(nc.sbuf_tensor("onesk1", [1, 128], BF16))
        eps_t = es.enter_context(nc.sbuf_tensor("eps_t", [128, 1], F32))
        sg_sb = es.enter_context(nc.sbuf_tensor("sg_sb", [128, NC], F32))
        su_sb = es.enter_context(nc.sbuf_tensor("su_sb", [128, NC], F32))
        sd_sb = es.enter_context(nc.sbuf_tensor("sd_sb", [128, NK], F32))

        sGP = es.enter_context(nc.semaphore("sGP"))
        sSY = es.enter_context(nc.semaphore("sSY"))
        sPE = es.enter_context(nc.semaphore("sPE"))
        sDV = es.enter_context(nc.semaphore("sDV"))
        sAC = es.enter_context(nc.semaphore("sAC"))
        sAC16 = es.enter_context(nc.semaphore("sAC16"))
        sPL = es.enter_context(nc.semaphore("sPL"))

        cGP, cSY = Ctr(sGP, 16), Ctr(sSY, 16)
        cPE, cDV, cAC = Ctr(sPE, 1), Ctr(sDV, 1), Ctr(sAC, 1)
        cAC16, cPL = Ctr(sAC16, 16), Ctr(sPL, 1)
        ALLC = [cGP, cSY, cPE, cDV, cAC, cAC16, cPL]

        block = es.enter_context(nc.Block())

        ops = {"gp": [], "sy": [], "pe": [], "dv": [], "ac": []}

        def flush():
            if ops["gp"]:
                block.gpsimd(lambda e, fns=tuple(ops["gp"]): [f(e) for f in fns])
            if ops["sy"]:
                block.sync(lambda e, fns=tuple(ops["sy"]): [f(e) for f in fns])
            if ops["pe"]:
                block.tensor(lambda e, fns=tuple(ops["pe"]): [f(e) for f in fns])
            if ops["dv"]:
                block.vector(lambda e, fns=tuple(ops["dv"]): [f(e) for f in fns])
            if ops["ac"]:
                block.scalar(lambda e, fns=tuple(ops["ac"]): [f(e) for f in fns])
            for k in ops:
                ops[k] = []

        def _run(eng, waits, make, inc):
            for (sem, v) in waits:
                if v > 0:
                    eng.wait_ge(sem, v)
            ins = make(eng)
            if inc is not None:
                ins.then_inc(inc[0], inc[1])

        def em(engname, make, waits=(), ctr=None):
            inc = None
            if ctr is not None:
                ctr.v += ctr.step
                inc = (ctr.sem, ctr.step)
            ops[engname].append(
                lambda eng, w=tuple(waits), m=make, i=inc: _run(eng, w, m, i))

        def wt(engname, *waits):
            ops[engname].append(
                lambda eng, w=tuple(waits): [eng.wait_ge(s, v)
                                             for (s, v) in w if v > 0])

        def barrier():
            snap = [(c.sem, c.v) for c in ALLC]
            for e in ("gp", "sy", "pe", "dv", "ac"):
                wt(e, *snap)

        def dma(engname, ctr, dst, src, waits=()):
            em(engname, lambda e, d=dst, s=src: e.dma_start(out=d, in_=s),
               waits=waits, ctr=ctr)

        def warm(psum_t, rhs_ap, *waits):
            em("pe", lambda e, p=psum_t[0:64, 0:64], r=rhs_ap:
               e.matmul(p, ones_st[:, 0:64], r, start=True, stop=True,
                        skip_group_check=True), waits=list(waits))

        def mm(psum, lhsT, rhs, start, stop, waits=(), ctr=None):
            em("pe", lambda e, p=psum, l=lhsT, r=rhs, s0=start, s1=stop:
               e.matmul(p, l, r, start=s0, stop=s1, skip_group_check=True),
               waits=waits, ctr=ctr)

        def stats_finish(ps_sum, ps_sq, m_b, var_b, nmsq, w_each, pe_done):
            wt("dv", (sPE, pe_done))
            for tc in range(len(ps_sum)):
                sl = slice(tc * w_each, (tc + 1) * w_each)
                em("dv", lambda e, o=m_b[:, sl], i=ps_sum[tc][:]:
                   e.tensor_scalar(o, i, 1.0 / D, None, ALU.mult))
                em("dv", lambda e, o=var_b[:, sl], i=ps_sq[tc][:]:
                   e.tensor_scalar(o, i, 1.0 / D, None, ALU.mult))
            em("dv", lambda e, o=nmsq[:], a=m_b[:]:
               e.scalar_tensor_tensor(o, a, -1.0, a, ALU.mult, ALU.mult))
            em("dv", lambda e, o=var_b[:], a=nmsq[:]:
               e.tensor_tensor(o, o, a, ALU.add), ctr=cDV)
            wt("ac", (sDV, cDV.v))
            em("ac", lambda e, o=var_b[:]:
               e.activation(o, o, AF.Ln, bias=eps_t[:]))
            em("ac", lambda e, o=var_b[:]:
               e.activation(o, o, AF.Exp, scale=-0.5), ctr=cAC)
            wt("dv", (sAC, cAC.v))
            em("dv", lambda e, o=m_b[:], b=var_b[:]:
               e.tensor_tensor(o, o, b, ALU.mult), ctr=cDV)
            return cDV.v

        dma("sy", cSY, tri[:], tri_in)
        dma("sy", cSY, sg_sb[:], sg2d)
        dma("sy", cSY, su_sb[:], su2d)
        dma("sy", cSY, sd_sb[:], sd2d)
        consts_ld = cSY.v
        em("dv", lambda e: e.memset(ones_st[:], 1.0))
        em("dv", lambda e: e.memset(ones1[:], 1.0))
        em("dv", lambda e: e.memset(onesk1[:], 1.0))
        em("dv", lambda e: e.memset(eps_t[:], EPS), ctr=cDV)
        consts_dv = cDV.v

        p123 = es.enter_context(contextlib.ExitStack())
        xn1 = p123.enter_context(nc.sbuf_tensor("xn1", [128, NK, T], BF16))
        v_tm = p123.enter_context(nc.sbuf_tensor("v_tm", [128, NK, D], BF16))
        wqb = [p123.enter_context(nc.sbuf_tensor(f"wqb{i}", [128, D], BF16))
               for i in range(2)]
        wkb = [p123.enter_context(nc.sbuf_tensor(f"wkb{i}", [128, D], BF16))
               for i in range(2)]
        wostg = [p123.enter_context(
            nc.sbuf_tensor(f"wostg{i}", [128, D], BF16)) for i in range(2)]
        wv_ld = {}
        qk_ld = {}

        with contextlib.ExitStack() as p1:
            xsq = [p1.enter_context(nc.sbuf_tensor(f"xsq{i}", [128, T], BF16))
                   for i in range(2)]
            xf = [p1.enter_context(nc.sbuf_tensor(f"xf{i}", [128, T], F32))
                  for i in range(2)]
            m_b = p1.enter_context(nc.sbuf_tensor("m_b", [128, T], F32))
            var_b = p1.enter_context(nc.sbuf_tensor("var_b", [128, T], F32))
            ps_sum = [p1.enter_context(
                nc.psum_tensor(f"ps_sum{i}", [128, 512], F32)) for i in range(4)]
            ps_sq = [p1.enter_context(
                nc.psum_tensor(f"ps_sq{i}", [128, 512], F32)) for i in range(4)]

            cp_guard = {}
            ld_gp = {}
            for k in range(8):
                dma("gp", cGP, xn1[:, k, :], xT[k * 128:(k + 1) * 128, :])
                ld_gp[k] = cGP.v
            ld_sy = {}

            ready = {}
            for k in range(8):
                ready[k] = (sGP, ld_gp[k])
            for k in range(8, NK):
                lw = []
                if k >= 10:
                    lw.append((sDV, cp_guard[k - 2]))
                dma("ac", cAC16, xf[k % 2][:], xT[k * 128:(k + 1) * 128, :],
                    waits=lw)
                ld_sy[k] = cAC16.v
                em("dv", lambda e, o=xn1[:, k, :], i=xf[k % 2][:]:
                   e.tensor_copy(o, i), waits=[(sAC16, ld_sy[k])], ctr=cDV)
                cp_guard[k] = cDV.v
                ready[k] = (sDV, cDV.v)

            pe_k = []
            for k in range(NK):
                wts = [ready[k], (sDV, consts_dv)]
                if k >= 2:
                    wts.append((sPE, pe_k[k - 2]))
                em("ac", lambda e, o=xsq[k % 2][:], i=xn1[:, k, :]:
                   e.activation(o, i, AF.Square), waits=wts, ctr=cAC)
                wt("pe", ready[k], (sAC, cAC.v))
                for tc in range(4):
                    sl = slice(tc * 512, (tc + 1) * 512)
                    mm(ps_sum[tc][:], ones_st[:], xn1[:, k, sl],
                       start=(k == 0), stop=(k == NK - 1))
                    mm(ps_sq[tc][:], ones_st[:], xsq[k % 2][:, sl],
                       start=(k == 0), stop=(k == NK - 1),
                       ctr=(cPE if tc == 3 else None))
                pe_k.append(cPE.v)

            ln1_ready = stats_finish(ps_sum, ps_sq, m_b, var_b, xsq[0], 512,
                                     cPE.v)
            r_ready_ac = cAC.v

            ap_mark = []
            last_cp = cDV.v
            POOLK = {3, 6, 9, 12, 15}
            for k in range(NK):
                wts = []
                if k < 2:
                    wts.append((sDV, last_cp))
                if k >= 2:
                    wts.append(ap_mark[k - 2])
                dma("sy", cSY, xf[k % 2][:], xT[k * 128:(k + 1) * 128, :],
                    waits=wts)
                eng = "gp" if k in POOLK else "dv"
                ctr = cPL if k in POOLK else cDV
                wt(eng, (sSY, cSY.v), (sAC, r_ready_ac), (sDV, ln1_ready))
                em(eng, lambda e, b=xf[k % 2][:], r=var_b[:]:
                   e.tensor_tensor(b, b, r, ALU.mult))
                em(eng, lambda e, o=xn1[:, k, :], b=xf[k % 2][:], m=m_b[:]:
                   e.tensor_tensor(o, b, m, ALU.subtract), ctr=ctr)
                ap_mark.append((ctr.sem, ctr.v))
                warm(ps_sum[0], xsq[0][:, 0:64], ap_mark[-1])
            flush()
        barrier()

        if upto == "ln1":
            for k in range(NK):
                dma("gp", cGP, out[k * 128:(k + 1) * 128, :], xn1[:, k, :])
            _finalize(ops, wt, barrier, flush, ALLC)
            return nc

        with contextlib.ExitStack() as p2:
            wvb = [p2.enter_context(
                nc.sbuf_tensor(f"wvb{i}", [128, NK, 512], BF16))
                for i in range(2)]
            ps_v = [p2.enter_context(
                nc.psum_tensor(f"ps_v{i}", [128, 512], F32)) for i in range(2)]
            evac = []
            vq_mark = []
            for q in range(4):
                cols = slice(q * 512, (q + 1) * 512)
                src2 = wv[:, cols].rearrange("(k p) n -> p k n", p=128)
                lw = [] if q < 2 else [(sPE, vq_mark[q - 2])]
                dma("gp", cGP, wvb[q % 2][:], src2, waits=lw)
                wld = cGP.v
                for st in range(NK):
                    j = q * NK + st
                    par = j % 2
                    wts = [(sGP, wld)]
                    if j >= 2:
                        wts.append((sAC, evac[j - 2]))
                    wt("pe", *wts)
                    for k in range(NK):
                        mm(ps_v[par][:],
                           xn1[:, k, st * 128:(st + 1) * 128],
                           wvb[q % 2][:, k, :],
                           start=(k == 0), stop=(k == NK - 1),
                           ctr=(cPE if k == NK - 1 else None))
                    dst = v_tm[:, st, q * 512:(q + 1) * 512]
                    em("ac", lambda e, o=dst, i=ps_v[par][:]:
                       e.activation(o, i, AF.Copy),
                       waits=[(sPE, cPE.v)], ctr=cAC)
                    evac.append(cAC.v)
                vq_mark.append(cPE.v)
            for h2 in range(2):
                dma("gp", cGP, wqb[h2][:], wq_h[h2])
                dma("gp", cGP, wkb[h2][:], wk_h[h2])
                qk_ld[h2] = cGP.v
            flush()
        barrier()

        with contextlib.ExitStack() as p3:
            qh = p3.enter_context(nc.sbuf_tensor("qh", [128, T], BF16))
            kh = p3.enter_context(nc.sbuf_tensor("kh", [128, T], BF16))
            at = p3.enter_context(nc.sbuf_tensor("at", [128, 16, 512], BF16))
            attn_h = [p3.enter_context(
                nc.sbuf_tensor(f"attn_h{i}", [128, T], BF16)) for i in range(2)]
            zt = p3.enter_context(nc.sbuf_tensor("zt", [1, 512], F32))
            suf_sb = p3.enter_context(nc.sbuf_tensor("suf_sb", [128, 4], F32))
            zinv_bf = p3.enter_context(nc.sbuf_tensor("zinv_bf", [1, 512], BF16))
            zb_sb = p3.enter_context(nc.sbuf_tensor("zb_sb", [128, 512], BF16))
            ps_qk = [p3.enter_context(
                nc.psum_tensor(f"ps_qk{i}", [128, 512], F32)) for i in range(2)]
            ps_s = [p3.enter_context(
                nc.psum_tensor(f"ps_s{i}", [128, 512], F32)) for i in range(2)]
            ps_o = [p3.enter_context(
                nc.psum_tensor(f"ps_o{i}", [128, 512], F32)) for i in range(2)]
            ps_z = p3.enter_context(nc.psum_tensor("ps_z", [1, 512], F32))
            ps_zb = p3.enter_context(nc.psum_tensor("ps_zb", [128, 512], F32))

            qk_mm_done = {}
            scores_last = {-1: 0}
            exp_all = [0, 0]
            qk_evac = [0, 0]
            at_free = 0
            o_evac = [0, 0]
            z_ts_prev = 0
            ln_done_prev = 0
            zbmm_prev = 0
            attn_store = [0, 0]
            suf_read = 0
            suf_dv_prev = 0

            def emit_tail(pend):
                nonlocal zbmm_prev
                (o_par_p, q_sl_p, h_p, zinv_mark, qc_p, sufdv) = pend
                wt("pe", (sAC, zinv_mark), (sDV, o_evac[1 - o_par_p]))
                mm(ps_zb[:], onesk1[:], zinv_bf[:], start=True, stop=True,
                   ctr=cPE)
                zbmm_prev = cPE.v
                wts = [(sPE, zbmm_prev), (sSY, attn_store[h_p % 2])]
                if sufdv:
                    wts.append((sDV, sufdv))
                wt("dv", *wts)
                em("dv", lambda e, o=zb_sb[:], i=ps_zb[:]: e.tensor_copy(o, i))
                sufop = suf_sb[:, qc_p:qc_p + 1] if qc_p < 3 else 0.0
                em("dv", lambda e, o=attn_h[h_p % 2][:, q_sl_p],
                   i=ps_o[o_par_p][:], z2=zb_sb[:], s3=sufop:
                   e.scalar_tensor_tensor(o, i, s3, z2, ALU.add, ALU.mult),
                   ctr=cDV)
                o_evac[o_par_p] = cDV.v

            wo_stg_mark = [0, 0]
            for h in range(H):
                if h in qk_ld:
                    wld = qk_ld[h]
                else:
                    hw = [(sPE, qk_mm_done[h - 2])]
                    dma("gp", cGP, wqb[h % 2][:], wq_h[h], waits=hw)
                    dma("gp", cGP, wkb[h % 2][:], wk_h[h])
                    wld = cGP.v
                sw = [] if wo_stg_mark[h % 2] == 0 else                     [(sSY, wo_stg_mark[h % 2])]
                dma("gp", cGP, wostg[h % 2][:], wo_m[h], waits=sw)
                dma("sy", cSY, wo_scr[h], wostg[h % 2][:],
                    waits=[(sGP, cGP.v)])
                wo_stg_mark[h % 2] = cSY.v

                for which, (wbuf, dst) in enumerate(
                        ((wqb, qh), (wkb, kh))):
                    for tc in range(4):
                        j = which * 4 + tc
                        par = j % 2
                        wts = [(sGP, wld)]
                        if qk_evac[par]:
                            wts.append((sAC, qk_evac[par]))
                        if which == 0 and tc == 0 and suf_dv_prev:
                            wts.append((sDV, suf_dv_prev))
                        wt("pe", *wts)
                        for k in range(NK):
                            mm(ps_qk[par][:],
                               wqb[h % 2][:, k * 128:(k + 1) * 128] if which == 0
                               else wkb[h % 2][:, k * 128:(k + 1) * 128],
                               xn1[:, k, tc * 512:(tc + 1) * 512],
                               start=(k == 0), stop=(k == NK - 1),
                               ctr=(cPE if k == NK - 1 else None))
                        ewts = [(sPE, cPE.v), (sPE, scores_last[h - 1])]
                        em("ac", lambda e,
                           o=dst[:, tc * 512:(tc + 1) * 512],
                           i=ps_qk[par][:]: e.activation(o, i, AF.Copy),
                           waits=ewts, ctr=cAC)
                        qk_evac[par] = cAC.v
                qk_mm_done[h] = cPE.v
                qk_ready = cAC.v

                hs0 = slice(h * 128, (h + 1) * 128)
                wt("pe", (sAC, qk_evac[0]), (sAC, qk_evac[1]),
                   (sDV, suf_read))
                for c in range(4, 16):
                    mm(ps_qk[0][:, c - 4:c - 3], v_tm[:, c, hs0], ones1[:],
                       start=True, stop=True,
                       ctr=(cPE if c == 15 else None))
                wt("dv", (sPE, cPE.v))
                for j in range(3):
                    em("dv", lambda e, o=suf_sb[:, j:j + 1],
                       i=ps_qk[0][:, 4 * j:12]:
                       e.reduce_sum(o, i, axis=mybir.AxisListType.X),
                       ctr=(cDV if j == 2 else None))
                suf_dv = cDV.v
                suf_dv_prev = suf_dv

                pend = None
                for qc in range(4):
                    c_hi = 4 * (qc + 1)
                    q_sl = slice(qc * 512, (qc + 1) * 512)
                    o_par = qc % 2
                    hs = slice(h * 128, (h + 1) * 128)
                    expm = {}
                    msm = {}

                    def emit_av(cav):
                        avw = [(sAC, expm[cav])] if cav in expm else []
                        if cav in msm:
                            avw.append((sDV, msm[cav]))
                        if cav == 0:
                            avw.append((sDV, o_evac[o_par]))
                        if avw:
                            wt("pe", *avw)
                        mm(ps_o[o_par][:], v_tm[:, cav, hs], at[:, cav, :],
                           start=(cav == 0), stop=(cav == c_hi - 1),
                           ctr=(cPE if cav == c_hi - 1 else None))

                    def emit_z(cz):
                        if cz == 0:
                            wt("pe", (sDV, z_ts_prev))
                        mm(ps_z[:], ones1[:], at[:, cz, :], start=(cz == 0),
                           stop=(cz == c_hi - 1),
                           ctr=(cPE if cz == c_hi - 1 else None))

                    for c in range(c_hi):
                        par = c % 2
                        wts = [(sAC, exp_all[par])]
                        if qc == 0 and c == 0:
                            wts.append((sAC, qk_ready))
                        wt("pe", *wts)
                        mm(ps_s[par][:], kh[:, c * 128:(c + 1) * 128],
                           qh[:, q_sl], start=True, stop=True, ctr=cPE)
                        sc = cPE.v
                        if c == 1 and pend is not None:
                            emit_tail(pend)
                            pend = None
                        band = c >= 4 * qc
                        dmark = 0
                        j0 = 0
                        if band:
                            j0 = c * 128 - qc * 512
                            em("dv", lambda e,
                               p=ps_s[par][:, j0:j0 + 128], t2=tri[:]:
                               e.tensor_tensor(p, p, t2, ALU.mult),
                               waits=[(sPE, sc), (sSY, consts_ld)], ctr=cDV)
                            dmark = cDV.v
                            if j0 > 0:
                                em("dv", lambda e, o=at[:, c, 0:j0]:
                                   e.memset(o, 1.0),
                                   waits=[(sPE, at_free)], ctr=cDV)
                                msm[c] = cDV.v
                        ewts = [(sPE, sc), (sPE, at_free)]
                        if dmark:
                            ewts.append((sDV, dmark))
                        em("ac", lambda e, o=at[:, c, j0:512],
                           i=ps_s[par][:, j0:512]:
                           e.activation(o, i, AF.Exp, scale=SM_SCALE),
                           waits=ewts, ctr=cAC)
                        exp_all[par] = cAC.v
                        expm[c] = cAC.v
                        if c >= 2:
                            emit_av(c - 2)
                            emit_z(c - 2)
                    for cav in range(c_hi - 2, c_hi):
                        emit_av(cav)
                        emit_z(cav)
                    z_done = cPE.v
                    at_free = z_done
                    cnt = float(2048 - c_hi * 128)
                    wt("dv", (sPE, z_done), (sAC, ln_done_prev))
                    em("dv", lambda e, o=zt[:], i=ps_z[:], c2=cnt:
                       e.tensor_scalar(o, i, c2, None, ALU.add), ctr=cDV)
                    z_ts_prev = cDV.v
                    wt("ac", (sDV, z_ts_prev), (sPE, zbmm_prev))
                    em("ac", lambda e, o=zt[:]: e.activation(o, o, AF.Ln))
                    em("ac", lambda e, o=zinv_bf[:], i=zt[:]:
                       e.activation(o, i, AF.Exp, scale=-1.0), ctr=cAC)
                    ln_done_prev = cAC.v
                    pend = (o_par, q_sl, h, cAC.v, qc, suf_dv)
                emit_tail(pend)
                pend = None
                suf_read = o_evac[1]
                scores_last[h] = cPE.v
                dma("sy", cSY, attn_scr[h], attn_h[h % 2][:],
                    waits=[(sDV, cDV.v)])
                attn_store[h % 2] = cSY.v
            flush()
        barrier()

        if upto == "attn":
            for hh in range(H):
                dma("gp", cGP, out[hh * 128:(hh + 1) * 128, :], attn_scr[hh])
            _finalize(ops, wt, barrier, flush, ALLC)
            return nc

        p123.close()

        p45 = es.enter_context(contextlib.ExitStack())
        r5_all = p45.enter_context(nc.sbuf_tensor("r5_all", [128, T], BF16))
        mrb5_all = p45.enter_context(nc.sbuf_tensor("mrb5_all", [128, T], BF16))

        with contextlib.ExitStack() as p4:
            attn_sb = p4.enter_context(
                nc.sbuf_tensor("attn_sb", [128, NK, T], BF16))
            y_half = p4.enter_context(
                nc.sbuf_tensor("y_half", [128, NK, 1024], F32))
            wob = [p4.enter_context(nc.sbuf_tensor(f"wob{i}", [128, D], BF16))
                   for i in range(2)]
            yb = [p4.enter_context(nc.sbuf_tensor(f"yb{i}", [128, 512], BF16))
                  for i in range(2)]
            ysq = [p4.enter_context(nc.sbuf_tensor(f"ysq{i}", [128, 512], BF16))
                   for i in range(2)]
            xf2 = [p4.enter_context(nc.sbuf_tensor(f"xf2{i}", [128, 1024], F32))
                   for i in range(2)]
            x1b = [p4.enter_context(
                nc.sbuf_tensor(f"x1b{i}", [128, 1024], BF16)) for i in range(2)]
            x1sq = [p4.enter_context(
                nc.sbuf_tensor(f"x1sq{i}", [128, 1024], BF16)) for i in range(2)]
            x1t = [p4.enter_context(nc.sbuf_tensor(f"x1t{i}", [128, 1024], F32))
                   for i in range(2)]
            m_b4 = p4.enter_context(nc.sbuf_tensor("m_b4", [128, 1024], F32))
            var_b4 = p4.enter_context(nc.sbuf_tensor("var_b4", [128, 1024], F32))
            nmsq4 = p4.enter_context(nc.sbuf_tensor("nmsq4", [128, 1024], F32))
            ps_y = [p4.enter_context(
                nc.psum_tensor(f"ps_y{i}", [128, 512], F32)) for i in range(2)]
            ps_s1 = [p4.enter_context(
                nc.psum_tensor(f"ps_s1{i}", [128, 512], F32)) for i in range(2)]
            ps_q1 = [p4.enter_context(
                nc.psum_tensor(f"ps_q1{i}", [128, 512], F32)) for i in range(2)]


            half_src = attn_scr.rearrange("h p t -> p h t")
            ald_cb = []
            for cb in range(4):
                cs = slice(cb * 512, (cb + 1) * 512)
                ceng, cctr2 = (("sy", cSY), ("ac", cAC16), ("gp", cGP),
                               ("sy", cSY))[cb]
                dma(ceng, cctr2, attn_sb[:, :, cs], half_src[:, :, cs])
                ald_cb.append((cctr2.sem, cctr2.v))

            y_evac = [0, 0]
            stats_mark = [0, 0]
            wo_mark = {}
            apm_all = {}
            apm = [0, 0]
            x1dma = [(None, 0), (None, 0)]
            x1stats_mark = [0, 0]

            def flush_ystats(p):
                (ptc, pm2, dvm, acm, extra) = p
                wt("pe", (sDV, dvm), (sAC, acm), *extra)
                mm(ps_s1[ptc][:], ones_st[:], yb[ptc][:],
                   start=(pm2 == 0), stop=(pm2 == NK - 1))
                mm(ps_q1[ptc][:], ones_st[:], ysq[ptc][:],
                   start=(pm2 == 0), stop=(pm2 == NK - 1), ctr=cPE)
                stats_mark[ptc] = cPE.v

            for half in range(2):
                base = half * 1024
                prev_stats_fin = cDV.v
                pend_st = None
                for m2 in range(NK):
                    hw = []
                    if (half, m2) != (0, 0):
                        pm2 = (half * NK + m2) - 2
                        if pm2 >= 0:
                            hw.append((sPE, wo_mark.get(pm2, 0)))
                    weng, wctr = (("gp", cGP), ("sy", cSY), ("ac", cAC16))[m2 % 3]
                    dma(weng, wctr, wob[m2 % 2][:], wo_scr[m2], waits=hw)
                    wldt = (wctr.sem, wctr.v)
                    for tc in range(2):
                        j = m2 * 2 + tc
                        par = j % 2
                        wts = [wldt, ald_cb[half * 2 + tc]]
                        if y_evac[par]:
                            wts.append((sAC, y_evac[par]))
                        wt("pe", *wts)
                        for k in range(NK):
                            mm(ps_y[par][:],
                               wob[m2 % 2][:, k * 128:(k + 1) * 128],
                               attn_sb[:, k, base + tc * 512:
                                       base + (tc + 1) * 512],
                               start=(k == 0), stop=(k == NK - 1),
                               ctr=(cPE if k == NK - 1 else None))
                        y_sl = y_half[:, m2, tc * 512:(tc + 1) * 512]
                        ew = [(sPE, cPE.v)]
                        if half == 1:
                            ew.append(apm_all[m2])
                        em("ac", lambda e, o=y_sl, i=ps_y[par][:]:
                           e.activation(o, i, AF.Copy), waits=ew, ctr=cAC)
                        y_evac[par] = cAC.v
                        sw = [(sAC, y_evac[par])]
                        if stats_mark[par]:
                            sw.append((sPE, stats_mark[par]))
                        em("dv", lambda e, o=yb[par][:], i=y_sl:
                           e.tensor_copy(o, i), waits=sw, ctr=cDV)
                        em("ac", lambda e, o=ysq[par][:], i=y_sl:
                           e.activation(o, i, AF.Square),
                           waits=([(sPE, stats_mark[par])]
                                  if stats_mark[par] else ()), ctr=cAC)
                        if pend_st is not None:
                            flush_ystats(pend_st)
                        extra = ([(sDV, prev_stats_fin)]
                                 if half == 1 and m2 == 0 else [])
                        pend_st = (tc, m2, cDV.v, cAC.v, extra)
                    wo_mark[half * NK + m2] = cPE.v

                flush_ystats(pend_st)
                pend_st = None
                fin = stats_finish(ps_s1, ps_q1, m_b4, var_b4, nmsq4, 512,
                                   cPE.v)
                r4_ac = cAC.v
                pend_x1 = None

                def flush_x1stats(p):
                    (pm2, dvm, acm) = p
                    ppar = pm2 % 2
                    pw = [(sDV, dvm), (sAC, acm)]
                    if pm2 == 0:
                        pw.append((sDV, fin))
                    wt("pe", *pw)
                    for tc2 in range(2):
                        s2 = slice(tc2 * 512, (tc2 + 1) * 512)
                        mm(ps_s1[tc2][:], ones_st[:], x1b[ppar][:, s2],
                           start=(pm2 == 0), stop=(pm2 == NK - 1))
                        mm(ps_q1[tc2][:], ones_st[:], x1sq[ppar][:, s2],
                           start=(pm2 == 0), stop=(pm2 == NK - 1),
                           ctr=(cPE if tc2 == 1 else None))
                    x1stats_mark[ppar] = cPE.v

                for m2 in range(NK):
                    par = m2 % 2
                    ceng, cctr = (("dv", cDV), ("gp", cPL))[m2 % 2]
                    rows = slice(m2 * 128, (m2 + 1) * 128)
                    cols = slice(base, base + 1024)
                    leng, lctr = (("sy", cSY), ("gp", cGP), ("ac", cAC16))[m2 % 3]
                    lw = []
                    if apm[par]:
                        lw.append(apm[par])
                    dma(leng, lctr, xf2[par][:], xT[rows, cols], waits=lw)
                    dwts = [(lctr.sem, lctr.v), (sAC, r4_ac), (sDV, fin)]
                    if x1dma[par][0] is not None:
                        dwts.append(x1dma[par])
                    wt(ceng, *dwts)
                    em(ceng, lambda e, o=y_half[:, m2, :], r=var_b4[:]:
                       e.tensor_tensor(o, o, r, ALU.mult))
                    em(ceng, lambda e, o=xf2[par][:], m=m_b4[:]:
                       e.tensor_tensor(o, o, m, ALU.subtract))
                    em(ceng, lambda e, o=x1t[par][:], a=y_half[:, m2, :],
                       b=xf2[par][:]: e.tensor_tensor(o, a, b, ALU.add),
                       ctr=cctr)
                    apm[par] = (cctr.sem, cctr.v)
                    apm_all[m2] = apm[par]
                    seng, sctr = (("ac", cAC16), ("sy", cSY), ("gp", cGP))[m2 % 3]
                    dma(seng, sctr, x1_scr[rows, cols], x1t[par][:],
                        waits=[apm[par]])
                    x1dma[par] = (sctr.sem, sctr.v)
                    xsw = [apm[par]]
                    if x1stats_mark[par]:
                        xsw.append((sPE, x1stats_mark[par]))
                    em("dv", lambda e, o=x1b[par][:], i=x1t[par][:]:
                       e.tensor_copy(o, i), waits=xsw, ctr=cDV)
                    em("ac", lambda e, o=x1sq[par][:], i=x1t[par][:]:
                       e.activation(o, i, AF.Square), waits=xsw, ctr=cAC)
                    if pend_x1 is not None:
                        flush_x1stats(pend_x1)
                    pend_x1 = (m2, cDV.v, cAC.v)
                flush_x1stats(pend_x1)
                pend_x1 = None
                fin5p = stats_finish(ps_s1, ps_q1, m_b4, var_b4, x1sq[0], 512,
                                     cPE.v)
                wt("dv", (sAC, cAC.v), (sDV, fin5p))
                em("dv", lambda e, o=r5_all[:, base:base + 1024], i=var_b4[:]:
                   e.tensor_copy(o, i))
                em("dv", lambda e, o=mrb5_all[:, base:base + 1024], i=m_b4[:]:
                   e.tensor_copy(o, i), ctr=cDV)
            flush()
        barrier()

        if upto == "x1":
            dma("sy", cSY, out[:, :], x1_scr[:, :])
            _finalize(ops, wt, barrier, flush, ALLC)
            return nc

        with contextlib.ExitStack() as p5:
            x1sb = p5.enter_context(nc.sbuf_tensor("x1sb", [128, NK, 512], F32))
            hbuf = p5.enter_context(nc.sbuf_tensor("hbuf", [128, NC, 512], BF16))
            y2 = p5.enter_context(nc.sbuf_tensor("y2", [128, NK, 512], BF16))
            xn2 = p5.enter_context(nc.sbuf_tensor("xn2", [128, NK, 512], BF16))
            xapp = [p5.enter_context(
                nc.sbuf_tensor(f"xapp{i}", [128, 512], F32)) for i in range(2)]
            wgub = [p5.enter_context(
                nc.sbuf_tensor(f"wgub{i}", [128, 2, D], BF16))
                for i in range(2)]
            wgu_i8 = [p5.enter_context(
                nc.sbuf_tensor(f"wgu_i8{i}", [128, 2, D], I8))
                for i in range(2)]
            wdb = [p5.enter_context(nc.sbuf_tensor(f"wdb{i}", [128, 8192], BF16))
                   for i in range(2)]
            sg_t = [p5.enter_context(nc.sbuf_tensor(f"sg_t{i}", [128, 512], BF16))
                    for i in range(2)]
            xq5 = [p5.enter_context(nc.sbuf_tensor(f"xq5{i}", [128, 512], BF16))
                   for i in range(2)]
            m_b5 = p5.enter_context(nc.sbuf_tensor("m_b5", [128, 512], F32))
            var_b5 = p5.enter_context(nc.sbuf_tensor("var_b5", [128, 512], F32))
            ps_a = p5.enter_context(nc.psum_tensor("ps_a", [128, 512], F32))
            ps_b = p5.enter_context(nc.psum_tensor("ps_b", [128, 512], F32))
            ps_g = [p5.enter_context(
                nc.psum_tensor(f"ps_g{i}", [128, 512], F32)) for i in range(2)]
            ps_u = [p5.enter_context(
                nc.psum_tensor(f"ps_u{i}", [128, 512], F32)) for i in range(2)]
            ps_d = [p5.enter_context(
                nc.psum_tensor(f"ps_d{i}", [128, 512], F32)) for i in range(2)]

            x2_all = {}
            gu_mark = [0, 0]
            h_mark = [0, 0]
            d_mark = [0, 0]
            y2_evac = [0, 0]
            st5_mark = [0, 0]
            apm5 = {}
            xld_g = []

            def emit_x1sb_load(dst_tt, gw):
                res = []
                cols2 = slice(dst_tt * 512, (dst_tt + 1) * 512)
                for g in range(4):
                    gsrc = x1_scr[g * 512:(g + 1) * 512, cols2].rearrange(
                        "(k p) t -> p k t", p=128)
                    geng, gctr = (("sy", cSY), ("ac", cAC16))[g % 2]
                    dma(geng, gctr, x1sb[:, 4 * g:4 * (g + 1), :], gsrc,
                        waits=gw)
                    res.append((gctr.sem, gctr.v))
                return res

            xld_g = emit_x1sb_load(0, [])
            ap5 = [0, 0]
            for k in range(NK):
                par = k % 2
                aw = [xld_g[k // 4]]
                if ap5[par]:
                    aw.append((sDV, ap5[par]))
                wt("dv", *aw)
                em("dv", lambda e, o=xapp[par][:], a=x1sb[:, k, :],
                   r=r5_all[:, 0:512]: e.tensor_tensor(o, a, r, ALU.mult))
                em("dv", lambda e, o=xn2[:, k, :], a=xapp[par][:],
                   m=mrb5_all[:, 0:512]: e.tensor_tensor(o, a, m, ALU.subtract),
                   ctr=cDV)
                ap5[par] = cDV.v
                apm5[k] = (sDV, cDV.v)
            xapp_free = (sDV, cDV.v)

            wgu_pre = {}
            gu_early = {}

            def emit_gu_group(m, wld, apm_map):
                par = m % 2
                pw = [wld]
                if m >= 2:
                    pw.append(apm_map[NK - 1])
                if h_mark[par]:
                    pw.append((sDV, h_mark[par]))
                wt("pe", *pw)
                for k in range(NK):
                    kw = [apm_map[k]] if m < 2 else ()
                    mm(ps_g[par][:], wgub[par][:, 0, k * 128:(k + 1) * 128],
                       xn2[:, k, :], start=(k == 0), stop=(k == NK - 1),
                       waits=kw, ctr=(cPE if k == NK - 1 else None))
                gmm = cPE.v
                for k in range(NK):
                    mm(ps_u[par][:], wgub[par][:, 1, k * 128:(k + 1) * 128],
                       xn2[:, k, :], start=(k == 0), stop=(k == NK - 1),
                       ctr=(cPE if k == NK - 1 else None))
                umm = cPE.v
                gu_mark[par] = cPE.v
                return gmm, umm

            def emit_wd_load(m2, dlw):
                par = m2 % 2
                dma("gp", cGP, wdb[par][:], wd_m[m2], waits=dlw)
                return (sGP, cGP.v)

            wgu_i8_free = [0, 0]

            def emit_wgu_load_gp(m):
                par = m % 2
                lw = []
                if gu_mark[par]:
                    lw.append((sPE, gu_mark[par]))
                dma("gp", cGP, wgub[par][:],
                    wgu_m[m].rearrange("a p f -> p a f"), waits=lw)
                return (sGP, cGP.v)

            def emit_wgu_load(m):
                par = m % 2
                weng, wctr = (("sy", cSY), ("ac", cAC16))[par]
                lw = []
                if wgu_i8_free[par]:
                    lw.append((sDV, wgu_i8_free[par]))
                dma(weng, wctr, wgu_i8[par][:],
                    wgu_m[m].rearrange("a p f -> p a f"), waits=lw)
                ew = [(wctr.sem, wctr.v)]
                if gu_mark[par]:
                    ew.append((sPE, gu_mark[par]))
                em("dv", lambda e, o=wgub[par][:], i=wgu_i8[par][:]:
                   e.tensor_copy(o, i), waits=ew, ctr=cDV)
                wgu_i8_free[par] = cDV.v
                return (sDV, cDV.v)

            def emit_gu_evac(m, gmm, umm, extra):
                par = m % 2
                em("ac", lambda e, o=sg_t[par][:], i=ps_g[par][:],
                   s=sg_sb[:, m:m + 1]:
                   e.activation(o, i, AF.Silu, scale=s),
                   waits=[(sPE, gmm), (sSY, consts_ld)] + list(extra),
                   ctr=cAC)
                em("dv", lambda e, o=hbuf[:, m, :], i=ps_u[par][:],
                   s=su_sb[:, m:m + 1], g=sg_t[par][:]:
                   e.scalar_tensor_tensor(o, i, s, g, ALU.mult, ALU.mult),
                   waits=[(sPE, umm), (sAC, cAC.v)] + list(extra), ctr=cDV)
                h_mark[par] = cDV.v

            for tt in range(4):
                cols = slice(tt * 512, (tt + 1) * 512)

                wd_pre = {}
                for m in range(NC):
                    par = m % 2
                    if m in gu_early:
                        gu_early.pop(m)
                        continue
                    if m in wgu_pre:
                        wld = wgu_pre.pop(m)
                    else:
                        wld = emit_wgu_load_gp(m)
                    gmm, umm = emit_gu_group(m, wld, apm5)
                    if m in (48, 56):
                        m2p = (m - 48) // 8
                        dlw = []
                        if d_mark[m2p % 2]:
                            dlw.append((sPE, d_mark[m2p % 2]))
                        wd_pre[m2p] = emit_wd_load(m2p, dlw)
                    emit_gu_evac(m, gmm, umm, ())
                h_all = cDV.v
                gu_last = cPE.v

                pend_d = None

                def flush_dstats(p):
                    (pm2, dvm, acm) = p
                    ppar = pm2 % 2
                    pw = [(sDV, dvm), (sAC, acm)]
                    if pm2 == 0 and tt > 0:
                        pw.append((sDV, prev_stats_fin5))
                    wt("pe", *pw)
                    mm(ps_a[:], ones_st[:], y2[:, pm2, :],
                       start=(pm2 == 0), stop=(pm2 == NK - 1))
                    mm(ps_b[:], ones_st[:], xq5[ppar][:],
                       start=(pm2 == 0), stop=(pm2 == NK - 1), ctr=cPE)
                    st5_mark[ppar] = cPE.v

                nap = apm5
                if tt < 3:
                    nap = {}
                    ncols = slice((tt + 1) * 512, (tt + 2) * 512)
                for m2 in range(NK):
                    par = m2 % 2
                    if tt < 3 and m2 < 8:
                        for k in (2 * m2, 2 * m2 + 1):
                            kp = k % 2
                            klw = [nap[k - 2]] if k >= 2 else [xapp_free]
                            dma(("sy", "ac")[kp],
                                (cSY, cAC16)[kp], xapp[kp][:],
                                x1_scr[k * 128:(k + 1) * 128, ncols],
                                waits=klw)
                            kld = ((sSY, cSY.v), (sAC16, cAC16.v))[kp]
                            wt("dv", kld, (sPE, gu_last))
                            em("dv", lambda e, o=xapp[kp][:],
                               r=r5_all[:, ncols]:
                               e.tensor_tensor(o, o, r, ALU.mult))
                            em("dv", lambda e, o=xn2[:, k, :], a=xapp[kp][:],
                               m=mrb5_all[:, ncols]:
                               e.tensor_tensor(o, a, m, ALU.subtract),
                               ctr=cDV)
                            nap[k] = (sDV, cDV.v)
                    if tt < 3 and m2 in (6, 7):
                        mpf = m2 - 6
                        wgu_pre[mpf] = emit_wgu_load_gp(mpf)
                    wld = wd_pre.pop(m2)
                    pw = [wld, (sDV, h_all)]
                    if y2_evac[par]:
                        pw.append((sDV, y2_evac[par]))
                    wt("pe", *pw)
                    for c in range(NC):
                        mm(ps_d[par][:], wdb[par][:, c * 128:(c + 1) * 128],
                           hbuf[:, c, :], start=(c == 0), stop=(c == NC - 1),
                           ctr=(cPE if c == NC - 1 else None))
                    d_mark[par] = cPE.v
                    if m2 + 2 < NK:
                        wd_pre[m2 + 2] = emit_wd_load(
                            m2 + 2, [(sPE, d_mark[par])])
                    if tt < 3 and m2 in (9, 10):
                        mpf = m2 - 9
                        gu_early[mpf] = emit_gu_group(
                            mpf, wgu_pre.pop(mpf), nap)
                    if tt < 3 and m2 in (11, 12):
                        wgu_pre[m2 - 9] = emit_wgu_load_gp(m2 - 9)
                    em("dv", lambda e, o=y2[:, m2, :], i=ps_d[par][:],
                       s=sd_sb[:, m2:m2 + 1]:
                       e.tensor_scalar(o, i, s, None, ALU.mult),
                       waits=[(sPE, d_mark[par]), (sSY, consts_ld)], ctr=cDV)
                    y2_evac[par] = cDV.v
                    em("ac", lambda e, o=xq5[par][:], i=y2[:, m2, :]:
                       e.activation(o, i, AF.Square),
                       waits=[(sDV, y2_evac[par])] + ([(sPE, st5_mark[par])]
                                                      if st5_mark[par] else []),
                       ctr=cAC)
                    if pend_d is not None:
                        flush_dstats(pend_d)
                    pend_d = (m2, y2_evac[par], cAC.v)
                flush_dstats(pend_d)
                pend_d = None
                d_last = cPE.v
                for mpf in sorted(gu_early):
                    gmm, umm = gu_early[mpf]
                    emit_gu_evac(mpf, gmm, umm,
                                 [(sPE, d_last), (sDV, h_mark[mpf % 2])])
                prev_stats_fin5 = stats_finish([ps_a], [ps_b], m_b5, var_b5,
                                               xapp[0], 512, cPE.v)
                r5c_ac = cAC.v
                for m2 in range(NK):
                    par = m2 % 2
                    rows = slice(m2 * 128, (m2 + 1) * 128)
                    aw = [(sAC, r5c_ac), (sDV, prev_stats_fin5),
                          xld_g[m2 // 4]]
                    wt("dv", *aw)
                    em("dv", lambda e, o=xapp[par][:], a=y2[:, m2, :],
                       r=var_b5[:]: e.tensor_tensor(o, a, r, ALU.mult))
                    em("dv", lambda e, o=x1sb[:, m2, :], m=m_b5[:]:
                       e.tensor_tensor(o, o, m, ALU.subtract))
                    em("dv", lambda e, o=x1sb[:, m2, :], a=xapp[par][:]:
                       e.tensor_tensor(o, o, a, ALU.add), ctr=cDV)
                    oeng, octr = (("sy", cSY), ("ac", cAC16))[m2 % 2]
                    dma(oeng, octr, out[rows, cols], x1sb[:, m2, :],
                        waits=[(sDV, cDV.v)])
                    x2_all[m2] = (octr.sem, octr.v)
                apm5 = nap
                xapp_free = (sDV, cDV.v)
                if tt < 3:
                    st_done = [(sSY, cSY.v), (sAC16, cAC16.v)]
                    xld_g = emit_x1sb_load(tt + 1, st_done)
            flush()

        _finalize(ops, wt, barrier, flush, ALLC)
    return nc


def _finalize(ops, wt, barrier, flush, ALLC):
    barrier()
    flush()


def _prep_inputs(L0_attn_x, L0_attn_Wq, L0_attn_Wk, L0_attn_Wv, L0_attn_Wo,
                 L0_ffn_w_gate, L0_ffn_w_up, L0_ffn_w_down,
                 L0_ffn_s_gate, L0_ffn_s_up, L0_ffn_s_down):
    c = np.ascontiguousarray
    wq = c(np.asarray(L0_attn_Wq, np.float32)
           .reshape(16, 128, 16, 128).transpose(2, 1, 0, 3)
           .reshape(16, 128, 2048))
    wk = c(np.asarray(L0_attn_Wk, np.float32)
           .reshape(16, 128, 16, 128).transpose(2, 1, 0, 3)
           .reshape(16, 128, 2048))
    wo = c(np.asarray(L0_attn_Wo, np.float32)
           .reshape(16, 128, 16, 128).transpose(2, 1, 0, 3)
           .reshape(16, 128, 2048))
    wv = c(np.asarray(L0_attn_Wv, np.float32))
    wg = (np.asarray(L0_ffn_w_gate, np.int8)
          .reshape(16, 128, 64, 128).transpose(2, 1, 0, 3)
          .reshape(64, 128, 2048))
    wu = (np.asarray(L0_ffn_w_up, np.int8)
          .reshape(16, 128, 64, 128).transpose(2, 1, 0, 3)
          .reshape(64, 128, 2048))
    wgu = c(np.stack([wg, wu], axis=1))
    wd = c(np.asarray(L0_ffn_w_down, np.int8)
           .reshape(64, 128, 16, 128).transpose(2, 1, 0, 3)
           .reshape(16, 128, 8192))
    sg = c(np.asarray(L0_ffn_s_gate, np.float32).reshape(64, 128).T)
    su = c(np.asarray(L0_ffn_s_up, np.float32).reshape(64, 128).T)
    sd = c(np.asarray(L0_ffn_s_down, np.float32).reshape(16, 128).T)
    tri = np.triu(np.ones((128, 128), np.float32))
    x = np.asarray(L0_attn_x, np.float32)
    shared = dict(wq_h=wq, wk_h=wk, wv=wv, wo_m=wo, wgu_m=wgu,
                  wd_m=wd, sg2d=sg, su2d=su, sd2d=sd, tri_in=tri)
    in_maps = []
    for b in range(8):
        m = dict(shared)
        m["xT"] = c(x[b].T)
        in_maps.append(m)
    return in_maps


_NC_CACHE = {}


def kernel(**inputs):
    in_maps = _prep_inputs(**inputs)
    if "full" not in _NC_CACHE:
        _NC_CACHE["full"] = build_nc("full")
    nc = _NC_CACHE["full"]
    res = run_bass_kernel_spmd(nc, in_maps, core_ids=list(range(8)))
    outs = [np.ascontiguousarray(np.asarray(r["out"], np.float32).T)
            for r in res.results]
    return np.stack(outs, axis=0)


# version 21
# speedup vs baseline: 1.1929x; 1.0912x over previous
import contextlib

import numpy as np
import concourse.bass as bass
import concourse.mybir as mybir
from concourse.bass_utils import run_bass_kernel_spmd

F32 = mybir.dt.float32
BF16 = mybir.dt.bfloat16
I8 = mybir.dt.int8
AF = mybir.ActivationFunctionType
ALU = mybir.AluOpType

D = 2048
T = 2048
H = 16
NK = D // 128
NC = 8192 // 128
SM_SCALE = float(1.0 / np.sqrt(2048.0))
EPS = 1e-5


class Ctr:
    def __init__(self, sem, step):
        self.sem = sem
        self.v = 0
        self.step = step


def build_nc(upto="full"):
    nc = bass.Bass("TRN2", target_bir_lowering=False)

    xT = nc.dram_tensor("xT", [D, T], F32, kind="ExternalInput").ap()
    wq_h = nc.dram_tensor("wq_h", [H, 128, D], F32, kind="ExternalInput").ap()
    wk_h = nc.dram_tensor("wk_h", [H, 128, D], F32, kind="ExternalInput").ap()
    wv = nc.dram_tensor("wv", [D, D], F32, kind="ExternalInput").ap()
    wo_m = nc.dram_tensor("wo_m", [NK, 128, D], F32, kind="ExternalInput").ap()
    wgu_m = nc.dram_tensor("wgu_m", [NC, 2, 128, D], I8,
                           kind="ExternalInput").ap()
    wd_m = nc.dram_tensor("wd_m", [NK, 128, 8192], I8, kind="ExternalInput").ap()
    sg2d = nc.dram_tensor("sg2d", [128, NC], F32, kind="ExternalInput").ap()
    su2d = nc.dram_tensor("su2d", [128, NC], F32, kind="ExternalInput").ap()
    sd2d = nc.dram_tensor("sd2d", [128, NK], F32, kind="ExternalInput").ap()
    tri_in = nc.dram_tensor("tri_in", [128, 128], F32, kind="ExternalInput").ap()

    out = nc.dram_tensor("out", [D, T], F32, kind="ExternalOutput").ap()

    x1_scr = nc.dram_tensor("x1_scr", [D, T], F32, kind="Internal").ap()
    wo_scr = nc.dram_tensor("wo_scr", [NK, 128, D], BF16, kind="Internal").ap()
    attn_scr = nc.dram_tensor("attn_scr", [H, 128, T], BF16, kind="Internal").ap()

    with contextlib.ExitStack() as es:
        tri = es.enter_context(nc.sbuf_tensor("tri", [128, 128], F32))
        ones_st = es.enter_context(nc.sbuf_tensor("ones_st", [128, 128], BF16))
        ones1 = es.enter_context(nc.sbuf_tensor("ones1", [128, 1], BF16))
        onesk1 = es.enter_context(nc.sbuf_tensor("onesk1", [1, 128], BF16))
        eps_t = es.enter_context(nc.sbuf_tensor("eps_t", [128, 1], F32))
        sg_sb = es.enter_context(nc.sbuf_tensor("sg_sb", [128, NC], F32))
        su_sb = es.enter_context(nc.sbuf_tensor("su_sb", [128, NC], F32))
        sd_sb = es.enter_context(nc.sbuf_tensor("sd_sb", [128, NK], F32))
        mx_sb = es.enter_context(nc.sbuf_tensor("mx_sb", [128, T], BF16))

        sGP = es.enter_context(nc.semaphore("sGP"))
        sSY = es.enter_context(nc.semaphore("sSY"))
        sPE = es.enter_context(nc.semaphore("sPE"))
        sDV = es.enter_context(nc.semaphore("sDV"))
        sAC = es.enter_context(nc.semaphore("sAC"))
        sAC16 = es.enter_context(nc.semaphore("sAC16"))
        sPL = es.enter_context(nc.semaphore("sPL"))

        cGP, cSY = Ctr(sGP, 16), Ctr(sSY, 16)
        cPE, cDV, cAC = Ctr(sPE, 1), Ctr(sDV, 1), Ctr(sAC, 1)
        cAC16, cPL = Ctr(sAC16, 16), Ctr(sPL, 1)
        ALLC = [cGP, cSY, cPE, cDV, cAC, cAC16, cPL]

        block = es.enter_context(nc.Block())

        ops = {"gp": [], "sy": [], "pe": [], "dv": [], "ac": []}

        def flush():
            if ops["gp"]:
                block.gpsimd(lambda e, fns=tuple(ops["gp"]): [f(e) for f in fns])
            if ops["sy"]:
                block.sync(lambda e, fns=tuple(ops["sy"]): [f(e) for f in fns])
            if ops["pe"]:
                block.tensor(lambda e, fns=tuple(ops["pe"]): [f(e) for f in fns])
            if ops["dv"]:
                block.vector(lambda e, fns=tuple(ops["dv"]): [f(e) for f in fns])
            if ops["ac"]:
                block.scalar(lambda e, fns=tuple(ops["ac"]): [f(e) for f in fns])
            for k in ops:
                ops[k] = []

        def _run(eng, waits, make, inc):
            for (sem, v) in waits:
                if v > 0:
                    eng.wait_ge(sem, v)
            ins = make(eng)
            if inc is not None:
                ins.then_inc(inc[0], inc[1])

        def em(engname, make, waits=(), ctr=None):
            inc = None
            if ctr is not None:
                ctr.v += ctr.step
                inc = (ctr.sem, ctr.step)
            ops[engname].append(
                lambda eng, w=tuple(waits), m=make, i=inc: _run(eng, w, m, i))

        def wt(engname, *waits):
            ops[engname].append(
                lambda eng, w=tuple(waits): [eng.wait_ge(s, v)
                                             for (s, v) in w if v > 0])

        def barrier():
            snap = [(c.sem, c.v) for c in ALLC]
            for e in ("gp", "sy", "pe", "dv", "ac"):
                wt(e, *snap)

        def dma(engname, ctr, dst, src, waits=()):
            em(engname, lambda e, d=dst, s=src: e.dma_start(out=d, in_=s),
               waits=waits, ctr=ctr)

        def warm(psum_t, rhs_ap, *waits):
            em("pe", lambda e, p=psum_t[0:64, 0:64], r=rhs_ap:
               e.matmul(p, ones_st[:, 0:64], r, start=True, stop=True,
                        skip_group_check=True), waits=list(waits))

        def mm(psum, lhsT, rhs, start, stop, waits=(), ctr=None):
            em("pe", lambda e, p=psum, l=lhsT, r=rhs, s0=start, s1=stop:
               e.matmul(p, l, r, start=s0, stop=s1, skip_group_check=True),
               waits=waits, ctr=ctr)

        def stats_finish(ps_sum, ps_sq, m_b, var_b, nmsq, w_each, pe_done,
                         save_m=None, extra=()):
            wt("dv", (sPE, pe_done), *extra)
            for tc in range(len(ps_sum)):
                sl = slice(tc * w_each, (tc + 1) * w_each)
                em("dv", lambda e, o=m_b[:, sl], i=ps_sum[tc][:]:
                   e.tensor_scalar(o, i, 1.0 / D, None, ALU.mult))
                em("dv", lambda e, o=var_b[:, sl], i=ps_sq[tc][:]:
                   e.tensor_scalar(o, i, 1.0 / D, None, ALU.mult))
            if save_m is not None:
                em("dv", lambda e, o=save_m, i=m_b[:]: e.tensor_copy(o, i))
            em("dv", lambda e, o=nmsq[:], a=m_b[:]:
               e.scalar_tensor_tensor(o, a, -1.0, a, ALU.mult, ALU.mult))
            em("dv", lambda e, o=var_b[:], a=nmsq[:]:
               e.tensor_tensor(o, o, a, ALU.add), ctr=cDV)
            wt("ac", (sDV, cDV.v))
            em("ac", lambda e, o=var_b[:]:
               e.activation(o, o, AF.Ln, bias=eps_t[:]))
            em("ac", lambda e, o=var_b[:]:
               e.activation(o, o, AF.Exp, scale=-0.5), ctr=cAC)
            wt("dv", (sAC, cAC.v))
            em("dv", lambda e, o=m_b[:], b=var_b[:]:
               e.tensor_tensor(o, o, b, ALU.mult), ctr=cDV)
            return cDV.v

        dma("sy", cSY, tri[:], tri_in)
        dma("sy", cSY, sg_sb[:], sg2d)
        dma("sy", cSY, su_sb[:], su2d)
        dma("sy", cSY, sd_sb[:], sd2d)
        consts_ld = cSY.v
        em("dv", lambda e: e.memset(ones_st[:], 1.0))
        em("dv", lambda e: e.memset(ones1[:], 1.0))
        em("dv", lambda e: e.memset(onesk1[:], 1.0))
        em("dv", lambda e: e.memset(eps_t[:], EPS), ctr=cDV)
        consts_dv = cDV.v

        p123 = es.enter_context(contextlib.ExitStack())
        xn1 = p123.enter_context(nc.sbuf_tensor("xn1", [128, NK, T], BF16))
        v_tm = p123.enter_context(nc.sbuf_tensor("v_tm", [128, NK, D], BF16))
        wqb = [p123.enter_context(nc.sbuf_tensor(f"wqb{i}", [128, D], BF16))
               for i in range(2)]
        wkb = [p123.enter_context(nc.sbuf_tensor(f"wkb{i}", [128, D], BF16))
               for i in range(2)]
        wostg = [p123.enter_context(
            nc.sbuf_tensor(f"wostg{i}", [128, D], BF16)) for i in range(2)]
        wv_ld = {}
        qk_ld = {}

        with contextlib.ExitStack() as p1:
            xsq = [p1.enter_context(nc.sbuf_tensor(f"xsq{i}", [128, T], BF16))
                   for i in range(2)]
            xf = [p1.enter_context(nc.sbuf_tensor(f"xf{i}", [128, T], F32))
                  for i in range(2)]
            m_b = p1.enter_context(nc.sbuf_tensor("m_b", [128, T], F32))
            var_b = p1.enter_context(nc.sbuf_tensor("var_b", [128, T], F32))
            ps_sum = [p1.enter_context(
                nc.psum_tensor(f"ps_sum{i}", [128, 512], F32)) for i in range(4)]
            ps_sq = [p1.enter_context(
                nc.psum_tensor(f"ps_sq{i}", [128, 512], F32)) for i in range(4)]

            cp_guard = {}
            ld_gp = {}
            for k in range(8):
                dma("gp", cGP, xn1[:, k, :], xT[k * 128:(k + 1) * 128, :])
                ld_gp[k] = cGP.v
            ld_sy = {}

            ready = {}
            for k in range(8):
                ready[k] = (sGP, ld_gp[k])
            for k in range(8, NK):
                lw = []
                if k >= 10:
                    lw.append((sDV, cp_guard[k - 2]))
                dma("ac", cAC16, xf[k % 2][:], xT[k * 128:(k + 1) * 128, :],
                    waits=lw)
                ld_sy[k] = cAC16.v
                em("dv", lambda e, o=xn1[:, k, :], i=xf[k % 2][:]:
                   e.tensor_copy(o, i), waits=[(sAC16, ld_sy[k])], ctr=cDV)
                cp_guard[k] = cDV.v
                ready[k] = (sDV, cDV.v)

            pe_k = []
            for k in range(NK):
                wts = [ready[k], (sDV, consts_dv)]
                if k >= 2:
                    wts.append((sPE, pe_k[k - 2]))
                em("ac", lambda e, o=xsq[k % 2][:], i=xn1[:, k, :]:
                   e.activation(o, i, AF.Square), waits=wts, ctr=cAC)
                wt("pe", ready[k], (sAC, cAC.v))
                for tc in range(4):
                    sl = slice(tc * 512, (tc + 1) * 512)
                    mm(ps_sum[tc][:], ones_st[:], xn1[:, k, sl],
                       start=(k == 0), stop=(k == NK - 1))
                    mm(ps_sq[tc][:], ones_st[:], xsq[k % 2][:, sl],
                       start=(k == 0), stop=(k == NK - 1),
                       ctr=(cPE if tc == 3 else None))
                pe_k.append(cPE.v)

            ln1_ready = stats_finish(ps_sum, ps_sq, m_b, var_b, xsq[0], 512,
                                     cPE.v, save_m=mx_sb[:])
            r_ready_ac = cAC.v

            ap_mark = []
            last_cp = cDV.v
            POOLK = {3, 6, 9, 12, 15}
            for k in range(NK):
                wts = []
                if k < 2:
                    wts.append((sDV, last_cp))
                if k >= 2:
                    wts.append(ap_mark[k - 2])
                dma("sy", cSY, xf[k % 2][:], xT[k * 128:(k + 1) * 128, :],
                    waits=wts)
                eng = "gp" if k in POOLK else "dv"
                ctr = cPL if k in POOLK else cDV
                wt(eng, (sSY, cSY.v), (sAC, r_ready_ac), (sDV, ln1_ready))
                em(eng, lambda e, b=xf[k % 2][:], r=var_b[:]:
                   e.tensor_tensor(b, b, r, ALU.mult))
                em(eng, lambda e, o=xn1[:, k, :], b=xf[k % 2][:], m=m_b[:]:
                   e.tensor_tensor(o, b, m, ALU.subtract), ctr=ctr)
                ap_mark.append((ctr.sem, ctr.v))
                warm(ps_sum[0], xsq[0][:, 0:64], ap_mark[-1])
            flush()
        barrier()

        if upto == "ln1":
            for k in range(NK):
                dma("gp", cGP, out[k * 128:(k + 1) * 128, :], xn1[:, k, :])
            _finalize(ops, wt, barrier, flush, ALLC)
            return nc

        with contextlib.ExitStack() as p2:
            wvb = [p2.enter_context(
                nc.sbuf_tensor(f"wvb{i}", [128, NK, 512], BF16))
                for i in range(2)]
            ps_v = [p2.enter_context(
                nc.psum_tensor(f"ps_v{i}", [128, 512], F32)) for i in range(2)]
            evac = []
            vq_mark = []
            for q in range(4):
                cols = slice(q * 512, (q + 1) * 512)
                src2 = wv[:, cols].rearrange("(k p) n -> p k n", p=128)
                lw = [] if q < 2 else [(sPE, vq_mark[q - 2])]
                dma("gp", cGP, wvb[q % 2][:], src2, waits=lw)
                wld = cGP.v
                for st in range(NK):
                    j = q * NK + st
                    par = j % 2
                    wts = [(sGP, wld)]
                    if j >= 2:
                        wts.append((sAC, evac[j - 2]))
                    wt("pe", *wts)
                    for k in range(NK):
                        mm(ps_v[par][:],
                           xn1[:, k, st * 128:(st + 1) * 128],
                           wvb[q % 2][:, k, :],
                           start=(k == 0), stop=(k == NK - 1),
                           ctr=(cPE if k == NK - 1 else None))
                    dst = v_tm[:, st, q * 512:(q + 1) * 512]
                    em("ac", lambda e, o=dst, i=ps_v[par][:]:
                       e.activation(o, i, AF.Copy),
                       waits=[(sPE, cPE.v)], ctr=cAC)
                    evac.append(cAC.v)
                vq_mark.append(cPE.v)
            for h2 in range(2):
                dma("gp", cGP, wqb[h2][:], wq_h[h2])
                dma("gp", cGP, wkb[h2][:], wk_h[h2])
                qk_ld[h2] = cGP.v
            flush()
        barrier()

        with contextlib.ExitStack() as p3:
            qh = p3.enter_context(nc.sbuf_tensor("qh", [128, T], BF16))
            kh = p3.enter_context(nc.sbuf_tensor("kh", [128, T], BF16))
            at = p3.enter_context(nc.sbuf_tensor("at", [128, 16, 512], BF16))
            attn_h = [p3.enter_context(
                nc.sbuf_tensor(f"attn_h{i}", [128, T], BF16)) for i in range(2)]
            zt = p3.enter_context(nc.sbuf_tensor("zt", [1, 512], F32))
            suf_sb = p3.enter_context(nc.sbuf_tensor("suf_sb", [128, 4], F32))
            zinv_bf = p3.enter_context(nc.sbuf_tensor("zinv_bf", [1, 512], BF16))
            zb_sb = p3.enter_context(nc.sbuf_tensor("zb_sb", [128, 512], BF16))
            ps_qk = [p3.enter_context(
                nc.psum_tensor(f"ps_qk{i}", [128, 512], F32)) for i in range(2)]
            ps_s = [p3.enter_context(
                nc.psum_tensor(f"ps_s{i}", [128, 512], F32)) for i in range(2)]
            ps_o = [p3.enter_context(
                nc.psum_tensor(f"ps_o{i}", [128, 512], F32)) for i in range(2)]
            ps_z = p3.enter_context(nc.psum_tensor("ps_z", [1, 512], F32))
            ps_zb = p3.enter_context(nc.psum_tensor("ps_zb", [128, 512], F32))

            qk_mm_done = {}
            scores_last = {-1: 0}
            exp_all = [0, 0]
            qk_evac = [0, 0]
            at_free = 0
            o_evac = [0, 0]
            z_ts_prev = 0
            ln_done_prev = 0
            zbmm_prev = 0
            attn_store = [0, 0]
            suf_read = 0
            suf_dv_prev = 0

            def emit_tail(pend):
                nonlocal zbmm_prev
                (o_par_p, q_sl_p, h_p, zinv_mark, qc_p, sufdv) = pend
                wt("pe", (sAC, zinv_mark), (sDV, o_evac[1 - o_par_p]))
                mm(ps_zb[:], onesk1[:], zinv_bf[:], start=True, stop=True,
                   ctr=cPE)
                zbmm_prev = cPE.v
                wts = [(sPE, zbmm_prev), (sSY, attn_store[h_p % 2])]
                if sufdv:
                    wts.append((sDV, sufdv))
                wt("dv", *wts)
                em("dv", lambda e, o=zb_sb[:], i=ps_zb[:]: e.tensor_copy(o, i))
                sufop = suf_sb[:, qc_p:qc_p + 1] if qc_p < 3 else 0.0
                em("dv", lambda e, o=attn_h[h_p % 2][:, q_sl_p],
                   i=ps_o[o_par_p][:], z2=zb_sb[:], s3=sufop:
                   e.scalar_tensor_tensor(o, i, s3, z2, ALU.add, ALU.mult),
                   ctr=cDV)
                o_evac[o_par_p] = cDV.v

            wo_stg_mark = [0, 0]
            for h in range(H):
                if h in qk_ld:
                    wld = qk_ld[h]
                else:
                    hw = [(sPE, qk_mm_done[h - 2])]
                    dma("gp", cGP, wqb[h % 2][:], wq_h[h], waits=hw)
                    dma("gp", cGP, wkb[h % 2][:], wk_h[h])
                    wld = cGP.v
                sw = [] if wo_stg_mark[h % 2] == 0 else                     [(sSY, wo_stg_mark[h % 2])]
                dma("gp", cGP, wostg[h % 2][:], wo_m[h], waits=sw)
                dma("sy", cSY, wo_scr[h], wostg[h % 2][:],
                    waits=[(sGP, cGP.v)])
                wo_stg_mark[h % 2] = cSY.v

                for which, (wbuf, dst) in enumerate(
                        ((wqb, qh), (wkb, kh))):
                    for tc in range(4):
                        j = which * 4 + tc
                        par = j % 2
                        wts = [(sGP, wld)]
                        if qk_evac[par]:
                            wts.append((sAC, qk_evac[par]))
                        if which == 0 and tc == 0 and suf_dv_prev:
                            wts.append((sDV, suf_dv_prev))
                        wt("pe", *wts)
                        for k in range(NK):
                            mm(ps_qk[par][:],
                               wqb[h % 2][:, k * 128:(k + 1) * 128] if which == 0
                               else wkb[h % 2][:, k * 128:(k + 1) * 128],
                               xn1[:, k, tc * 512:(tc + 1) * 512],
                               start=(k == 0), stop=(k == NK - 1),
                               ctr=(cPE if k == NK - 1 else None))
                        ewts = [(sPE, cPE.v), (sPE, scores_last[h - 1])]
                        em("ac", lambda e,
                           o=dst[:, tc * 512:(tc + 1) * 512],
                           i=ps_qk[par][:]: e.activation(o, i, AF.Copy),
                           waits=ewts, ctr=cAC)
                        qk_evac[par] = cAC.v
                qk_mm_done[h] = cPE.v
                qk_ready = cAC.v

                hs0 = slice(h * 128, (h + 1) * 128)
                wt("pe", (sAC, qk_evac[0]), (sAC, qk_evac[1]),
                   (sDV, suf_read))
                for c in range(4, 16):
                    mm(ps_qk[0][:, c - 4:c - 3], v_tm[:, c, hs0], ones1[:],
                       start=True, stop=True,
                       ctr=(cPE if c == 15 else None))
                wt("dv", (sPE, cPE.v))
                for j in range(3):
                    em("dv", lambda e, o=suf_sb[:, j:j + 1],
                       i=ps_qk[0][:, 4 * j:12]:
                       e.reduce_sum(o, i, axis=mybir.AxisListType.X),
                       ctr=(cDV if j == 2 else None))
                suf_dv = cDV.v
                suf_dv_prev = suf_dv

                pend = None
                for qc in range(4):
                    c_hi = 4 * (qc + 1)
                    q_sl = slice(qc * 512, (qc + 1) * 512)
                    o_par = qc % 2
                    hs = slice(h * 128, (h + 1) * 128)
                    expm = {}
                    msm = {}

                    def emit_av(cav):
                        avw = [(sAC, expm[cav])] if cav in expm else []
                        if cav in msm:
                            avw.append((sDV, msm[cav]))
                        if cav == 0:
                            avw.append((sDV, o_evac[o_par]))
                        if avw:
                            wt("pe", *avw)
                        mm(ps_o[o_par][:], v_tm[:, cav, hs], at[:, cav, :],
                           start=(cav == 0), stop=(cav == c_hi - 1),
                           ctr=(cPE if cav == c_hi - 1 else None))

                    def emit_z(cz):
                        if cz == 0:
                            wt("pe", (sDV, z_ts_prev))
                        mm(ps_z[:], ones1[:], at[:, cz, :], start=(cz == 0),
                           stop=(cz == c_hi - 1),
                           ctr=(cPE if cz == c_hi - 1 else None))

                    for c in range(c_hi):
                        par = c % 2
                        wts = [(sAC, exp_all[par])]
                        if qc == 0 and c == 0:
                            wts.append((sAC, qk_ready))
                        wt("pe", *wts)
                        mm(ps_s[par][:], kh[:, c * 128:(c + 1) * 128],
                           qh[:, q_sl], start=True, stop=True, ctr=cPE)
                        sc = cPE.v
                        if c == 1 and pend is not None:
                            emit_tail(pend)
                            pend = None
                        band = c >= 4 * qc
                        dmark = 0
                        j0 = 0
                        if band:
                            j0 = c * 128 - qc * 512
                            em("dv", lambda e,
                               p=ps_s[par][:, j0:j0 + 128], t2=tri[:]:
                               e.tensor_tensor(p, p, t2, ALU.mult),
                               waits=[(sPE, sc), (sSY, consts_ld)], ctr=cDV)
                            dmark = cDV.v
                            if j0 > 0:
                                em("dv", lambda e, o=at[:, c, 0:j0]:
                                   e.memset(o, 1.0),
                                   waits=[(sPE, at_free)], ctr=cDV)
                                msm[c] = cDV.v
                        ewts = [(sPE, sc), (sPE, at_free)]
                        if dmark:
                            ewts.append((sDV, dmark))
                        em("ac", lambda e, o=at[:, c, j0:512],
                           i=ps_s[par][:, j0:512]:
                           e.activation(o, i, AF.Exp, scale=SM_SCALE),
                           waits=ewts, ctr=cAC)
                        exp_all[par] = cAC.v
                        expm[c] = cAC.v
                        if c >= 2:
                            emit_av(c - 2)
                            emit_z(c - 2)
                    for cav in range(c_hi - 2, c_hi):
                        emit_av(cav)
                        emit_z(cav)
                    z_done = cPE.v
                    at_free = z_done
                    cnt = float(2048 - c_hi * 128)
                    wt("dv", (sPE, z_done), (sAC, ln_done_prev))
                    em("dv", lambda e, o=zt[:], i=ps_z[:], c2=cnt:
                       e.tensor_scalar(o, i, c2, None, ALU.add), ctr=cDV)
                    z_ts_prev = cDV.v
                    wt("ac", (sDV, z_ts_prev), (sPE, zbmm_prev))
                    em("ac", lambda e, o=zt[:]: e.activation(o, o, AF.Ln))
                    em("ac", lambda e, o=zinv_bf[:], i=zt[:]:
                       e.activation(o, i, AF.Exp, scale=-1.0), ctr=cAC)
                    ln_done_prev = cAC.v
                    pend = (o_par, q_sl, h, cAC.v, qc, suf_dv)
                emit_tail(pend)
                pend = None
                suf_read = o_evac[1]
                scores_last[h] = cPE.v
                dma("sy", cSY, attn_scr[h], attn_h[h % 2][:],
                    waits=[(sDV, cDV.v)])
                attn_store[h % 2] = cSY.v
            flush()
        barrier()

        if upto == "attn":
            for hh in range(H):
                dma("gp", cGP, out[hh * 128:(hh + 1) * 128, :], attn_scr[hh])
            _finalize(ops, wt, barrier, flush, ALLC)
            return nc

        p123.close()

        p45 = es.enter_context(contextlib.ExitStack())
        r5_all = p45.enter_context(nc.sbuf_tensor("r5_all", [128, T], BF16))
        mrb5_all = p45.enter_context(nc.sbuf_tensor("mrb5_all", [128, T], BF16))

        with contextlib.ExitStack() as p4:
            attn_sb = p4.enter_context(
                nc.sbuf_tensor("attn_sb", [128, NK, T], BF16))
            y_half = p4.enter_context(
                nc.sbuf_tensor("y_half", [128, NK, 1024], F32))
            wob = [p4.enter_context(nc.sbuf_tensor(f"wob{i}", [128, D], BF16))
                   for i in range(2)]
            yb = [p4.enter_context(nc.sbuf_tensor(f"yb{i}", [128, 512], BF16))
                  for i in range(2)]
            ysq = [p4.enter_context(nc.sbuf_tensor(f"ysq{i}", [128, 512], BF16))
                   for i in range(2)]
            xf2 = [p4.enter_context(nc.sbuf_tensor(f"xf2{i}", [128, 1024], F32))
                   for i in range(2)]
            x1sq = [p4.enter_context(
                nc.sbuf_tensor(f"x1sq{i}", [128, 1024], BF16)) for i in range(2)]
            x1t = [p4.enter_context(nc.sbuf_tensor(f"x1t{i}", [128, 1024], F32))
                   for i in range(2)]
            m_b4 = p4.enter_context(nc.sbuf_tensor("m_b4", [128, 1024], F32))
            var_b4 = p4.enter_context(nc.sbuf_tensor("var_b4", [128, 1024], F32))
            var_b4x = p4.enter_context(
                nc.sbuf_tensor("var_b4x", [128, 1024], F32))
            nmsq4 = p4.enter_context(nc.sbuf_tensor("nmsq4", [128, 1024], F32))
            ps_y = [p4.enter_context(
                nc.psum_tensor(f"ps_y{i}", [128, 512], F32)) for i in range(2)]
            ps_s1 = [p4.enter_context(
                nc.psum_tensor(f"ps_s1{i}", [128, 512], F32)) for i in range(2)]
            ps_q1 = [p4.enter_context(
                nc.psum_tensor(f"ps_q1{i}", [128, 512], F32)) for i in range(2)]
            ps_x1q = [p4.enter_context(
                nc.psum_tensor(f"ps_x1q{i}", [128, 512], F32))
                for i in range(2)]


            half_src = attn_scr.rearrange("h p t -> p h t")
            ald_cb = []
            for cb in range(4):
                cs = slice(cb * 512, (cb + 1) * 512)
                ceng, cctr2 = (("sy", cSY), ("ac", cAC16), ("gp", cGP),
                               ("sy", cSY))[cb]
                dma(ceng, cctr2, attn_sb[:, :, cs], half_src[:, :, cs])
                ald_cb.append((cctr2.sem, cctr2.v))

            y_evac = [0, 0]
            stats_mark = [0, 0]
            wo_mark = {}
            apm_all = {}
            apm = [0, 0]
            x1dma = [(None, 0), (None, 0)]
            x1q_mark = [0, 0]
            half_fin = {}
            finx_mark = [0]
            pend_st = [None]
            pend_x1q = [None]

            def flush_ystats():
                (ptc, pm2, dvm, acm, extra) = pend_st[0]
                pend_st[0] = None
                wt("pe", (sDV, dvm), (sAC, acm), *extra)
                mm(ps_s1[ptc][:], ones_st[:], yb[ptc][:],
                   start=(pm2 == 0), stop=(pm2 == NK - 1))
                mm(ps_q1[ptc][:], ones_st[:], ysq[ptc][:],
                   start=(pm2 == 0), stop=(pm2 == NK - 1), ctr=cPE)
                stats_mark[ptc] = cPE.v

            def flush_x1q():
                (pm2, acm, extra) = pend_x1q[0]
                pend_x1q[0] = None
                ppar = pm2 % 2
                wt("pe", (sAC, acm), *extra)
                for tc2 in range(2):
                    s2 = slice(tc2 * 512, (tc2 + 1) * 512)
                    mm(ps_x1q[tc2][:], ones_st[:], x1sq[ppar][:, s2],
                       start=(pm2 == 0), stop=(pm2 == NK - 1),
                       ctr=(cPE if tc2 == 1 else None))
                x1q_mark[ppar] = cPE.v

            def emit_wo_iter(half, m2, fin0dv):
                base = half * 1024
                hw = []
                gi = half * NK + m2
                if gi >= 2:
                    hw.append((sPE, wo_mark.get(gi - 2, 0)))
                weng, wctr = (("gp", cGP), ("sy", cSY), ("ac", cAC16))[m2 % 3]
                dma(weng, wctr, wob[m2 % 2][:], wo_scr[m2], waits=hw)
                wldt = (wctr.sem, wctr.v)
                for tc in range(2):
                    par = tc
                    wts = [wldt, ald_cb[half * 2 + tc]]
                    if y_evac[par]:
                        wts.append((sAC, y_evac[par]))
                    wt("pe", *wts)
                    for k in range(NK):
                        mm(ps_y[par][:],
                           wob[m2 % 2][:, k * 128:(k + 1) * 128],
                           attn_sb[:, k, base + tc * 512:
                                   base + (tc + 1) * 512],
                           start=(k == 0), stop=(k == NK - 1),
                           ctr=(cPE if k == NK - 1 else None))
                    y_sl = y_half[:, m2, tc * 512:(tc + 1) * 512]
                    ew = [(sPE, cPE.v)]
                    if half == 1:
                        ew.append(apm_all[m2])
                    em("ac", lambda e, o=y_sl, i=ps_y[par][:]:
                       e.activation(o, i, AF.Copy), waits=ew, ctr=cAC)
                    y_evac[par] = cAC.v
                    sw = [(sAC, y_evac[par])]
                    if stats_mark[par]:
                        sw.append((sPE, stats_mark[par]))
                    em("dv", lambda e, o=yb[par][:], i=y_sl:
                       e.tensor_copy(o, i), waits=sw, ctr=cDV)
                    em("ac", lambda e, o=ysq[par][:], i=y_sl:
                       e.activation(o, i, AF.Square),
                       waits=([(sPE, stats_mark[par])]
                              if stats_mark[par] else ()), ctr=cAC)
                    if pend_st[0] is not None:
                        flush_ystats()
                    extra = ([(sDV, fin0dv)]
                             if half == 1 and m2 == 0 else [])
                    pend_st[0] = (tc, m2, cDV.v, cAC.v, extra)
                wo_mark[gi] = cPE.v

            def emit_chain(half, m2):
                base = half * 1024
                par = m2 % 2
                ceng, cctr = (("dv", cDV), ("gp", cPL))[par]
                fin, r4a = half_fin[half]
                rows = slice(m2 * 128, (m2 + 1) * 128)
                cols = slice(base, base + 1024)
                leng, lctr = (("sy", cSY), ("gp", cGP), ("ac", cAC16))[m2 % 3]
                lw = []
                if apm[par]:
                    lw.append(apm[par])
                dma(leng, lctr, xf2[par][:], xT[rows, cols], waits=lw)
                dwts = [(lctr.sem, lctr.v), (sAC, r4a), (sDV, fin)]
                if x1dma[par][0] is not None:
                    dwts.append(x1dma[par])
                wt(ceng, *dwts)
                em(ceng, lambda e, o=y_half[:, m2, :], r=var_b4[:]:
                   e.tensor_tensor(o, o, r, ALU.mult))
                em(ceng, lambda e, o=xf2[par][:], m=m_b4[:]:
                   e.tensor_tensor(o, o, m, ALU.subtract))
                em(ceng, lambda e, o=x1t[par][:], a=y_half[:, m2, :],
                   b=xf2[par][:]: e.tensor_tensor(o, a, b, ALU.add),
                   ctr=cctr)
                apm[par] = (cctr.sem, cctr.v)
                apm_all[m2] = apm[par]
                seng, sctr = (("ac", cAC16), ("sy", cSY), ("gp", cGP))[m2 % 3]
                dma(seng, sctr, x1_scr[rows, cols], x1t[par][:],
                    waits=[apm[par]])
                x1dma[par] = (sctr.sem, sctr.v)
                xsw = [apm[par]]
                if x1q_mark[par]:
                    xsw.append((sPE, x1q_mark[par]))
                em("ac", lambda e, o=x1sq[par][:], i=x1t[par][:]:
                   e.activation(o, i, AF.Square), waits=xsw, ctr=cAC)
                if pend_x1q[0] is not None:
                    flush_x1q()
                extra = ([(sDV, finx_mark[0])]
                         if half == 1 and m2 == 0 else [])
                pend_x1q[0] = (m2, cAC.v, extra)

            def finx(half):
                base = half * 1024
                sl_t = slice(base, base + 1024)
                mxs = mx_sb[:, sl_t]
                r5s = r5_all[:, sl_t]
                wt("dv", (sPE, cPE.v))
                for tc2 in range(2):
                    s2 = slice(tc2 * 512, (tc2 + 1) * 512)
                    em("dv", lambda e, o=var_b4x[:, s2], i=ps_x1q[tc2][:]:
                       e.tensor_scalar(o, i, 1.0 / D, None, ALU.mult))
                em("dv", lambda e, o=nmsq4[:], a=mxs:
                   e.scalar_tensor_tensor(o, a, -1.0, a, ALU.mult, ALU.mult))
                em("dv", lambda e, o=var_b4x[:], a=nmsq4[:]:
                   e.tensor_tensor(o, o, a, ALU.add), ctr=cDV)
                wt("ac", (sDV, cDV.v))
                em("ac", lambda e, o=var_b4x[:]:
                   e.activation(o, o, AF.Ln, bias=eps_t[:]))
                em("ac", lambda e, o=r5s, i=var_b4x[:]:
                   e.activation(o, i, AF.Exp, scale=-0.5), ctr=cAC)
                wt("dv", (sAC, cAC.v))
                em("dv", lambda e, o=mrb5_all[:, sl_t], a=mxs, b=r5s:
                   e.tensor_tensor(o, a, b, ALU.mult), ctr=cDV)
                finx_mark[0] = cDV.v

            for m2 in range(NK):
                emit_wo_iter(0, m2, 0)
            flush_ystats()
            fin0 = stats_finish(ps_s1, ps_q1, m_b4, var_b4, nmsq4, 512,
                                cPE.v)
            half_fin[0] = (fin0, cAC.v)
            emit_chain(0, 0)
            emit_chain(0, 1)
            for m2 in range(NK):
                emit_wo_iter(1, m2, fin0)
                if m2 + 2 < NK:
                    emit_chain(0, m2 + 2)
            flush_ystats()
            flush_x1q()
            fin1 = stats_finish(ps_s1, ps_q1, m_b4, var_b4, nmsq4, 512,
                                cPE.v, extra=[(sPL, cPL.v)])
            half_fin[1] = (fin1, cAC.v)
            finx(0)
            for m2 in range(NK):
                emit_chain(1, m2)
            flush_x1q()
            finx(1)
            flush()
        barrier()

        if upto == "x1":
            dma("sy", cSY, out[:, :], x1_scr[:, :])
            _finalize(ops, wt, barrier, flush, ALLC)
            return nc

        with contextlib.ExitStack() as p5:
            x1sb = p5.enter_context(nc.sbuf_tensor("x1sb", [128, NK, 512], F32))
            hbuf = p5.enter_context(nc.sbuf_tensor("hbuf", [128, NC, 512], BF16))
            y2 = p5.enter_context(nc.sbuf_tensor("y2", [128, NK, 512], BF16))
            xn2 = p5.enter_context(nc.sbuf_tensor("xn2", [128, NK, 512], BF16))
            xapp = [p5.enter_context(
                nc.sbuf_tensor(f"xapp{i}", [128, 512], F32)) for i in range(2)]
            wgub = [p5.enter_context(
                nc.sbuf_tensor(f"wgub{i}", [128, 2, D], BF16))
                for i in range(2)]
            wdb = [p5.enter_context(nc.sbuf_tensor(f"wdb{i}", [128, 8192], BF16))
                   for i in range(2)]
            sg_t = [p5.enter_context(nc.sbuf_tensor(f"sg_t{i}", [128, 512], BF16))
                    for i in range(2)]
            xq5 = [p5.enter_context(nc.sbuf_tensor(f"xq5{i}", [128, 512], BF16))
                   for i in range(2)]
            m_b5 = p5.enter_context(nc.sbuf_tensor("m_b5", [128, 512], F32))
            var_b5 = p5.enter_context(nc.sbuf_tensor("var_b5", [128, 512], F32))
            ps_a = p5.enter_context(nc.psum_tensor("ps_a", [128, 512], F32))
            ps_b = p5.enter_context(nc.psum_tensor("ps_b", [128, 512], F32))
            ps_g = [p5.enter_context(
                nc.psum_tensor(f"ps_g{i}", [128, 512], F32)) for i in range(2)]
            ps_u = [p5.enter_context(
                nc.psum_tensor(f"ps_u{i}", [128, 512], F32)) for i in range(2)]
            ps_d = [p5.enter_context(
                nc.psum_tensor(f"ps_d{i}", [128, 512], F32)) for i in range(2)]

            x2_all = {}
            gu_mark = [0, 0]
            h_mark = [0, 0]
            d_mark = [0, 0]
            y2_evac = [0, 0]
            st5_mark = [0, 0]
            apm5 = {}
            xld_g = []

            def emit_x1sb_load(dst_tt, gw):
                res = []
                cols2 = slice(dst_tt * 512, (dst_tt + 1) * 512)
                for g in range(4):
                    gsrc = x1_scr[g * 512:(g + 1) * 512, cols2].rearrange(
                        "(k p) t -> p k t", p=128)
                    geng, gctr = (("sy", cSY), ("ac", cAC16))[g % 2]
                    dma(geng, gctr, x1sb[:, 4 * g:4 * (g + 1), :], gsrc,
                        waits=gw)
                    res.append((gctr.sem, gctr.v))
                return res

            xld_g = emit_x1sb_load(0, [])
            ap5 = [0, 0]
            for k in range(NK):
                par = k % 2
                aw = [xld_g[k // 4]]
                if ap5[par]:
                    aw.append((sDV, ap5[par]))
                wt("dv", *aw)
                em("dv", lambda e, o=xapp[par][:], a=x1sb[:, k, :],
                   r=r5_all[:, 0:512]: e.tensor_tensor(o, a, r, ALU.mult))
                em("dv", lambda e, o=xn2[:, k, :], a=xapp[par][:],
                   m=mrb5_all[:, 0:512]: e.tensor_tensor(o, a, m, ALU.subtract),
                   ctr=cDV)
                ap5[par] = cDV.v
                apm5[k] = (sDV, cDV.v)
            xapp_free = (sDV, cDV.v)

            wgu_pre = {}
            gu_early = {}

            def emit_gu_group(m, wld, apm_map):
                par = m % 2
                pw = [wld]
                if m >= 2:
                    pw.append(apm_map[NK - 1])
                if h_mark[par]:
                    pw.append((sDV, h_mark[par]))
                wt("pe", *pw)
                for k in range(NK):
                    kw = [apm_map[k]] if m < 2 else ()
                    mm(ps_g[par][:], wgub[par][:, 0, k * 128:(k + 1) * 128],
                       xn2[:, k, :], start=(k == 0), stop=(k == NK - 1),
                       waits=kw, ctr=(cPE if k == NK - 1 else None))
                gmm = cPE.v
                for k in range(NK):
                    mm(ps_u[par][:], wgub[par][:, 1, k * 128:(k + 1) * 128],
                       xn2[:, k, :], start=(k == 0), stop=(k == NK - 1),
                       ctr=(cPE if k == NK - 1 else None))
                umm = cPE.v
                gu_mark[par] = cPE.v
                return gmm, umm

            def emit_wd_load(m2, dlw):
                par = m2 % 2
                dma("gp", cGP, wdb[par][:], wd_m[m2], waits=dlw)
                return (sGP, cGP.v)

            def emit_wgu_load_gp(m):
                par = m % 2
                lw = []
                if gu_mark[par]:
                    lw.append((sPE, gu_mark[par]))
                dma("gp", cGP, wgub[par][:],
                    wgu_m[m].rearrange("a p f -> p a f"), waits=lw)
                return (sGP, cGP.v)

            def emit_gu_evac(m, gmm, umm, extra):
                par = m % 2
                em("ac", lambda e, o=sg_t[par][:], i=ps_g[par][:],
                   s=sg_sb[:, m:m + 1]:
                   e.activation(o, i, AF.Silu, scale=s),
                   waits=[(sPE, gmm), (sSY, consts_ld)] + list(extra),
                   ctr=cAC)
                em("dv", lambda e, o=hbuf[:, m, :], i=ps_u[par][:],
                   s=su_sb[:, m:m + 1], g=sg_t[par][:]:
                   e.scalar_tensor_tensor(o, i, s, g, ALU.mult, ALU.mult),
                   waits=[(sPE, umm), (sAC, cAC.v)] + list(extra), ctr=cDV)
                h_mark[par] = cDV.v

            for tt in range(4):
                cols = slice(tt * 512, (tt + 1) * 512)

                wd_pre = {}
                for m in range(NC):
                    par = m % 2
                    if m in gu_early:
                        gu_early.pop(m)
                        continue
                    if m in wgu_pre:
                        wld = wgu_pre.pop(m)
                    else:
                        wld = emit_wgu_load_gp(m)
                    gmm, umm = emit_gu_group(m, wld, apm5)
                    if m in (48, 56):
                        m2p = (m - 48) // 8
                        dlw = []
                        if d_mark[m2p % 2]:
                            dlw.append((sPE, d_mark[m2p % 2]))
                        wd_pre[m2p] = emit_wd_load(m2p, dlw)
                    emit_gu_evac(m, gmm, umm, ())
                h_all = cDV.v
                gu_last = cPE.v

                pend_d = None

                def flush_dstats(p):
                    (pm2, dvm, acm) = p
                    ppar = pm2 % 2
                    pw = [(sDV, dvm), (sAC, acm)]
                    if pm2 == 0 and tt > 0:
                        pw.append((sDV, prev_stats_fin5))
                    wt("pe", *pw)
                    mm(ps_a[:], ones_st[:], y2[:, pm2, :],
                       start=(pm2 == 0), stop=(pm2 == NK - 1))
                    mm(ps_b[:], ones_st[:], xq5[ppar][:],
                       start=(pm2 == 0), stop=(pm2 == NK - 1), ctr=cPE)
                    st5_mark[ppar] = cPE.v

                nap = apm5
                if tt < 3:
                    nap = {}
                    ncols = slice((tt + 1) * 512, (tt + 2) * 512)
                for m2 in range(NK):
                    par = m2 % 2
                    if tt < 3 and m2 < 8:
                        for k in (2 * m2, 2 * m2 + 1):
                            kp = k % 2
                            klw = [nap[k - 2]] if k >= 2 else [xapp_free]
                            dma(("sy", "ac")[kp],
                                (cSY, cAC16)[kp], xapp[kp][:],
                                x1_scr[k * 128:(k + 1) * 128, ncols],
                                waits=klw)
                            kld = ((sSY, cSY.v), (sAC16, cAC16.v))[kp]
                            wt("dv", kld, (sPE, gu_last))
                            em("dv", lambda e, o=xapp[kp][:],
                               r=r5_all[:, ncols]:
                               e.tensor_tensor(o, o, r, ALU.mult))
                            em("dv", lambda e, o=xn2[:, k, :], a=xapp[kp][:],
                               m=mrb5_all[:, ncols]:
                               e.tensor_tensor(o, a, m, ALU.subtract),
                               ctr=cDV)
                            nap[k] = (sDV, cDV.v)
                    if tt < 3 and m2 in (6, 7):
                        mpf = m2 - 6
                        wgu_pre[mpf] = emit_wgu_load_gp(mpf)
                    wld = wd_pre.pop(m2)
                    pw = [wld, (sDV, h_all)]
                    if y2_evac[par]:
                        pw.append((sDV, y2_evac[par]))
                    wt("pe", *pw)
                    for c in range(NC):
                        mm(ps_d[par][:], wdb[par][:, c * 128:(c + 1) * 128],
                           hbuf[:, c, :], start=(c == 0), stop=(c == NC - 1),
                           ctr=(cPE if c == NC - 1 else None))
                    d_mark[par] = cPE.v
                    if m2 + 2 < NK:
                        wd_pre[m2 + 2] = emit_wd_load(
                            m2 + 2, [(sPE, d_mark[par])])
                    if tt < 3 and m2 in (9, 10):
                        mpf = m2 - 9
                        gu_early[mpf] = emit_gu_group(
                            mpf, wgu_pre.pop(mpf), nap)
                    if tt < 3 and m2 in (11, 12):
                        wgu_pre[m2 - 9] = emit_wgu_load_gp(m2 - 9)
                    em("dv", lambda e, o=y2[:, m2, :], i=ps_d[par][:],
                       s=sd_sb[:, m2:m2 + 1]:
                       e.tensor_scalar(o, i, s, None, ALU.mult),
                       waits=[(sPE, d_mark[par]), (sSY, consts_ld)], ctr=cDV)
                    y2_evac[par] = cDV.v
                    em("ac", lambda e, o=xq5[par][:], i=y2[:, m2, :]:
                       e.activation(o, i, AF.Square),
                       waits=[(sDV, y2_evac[par])] + ([(sPE, st5_mark[par])]
                                                      if st5_mark[par] else []),
                       ctr=cAC)
                    if pend_d is not None:
                        flush_dstats(pend_d)
                    pend_d = (m2, y2_evac[par], cAC.v)
                flush_dstats(pend_d)
                pend_d = None
                d_last = cPE.v
                for mpf in sorted(gu_early):
                    gmm, umm = gu_early[mpf]
                    emit_gu_evac(mpf, gmm, umm,
                                 [(sPE, d_last), (sDV, h_mark[mpf % 2])])
                prev_stats_fin5 = stats_finish([ps_a], [ps_b], m_b5, var_b5,
                                               xapp[0], 512, cPE.v)
                r5c_ac = cAC.v
                for m2 in range(NK):
                    par = m2 % 2
                    rows = slice(m2 * 128, (m2 + 1) * 128)
                    aw = [(sAC, r5c_ac), (sDV, prev_stats_fin5),
                          xld_g[m2 // 4]]
                    wt("dv", *aw)
                    em("dv", lambda e, o=xapp[par][:], a=y2[:, m2, :],
                       r=var_b5[:]: e.tensor_tensor(o, a, r, ALU.mult))
                    em("dv", lambda e, o=x1sb[:, m2, :], m=m_b5[:]:
                       e.tensor_tensor(o, o, m, ALU.subtract))
                    em("dv", lambda e, o=x1sb[:, m2, :], a=xapp[par][:]:
                       e.tensor_tensor(o, o, a, ALU.add), ctr=cDV)
                    oeng, octr = (("sy", cSY), ("ac", cAC16))[m2 % 2]
                    dma(oeng, octr, out[rows, cols], x1sb[:, m2, :],
                        waits=[(sDV, cDV.v)])
                    x2_all[m2] = (octr.sem, octr.v)
                apm5 = nap
                xapp_free = (sDV, cDV.v)
                if tt < 3:
                    st_done = [(sSY, cSY.v), (sAC16, cAC16.v)]
                    xld_g = emit_x1sb_load(tt + 1, st_done)
            flush()

        _finalize(ops, wt, barrier, flush, ALLC)
    return nc


def _finalize(ops, wt, barrier, flush, ALLC):
    barrier()
    flush()


def _prep_inputs(L0_attn_x, L0_attn_Wq, L0_attn_Wk, L0_attn_Wv, L0_attn_Wo,
                 L0_ffn_w_gate, L0_ffn_w_up, L0_ffn_w_down,
                 L0_ffn_s_gate, L0_ffn_s_up, L0_ffn_s_down):
    c = np.ascontiguousarray
    wq = c(np.asarray(L0_attn_Wq, np.float32)
           .reshape(16, 128, 16, 128).transpose(2, 1, 0, 3)
           .reshape(16, 128, 2048))
    wk = c(np.asarray(L0_attn_Wk, np.float32)
           .reshape(16, 128, 16, 128).transpose(2, 1, 0, 3)
           .reshape(16, 128, 2048))
    wo = c(np.asarray(L0_attn_Wo, np.float32)
           .reshape(16, 128, 16, 128).transpose(2, 1, 0, 3)
           .reshape(16, 128, 2048))
    wv = c(np.asarray(L0_attn_Wv, np.float32))
    wg = (np.asarray(L0_ffn_w_gate, np.int8)
          .reshape(16, 128, 64, 128).transpose(2, 1, 0, 3)
          .reshape(64, 128, 2048))
    wu = (np.asarray(L0_ffn_w_up, np.int8)
          .reshape(16, 128, 64, 128).transpose(2, 1, 0, 3)
          .reshape(64, 128, 2048))
    wgu = c(np.stack([wg, wu], axis=1))
    wd = c(np.asarray(L0_ffn_w_down, np.int8)
           .reshape(64, 128, 16, 128).transpose(2, 1, 0, 3)
           .reshape(16, 128, 8192))
    sg = c(np.asarray(L0_ffn_s_gate, np.float32).reshape(64, 128).T)
    su = c(np.asarray(L0_ffn_s_up, np.float32).reshape(64, 128).T)
    sd = c(np.asarray(L0_ffn_s_down, np.float32).reshape(16, 128).T)
    tri = np.triu(np.ones((128, 128), np.float32))
    x = np.asarray(L0_attn_x, np.float32)
    shared = dict(wq_h=wq, wk_h=wk, wv=wv, wo_m=wo, wgu_m=wgu,
                  wd_m=wd, sg2d=sg, su2d=su, sd2d=sd, tri_in=tri)
    in_maps = []
    for b in range(8):
        m = dict(shared)
        m["xT"] = c(x[b].T)
        in_maps.append(m)
    return in_maps


_NC_CACHE = {}


def kernel(**inputs):
    in_maps = _prep_inputs(**inputs)
    if "full" not in _NC_CACHE:
        _NC_CACHE["full"] = build_nc("full")
    nc = _NC_CACHE["full"]
    res = run_bass_kernel_spmd(nc, in_maps, core_ids=list(range(8)))
    outs = [np.ascontiguousarray(np.asarray(r["out"], np.float32).T)
            for r in res.results]
    return np.stack(outs, axis=0)


# version 22
# speedup vs baseline: 1.2042x; 1.0094x over previous
import contextlib

import numpy as np
import concourse.bass as bass
import concourse.mybir as mybir
from concourse.bass_utils import run_bass_kernel_spmd

F32 = mybir.dt.float32
BF16 = mybir.dt.bfloat16
I8 = mybir.dt.int8
AF = mybir.ActivationFunctionType
ALU = mybir.AluOpType

D = 2048
T = 2048
H = 16
NK = D // 128
NC = 8192 // 128
SM_SCALE = float(1.0 / np.sqrt(2048.0))
EPS = 1e-5


class Ctr:
    def __init__(self, sem, step):
        self.sem = sem
        self.v = 0
        self.step = step


def build_nc(upto="full"):
    nc = bass.Bass("TRN2", target_bir_lowering=False)

    xT = nc.dram_tensor("xT", [D, T], F32, kind="ExternalInput").ap()
    wq_h = nc.dram_tensor("wq_h", [H, 128, D], F32, kind="ExternalInput").ap()
    wk_h = nc.dram_tensor("wk_h", [H, 128, D], F32, kind="ExternalInput").ap()
    wv = nc.dram_tensor("wv", [D, D], F32, kind="ExternalInput").ap()
    wo_m = nc.dram_tensor("wo_m", [NK, 128, D], F32, kind="ExternalInput").ap()
    wgu_m = nc.dram_tensor("wgu_m", [NC, 2, 128, D], I8,
                           kind="ExternalInput").ap()
    wd_m = nc.dram_tensor("wd_m", [NK, 128, 8192], I8, kind="ExternalInput").ap()
    sg2d = nc.dram_tensor("sg2d", [128, NC], F32, kind="ExternalInput").ap()
    su2d = nc.dram_tensor("su2d", [128, NC], F32, kind="ExternalInput").ap()
    sd2d = nc.dram_tensor("sd2d", [128, NK], F32, kind="ExternalInput").ap()
    tri_in = nc.dram_tensor("tri_in", [128, 128], F32, kind="ExternalInput").ap()

    out = nc.dram_tensor("out", [D, T], F32, kind="ExternalOutput").ap()

    x1_scr = nc.dram_tensor("x1_scr", [D, T], F32, kind="Internal").ap()
    wo_scr = nc.dram_tensor("wo_scr", [NK, 128, D], BF16, kind="Internal").ap()
    attn_scr = nc.dram_tensor("attn_scr", [H, 128, T], BF16, kind="Internal").ap()

    with contextlib.ExitStack() as es:
        tri = es.enter_context(nc.sbuf_tensor("tri", [128, 128], F32))
        ones_st = es.enter_context(nc.sbuf_tensor("ones_st", [128, 128], BF16))
        ones1 = es.enter_context(nc.sbuf_tensor("ones1", [128, 1], BF16))
        onesk1 = es.enter_context(nc.sbuf_tensor("onesk1", [1, 128], BF16))
        eps_t = es.enter_context(nc.sbuf_tensor("eps_t", [128, 1], F32))
        sg_sb = es.enter_context(nc.sbuf_tensor("sg_sb", [128, NC], F32))
        su_sb = es.enter_context(nc.sbuf_tensor("su_sb", [128, NC], F32))
        sd_sb = es.enter_context(nc.sbuf_tensor("sd_sb", [128, NK], F32))
        mx_sb = es.enter_context(nc.sbuf_tensor("mx_sb", [128, T], BF16))

        sGP = es.enter_context(nc.semaphore("sGP"))
        sSY = es.enter_context(nc.semaphore("sSY"))
        sPE = es.enter_context(nc.semaphore("sPE"))
        sDV = es.enter_context(nc.semaphore("sDV"))
        sAC = es.enter_context(nc.semaphore("sAC"))
        sAC16 = es.enter_context(nc.semaphore("sAC16"))
        sPL = es.enter_context(nc.semaphore("sPL"))

        cGP, cSY = Ctr(sGP, 16), Ctr(sSY, 16)
        cPE, cDV, cAC = Ctr(sPE, 1), Ctr(sDV, 1), Ctr(sAC, 1)
        cAC16, cPL = Ctr(sAC16, 16), Ctr(sPL, 1)
        ALLC = [cGP, cSY, cPE, cDV, cAC, cAC16, cPL]

        block = es.enter_context(nc.Block())

        ops = {"gp": [], "sy": [], "pe": [], "dv": [], "ac": []}

        def flush():
            if ops["gp"]:
                block.gpsimd(lambda e, fns=tuple(ops["gp"]): [f(e) for f in fns])
            if ops["sy"]:
                block.sync(lambda e, fns=tuple(ops["sy"]): [f(e) for f in fns])
            if ops["pe"]:
                block.tensor(lambda e, fns=tuple(ops["pe"]): [f(e) for f in fns])
            if ops["dv"]:
                block.vector(lambda e, fns=tuple(ops["dv"]): [f(e) for f in fns])
            if ops["ac"]:
                block.scalar(lambda e, fns=tuple(ops["ac"]): [f(e) for f in fns])
            for k in ops:
                ops[k] = []

        def _run(eng, waits, make, inc):
            for (sem, v) in waits:
                if v > 0:
                    eng.wait_ge(sem, v)
            ins = make(eng)
            if inc is not None:
                ins.then_inc(inc[0], inc[1])

        def em(engname, make, waits=(), ctr=None):
            inc = None
            if ctr is not None:
                ctr.v += ctr.step
                inc = (ctr.sem, ctr.step)
            ops[engname].append(
                lambda eng, w=tuple(waits), m=make, i=inc: _run(eng, w, m, i))

        def wt(engname, *waits):
            ops[engname].append(
                lambda eng, w=tuple(waits): [eng.wait_ge(s, v)
                                             for (s, v) in w if v > 0])

        def barrier():
            snap = [(c.sem, c.v) for c in ALLC]
            for e in ("gp", "sy", "pe", "dv", "ac"):
                wt(e, *snap)

        def dma(engname, ctr, dst, src, waits=()):
            em(engname, lambda e, d=dst, s=src: e.dma_start(out=d, in_=s),
               waits=waits, ctr=ctr)

        def warm(psum_t, rhs_ap, *waits):
            em("pe", lambda e, p=psum_t[0:64, 0:64], r=rhs_ap:
               e.matmul(p, ones_st[:, 0:64], r, start=True, stop=True,
                        skip_group_check=True), waits=list(waits))

        def mm(psum, lhsT, rhs, start, stop, waits=(), ctr=None):
            em("pe", lambda e, p=psum, l=lhsT, r=rhs, s0=start, s1=stop:
               e.matmul(p, l, r, start=s0, stop=s1, skip_group_check=True),
               waits=waits, ctr=ctr)

        def stats_finish(ps_sum, ps_sq, m_b, var_b, nmsq, w_each, pe_done,
                         save_m=None, extra=()):
            wt("dv", (sPE, pe_done), *extra)
            for tc in range(len(ps_sum)):
                sl = slice(tc * w_each, (tc + 1) * w_each)
                em("dv", lambda e, o=m_b[:, sl], i=ps_sum[tc][:]:
                   e.tensor_scalar(o, i, 1.0 / D, None, ALU.mult))
                em("dv", lambda e, o=var_b[:, sl], i=ps_sq[tc][:]:
                   e.tensor_scalar(o, i, 1.0 / D, None, ALU.mult))
            if save_m is not None:
                em("dv", lambda e, o=save_m, i=m_b[:]: e.tensor_copy(o, i))
            em("dv", lambda e, o=nmsq[:], a=m_b[:]:
               e.scalar_tensor_tensor(o, a, -1.0, a, ALU.mult, ALU.mult))
            em("dv", lambda e, o=var_b[:], a=nmsq[:]:
               e.tensor_tensor(o, o, a, ALU.add), ctr=cDV)
            wt("ac", (sDV, cDV.v))
            em("ac", lambda e, o=var_b[:]:
               e.activation(o, o, AF.Ln, bias=eps_t[:]))
            em("ac", lambda e, o=var_b[:]:
               e.activation(o, o, AF.Exp, scale=-0.5), ctr=cAC)
            wt("dv", (sAC, cAC.v))
            em("dv", lambda e, o=m_b[:], b=var_b[:]:
               e.tensor_tensor(o, o, b, ALU.mult), ctr=cDV)
            return cDV.v

        dma("sy", cSY, tri[:], tri_in)
        dma("sy", cSY, sg_sb[:], sg2d)
        dma("sy", cSY, su_sb[:], su2d)
        dma("sy", cSY, sd_sb[:], sd2d)
        consts_ld = cSY.v
        em("dv", lambda e: e.memset(ones_st[:], 1.0))
        em("dv", lambda e: e.memset(ones1[:], 1.0))
        em("dv", lambda e: e.memset(onesk1[:], 1.0))
        em("dv", lambda e: e.memset(eps_t[:], EPS), ctr=cDV)
        consts_dv = cDV.v

        p123 = es.enter_context(contextlib.ExitStack())
        xn1 = p123.enter_context(nc.sbuf_tensor("xn1", [128, NK, T], BF16))
        v_tm = p123.enter_context(nc.sbuf_tensor("v_tm", [128, NK, D], BF16))
        wqb = [p123.enter_context(nc.sbuf_tensor(f"wqb{i}", [128, D], BF16))
               for i in range(2)]
        wkb = [p123.enter_context(nc.sbuf_tensor(f"wkb{i}", [128, D], BF16))
               for i in range(2)]
        wostg = [p123.enter_context(
            nc.sbuf_tensor(f"wostg{i}", [128, D], BF16)) for i in range(2)]
        wv_ld = {}
        qk_ld = {}

        with contextlib.ExitStack() as p1:
            xsq = [p1.enter_context(nc.sbuf_tensor(f"xsq{i}", [128, T], BF16))
                   for i in range(2)]
            xf = [p1.enter_context(nc.sbuf_tensor(f"xf{i}", [128, T], F32))
                  for i in range(2)]
            m_b = p1.enter_context(nc.sbuf_tensor("m_b", [128, T], F32))
            var_b = p1.enter_context(nc.sbuf_tensor("var_b", [128, T], F32))
            ps_sum = [p1.enter_context(
                nc.psum_tensor(f"ps_sum{i}", [128, 512], F32)) for i in range(4)]
            ps_sq = [p1.enter_context(
                nc.psum_tensor(f"ps_sq{i}", [128, 512], F32)) for i in range(4)]

            cp_guard = {}
            ld_gp = {}
            for k in range(8):
                dma("gp", cGP, xn1[:, k, :], xT[k * 128:(k + 1) * 128, :])
                ld_gp[k] = cGP.v
            ld_sy = {}

            ready = {}
            for k in range(8):
                ready[k] = (sGP, ld_gp[k])
            for k in range(8, NK):
                lw = []
                if k >= 10:
                    lw.append((sDV, cp_guard[k - 2]))
                dma("ac", cAC16, xf[k % 2][:], xT[k * 128:(k + 1) * 128, :],
                    waits=lw)
                ld_sy[k] = cAC16.v
                em("dv", lambda e, o=xn1[:, k, :], i=xf[k % 2][:]:
                   e.tensor_copy(o, i), waits=[(sAC16, ld_sy[k])], ctr=cDV)
                cp_guard[k] = cDV.v
                ready[k] = (sDV, cDV.v)

            pe_k = []
            for k in range(NK):
                wts = [ready[k], (sDV, consts_dv)]
                if k >= 2:
                    wts.append((sPE, pe_k[k - 2]))
                em("ac", lambda e, o=xsq[k % 2][:], i=xn1[:, k, :]:
                   e.activation(o, i, AF.Square), waits=wts, ctr=cAC)
                wt("pe", ready[k], (sAC, cAC.v))
                for tc in range(4):
                    sl = slice(tc * 512, (tc + 1) * 512)
                    mm(ps_sum[tc][:], ones_st[:], xn1[:, k, sl],
                       start=(k == 0), stop=(k == NK - 1))
                    mm(ps_sq[tc][:], ones_st[:], xsq[k % 2][:, sl],
                       start=(k == 0), stop=(k == NK - 1),
                       ctr=(cPE if tc == 3 else None))
                pe_k.append(cPE.v)

            ln1_ready = stats_finish(ps_sum, ps_sq, m_b, var_b, xsq[0], 512,
                                     cPE.v, save_m=mx_sb[:])
            r_ready_ac = cAC.v

            ap_mark = []
            last_cp = cDV.v
            POOLK = {3, 6, 9, 12, 15}
            for k in range(NK):
                wts = []
                if k < 2:
                    wts.append((sDV, last_cp))
                if k >= 2:
                    wts.append(ap_mark[k - 2])
                dma("sy", cSY, xf[k % 2][:], xT[k * 128:(k + 1) * 128, :],
                    waits=wts)
                eng = "gp" if k in POOLK else "dv"
                ctr = cPL if k in POOLK else cDV
                wt(eng, (sSY, cSY.v), (sAC, r_ready_ac), (sDV, ln1_ready))
                em(eng, lambda e, b=xf[k % 2][:], r=var_b[:]:
                   e.tensor_tensor(b, b, r, ALU.mult))
                em(eng, lambda e, o=xn1[:, k, :], b=xf[k % 2][:], m=m_b[:]:
                   e.tensor_tensor(o, b, m, ALU.subtract), ctr=ctr)
                ap_mark.append((ctr.sem, ctr.v))
                warm(ps_sum[0], xsq[0][:, 0:64], ap_mark[-1])
            flush()
        barrier()

        if upto == "ln1":
            for k in range(NK):
                dma("gp", cGP, out[k * 128:(k + 1) * 128, :], xn1[:, k, :])
            _finalize(ops, wt, barrier, flush, ALLC)
            return nc

        with contextlib.ExitStack() as p2:
            wvb = [p2.enter_context(
                nc.sbuf_tensor(f"wvb{i}", [128, NK, 512], BF16))
                for i in range(2)]
            ps_v = [p2.enter_context(
                nc.psum_tensor(f"ps_v{i}", [128, 512], F32)) for i in range(2)]
            evac = []
            vq_mark = []
            for q in range(4):
                cols = slice(q * 512, (q + 1) * 512)
                src2 = wv[:, cols].rearrange("(k p) n -> p k n", p=128)
                lw = [] if q < 2 else [(sPE, vq_mark[q - 2])]
                dma("gp", cGP, wvb[q % 2][:], src2, waits=lw)
                wld = cGP.v
                for st in range(NK):
                    j = q * NK + st
                    par = j % 2
                    wts = [(sGP, wld)]
                    if j >= 2:
                        wts.append((sAC, evac[j - 2]))
                    wt("pe", *wts)
                    for k in range(NK):
                        mm(ps_v[par][:],
                           xn1[:, k, st * 128:(st + 1) * 128],
                           wvb[q % 2][:, k, :],
                           start=(k == 0), stop=(k == NK - 1),
                           ctr=(cPE if k == NK - 1 else None))
                    dst = v_tm[:, st, q * 512:(q + 1) * 512]
                    em("ac", lambda e, o=dst, i=ps_v[par][:]:
                       e.activation(o, i, AF.Copy),
                       waits=[(sPE, cPE.v)], ctr=cAC)
                    evac.append(cAC.v)
                vq_mark.append(cPE.v)
            for h2 in range(2):
                dma("gp", cGP, wqb[h2][:], wq_h[h2])
                dma("gp", cGP, wkb[h2][:], wk_h[h2])
                qk_ld[h2] = cGP.v
            flush()
        barrier()

        with contextlib.ExitStack() as p3:
            qh = p3.enter_context(nc.sbuf_tensor("qh", [128, T], BF16))
            kh = p3.enter_context(nc.sbuf_tensor("kh", [128, T], BF16))
            at = p3.enter_context(nc.sbuf_tensor("at", [128, 16, 512], BF16))
            attn_h = [p3.enter_context(
                nc.sbuf_tensor(f"attn_h{i}", [128, T], BF16)) for i in range(2)]
            zt = p3.enter_context(nc.sbuf_tensor("zt", [1, 512], F32))
            suf_sb = p3.enter_context(nc.sbuf_tensor("suf_sb", [128, 4], F32))
            zinv_bf = p3.enter_context(nc.sbuf_tensor("zinv_bf", [1, 512], BF16))
            zb_sb = p3.enter_context(nc.sbuf_tensor("zb_sb", [128, 512], BF16))
            ps_qk = [p3.enter_context(
                nc.psum_tensor(f"ps_qk{i}", [128, 512], F32)) for i in range(2)]
            ps_s = [p3.enter_context(
                nc.psum_tensor(f"ps_s{i}", [128, 512], F32)) for i in range(2)]
            ps_o = [p3.enter_context(
                nc.psum_tensor(f"ps_o{i}", [128, 512], F32)) for i in range(2)]
            ps_z = p3.enter_context(nc.psum_tensor("ps_z", [1, 512], F32))
            ps_zb = p3.enter_context(nc.psum_tensor("ps_zb", [128, 512], F32))

            qk_mm_done = {}
            scores_last = {-1: 0}
            exp_all = [0, 0]
            qk_evac = [0, 0]
            at_free = 0
            o_evac = [0, 0]
            z_ts_prev = 0
            ln_done_prev = 0
            zbmm_prev = 0
            attn_store = [0, 0]
            suf_read = 0
            suf_dv_prev = 0

            def emit_tail(pend):
                nonlocal zbmm_prev
                (o_par_p, q_sl_p, h_p, zinv_mark, qc_p, sufdv) = pend
                wt("pe", (sAC, zinv_mark), (sDV, o_evac[1 - o_par_p]))
                mm(ps_zb[:], onesk1[:], zinv_bf[:], start=True, stop=True,
                   ctr=cPE)
                zbmm_prev = cPE.v
                wts = [(sPE, zbmm_prev), (sSY, attn_store[h_p % 2])]
                if sufdv:
                    wts.append((sDV, sufdv))
                wt("dv", *wts)
                em("dv", lambda e, o=zb_sb[:], i=ps_zb[:]: e.tensor_copy(o, i))
                sufop = suf_sb[:, qc_p:qc_p + 1] if qc_p < 3 else 0.0
                em("dv", lambda e, o=attn_h[h_p % 2][:, q_sl_p],
                   i=ps_o[o_par_p][:], z2=zb_sb[:], s3=sufop:
                   e.scalar_tensor_tensor(o, i, s3, z2, ALU.add, ALU.mult),
                   ctr=cDV)
                o_evac[o_par_p] = cDV.v

            wo_stg_mark = [0, 0]
            for h in range(H):
                if h in qk_ld:
                    wld = qk_ld[h]
                else:
                    hw = [(sPE, qk_mm_done[h - 2])]
                    dma("gp", cGP, wqb[h % 2][:], wq_h[h], waits=hw)
                    dma("gp", cGP, wkb[h % 2][:], wk_h[h])
                    wld = cGP.v
                sw = [] if wo_stg_mark[h % 2] == 0 else                     [(sSY, wo_stg_mark[h % 2])]
                dma("gp", cGP, wostg[h % 2][:], wo_m[h], waits=sw)
                dma("sy", cSY, wo_scr[h], wostg[h % 2][:],
                    waits=[(sGP, cGP.v)])
                wo_stg_mark[h % 2] = cSY.v

                for which, (wbuf, dst) in enumerate(
                        ((wqb, qh), (wkb, kh))):
                    for tc in range(4):
                        j = which * 4 + tc
                        par = j % 2
                        wts = [(sGP, wld)]
                        if qk_evac[par]:
                            wts.append((sAC, qk_evac[par]))
                        if which == 0 and tc == 0 and suf_dv_prev:
                            wts.append((sDV, suf_dv_prev))
                        wt("pe", *wts)
                        for k in range(NK):
                            mm(ps_qk[par][:],
                               wqb[h % 2][:, k * 128:(k + 1) * 128] if which == 0
                               else wkb[h % 2][:, k * 128:(k + 1) * 128],
                               xn1[:, k, tc * 512:(tc + 1) * 512],
                               start=(k == 0), stop=(k == NK - 1),
                               ctr=(cPE if k == NK - 1 else None))
                        ewts = [(sPE, cPE.v), (sPE, scores_last[h - 1])]
                        em("ac", lambda e,
                           o=dst[:, tc * 512:(tc + 1) * 512],
                           i=ps_qk[par][:]: e.activation(o, i, AF.Copy),
                           waits=ewts, ctr=cAC)
                        qk_evac[par] = cAC.v
                qk_mm_done[h] = cPE.v
                qk_ready = cAC.v

                hs0 = slice(h * 128, (h + 1) * 128)
                wt("pe", (sAC, qk_evac[0]), (sAC, qk_evac[1]),
                   (sDV, suf_read))
                for c in range(4, 16):
                    mm(ps_qk[0][:, c - 4:c - 3], v_tm[:, c, hs0], ones1[:],
                       start=True, stop=True,
                       ctr=(cPE if c == 15 else None))
                wt("dv", (sPE, cPE.v))
                for j in range(3):
                    em("dv", lambda e, o=suf_sb[:, j:j + 1],
                       i=ps_qk[0][:, 4 * j:12]:
                       e.reduce_sum(o, i, axis=mybir.AxisListType.X),
                       ctr=(cDV if j == 2 else None))
                suf_dv = cDV.v
                suf_dv_prev = suf_dv

                pend = None
                for qc in range(4):
                    c_hi = 4 * (qc + 1)
                    q_sl = slice(qc * 512, (qc + 1) * 512)
                    o_par = qc % 2
                    hs = slice(h * 128, (h + 1) * 128)
                    expm = {}
                    msm = {}

                    def emit_av(cav):
                        avw = [(sAC, expm[cav])] if cav in expm else []
                        if cav in msm:
                            avw.append((sDV, msm[cav]))
                        if cav == 0:
                            avw.append((sDV, o_evac[o_par]))
                        if avw:
                            wt("pe", *avw)
                        mm(ps_o[o_par][:], v_tm[:, cav, hs], at[:, cav, :],
                           start=(cav == 0), stop=(cav == c_hi - 1),
                           ctr=(cPE if cav == c_hi - 1 else None))

                    def emit_z(cz):
                        if cz == 0:
                            wt("pe", (sDV, z_ts_prev))
                        mm(ps_z[:], ones1[:], at[:, cz, :], start=(cz == 0),
                           stop=(cz == c_hi - 1),
                           ctr=(cPE if cz == c_hi - 1 else None))

                    for c in range(c_hi):
                        par = c % 2
                        wts = [(sAC, exp_all[par])]
                        if qc == 0 and c == 0:
                            wts.append((sAC, qk_ready))
                        wt("pe", *wts)
                        mm(ps_s[par][:], kh[:, c * 128:(c + 1) * 128],
                           qh[:, q_sl], start=True, stop=True, ctr=cPE)
                        sc = cPE.v
                        if c == 1 and pend is not None:
                            emit_tail(pend)
                            pend = None
                        band = c >= 4 * qc
                        dmark = 0
                        j0 = 0
                        if band:
                            j0 = c * 128 - qc * 512
                            em("dv", lambda e,
                               p=ps_s[par][:, j0:j0 + 128], t2=tri[:]:
                               e.tensor_tensor(p, p, t2, ALU.mult),
                               waits=[(sPE, sc), (sSY, consts_ld)], ctr=cDV)
                            dmark = cDV.v
                            if j0 > 0:
                                em("dv", lambda e, o=at[:, c, 0:j0]:
                                   e.memset(o, 1.0),
                                   waits=[(sPE, at_free)], ctr=cDV)
                                msm[c] = cDV.v
                        ewts = [(sPE, sc), (sPE, at_free)]
                        if dmark:
                            ewts.append((sDV, dmark))
                        em("ac", lambda e, o=at[:, c, j0:512],
                           i=ps_s[par][:, j0:512]:
                           e.activation(o, i, AF.Exp, scale=SM_SCALE),
                           waits=ewts, ctr=cAC)
                        exp_all[par] = cAC.v
                        expm[c] = cAC.v
                        if c >= 2:
                            emit_av(c - 2)
                            emit_z(c - 2)
                    for cav in range(c_hi - 2, c_hi):
                        emit_av(cav)
                        emit_z(cav)
                    z_done = cPE.v
                    at_free = z_done
                    cnt = float(2048 - c_hi * 128)
                    wt("dv", (sPE, z_done), (sAC, ln_done_prev))
                    em("dv", lambda e, o=zt[:], i=ps_z[:], c2=cnt:
                       e.tensor_scalar(o, i, c2, None, ALU.add), ctr=cDV)
                    z_ts_prev = cDV.v
                    wt("ac", (sDV, z_ts_prev), (sPE, zbmm_prev))
                    em("ac", lambda e, o=zt[:]: e.activation(o, o, AF.Ln))
                    em("ac", lambda e, o=zinv_bf[:], i=zt[:]:
                       e.activation(o, i, AF.Exp, scale=-1.0), ctr=cAC)
                    ln_done_prev = cAC.v
                    pend = (o_par, q_sl, h, cAC.v, qc, suf_dv)
                emit_tail(pend)
                pend = None
                suf_read = o_evac[1]
                scores_last[h] = cPE.v
                dma("sy", cSY, attn_scr[h], attn_h[h % 2][:],
                    waits=[(sDV, cDV.v)])
                attn_store[h % 2] = cSY.v
            flush()
        barrier()

        if upto == "attn":
            for hh in range(H):
                dma("gp", cGP, out[hh * 128:(hh + 1) * 128, :], attn_scr[hh])
            _finalize(ops, wt, barrier, flush, ALLC)
            return nc

        p123.close()

        p45 = es.enter_context(contextlib.ExitStack())
        r5_all = p45.enter_context(nc.sbuf_tensor("r5_all", [128, T], BF16))
        mrb5_all = p45.enter_context(nc.sbuf_tensor("mrb5_all", [128, T], BF16))

        with contextlib.ExitStack() as p4:
            attn_sb = p4.enter_context(
                nc.sbuf_tensor("attn_sb", [128, NK, T], BF16))
            y_half = p4.enter_context(
                nc.sbuf_tensor("y_half", [128, NK, 1024], F32))
            wob = [p4.enter_context(nc.sbuf_tensor(f"wob{i}", [128, D], BF16))
                   for i in range(2)]
            yb = [p4.enter_context(nc.sbuf_tensor(f"yb{i}", [128, 512], BF16))
                  for i in range(2)]
            ysq = [p4.enter_context(nc.sbuf_tensor(f"ysq{i}", [128, 512], BF16))
                   for i in range(2)]
            xf2 = [p4.enter_context(nc.sbuf_tensor(f"xf2{i}", [128, 1024], F32))
                   for i in range(4)]
            x1sq = [p4.enter_context(
                nc.sbuf_tensor(f"x1sq{i}", [128, 1024], BF16)) for i in range(2)]
            x1t = [p4.enter_context(nc.sbuf_tensor(f"x1t{i}", [128, 1024], F32))
                   for i in range(4)]
            m_b4 = p4.enter_context(nc.sbuf_tensor("m_b4", [128, 1024], F32))
            var_b4 = p4.enter_context(nc.sbuf_tensor("var_b4", [128, 1024], F32))
            var_b4x = p4.enter_context(
                nc.sbuf_tensor("var_b4x", [128, 1024], F32))
            nmsq4 = p4.enter_context(nc.sbuf_tensor("nmsq4", [128, 1024], F32))
            ps_y = [p4.enter_context(
                nc.psum_tensor(f"ps_y{i}", [128, 512], F32)) for i in range(2)]
            ps_s1 = [p4.enter_context(
                nc.psum_tensor(f"ps_s1{i}", [128, 512], F32)) for i in range(2)]
            ps_q1 = [p4.enter_context(
                nc.psum_tensor(f"ps_q1{i}", [128, 512], F32)) for i in range(2)]
            ps_x1q = [p4.enter_context(
                nc.psum_tensor(f"ps_x1q{i}", [128, 512], F32))
                for i in range(2)]


            half_src = attn_scr.rearrange("h p t -> p h t")
            ald_cb = []
            qrot = [("sy", cSY), ("ac", cAC16), ("gp", cGP)]
            for cb in range(4):
                cs = slice(cb * 512, (cb + 1) * 512)
                marks = []
                for hh in range(2):
                    ceng, cctr2 = qrot[(2 * cb + hh) % 3]
                    dma(ceng, cctr2,
                        attn_sb[:, 8 * hh:8 * (hh + 1), cs],
                        half_src[:, 8 * hh:8 * (hh + 1), cs])
                    marks.append((cctr2.sem, cctr2.v))
                ald_cb.append(marks)

            y_evac = [0, 0]
            stats_mark = [0, 0]
            wo_mark = {}
            apm_all = {}
            apm = [0, 0]
            x1dma = [(None, 0), (None, 0)]
            x1q_mark = [0, 0]
            half_fin = {}
            finx_mark = [0]
            pend_st = [None]
            pend_x1q = [None]

            def flush_ystats():
                (ptc, pm2, dvm, acm, extra) = pend_st[0]
                pend_st[0] = None
                wt("pe", (sDV, dvm), (sAC, acm), *extra)
                mm(ps_s1[ptc][:], ones_st[:], yb[ptc][:],
                   start=(pm2 == 0), stop=(pm2 == NK - 1))
                mm(ps_q1[ptc][:], ones_st[:], ysq[ptc][:],
                   start=(pm2 == 0), stop=(pm2 == NK - 1), ctr=cPE)
                stats_mark[ptc] = cPE.v

            def flush_x1q():
                (pm2, acm, extra) = pend_x1q[0]
                pend_x1q[0] = None
                ppar = pm2 % 2
                wt("pe", (sAC, acm), *extra)
                for tc2 in range(2):
                    s2 = slice(tc2 * 512, (tc2 + 1) * 512)
                    mm(ps_x1q[tc2][:], ones_st[:], x1sq[ppar][:, s2],
                       start=(pm2 == 0), stop=(pm2 == NK - 1),
                       ctr=(cPE if tc2 == 1 else None))
                x1q_mark[ppar] = cPE.v

            def emit_wo_iter(half, m2, fin0dv):
                base = half * 1024
                hw = []
                gi = half * NK + m2
                if gi >= 2:
                    hw.append((sPE, wo_mark.get(gi - 2, 0)))
                weng, wctr = (("gp", cGP), ("sy", cSY), ("ac", cAC16))[m2 % 3]
                dma(weng, wctr, wob[m2 % 2][:], wo_scr[m2], waits=hw)
                wldt = (wctr.sem, wctr.v)
                for tc in range(2):
                    par = tc
                    wts = [wldt] + ald_cb[half * 2 + tc]
                    if y_evac[par]:
                        wts.append((sAC, y_evac[par]))
                    wt("pe", *wts)
                    for k in range(NK):
                        mm(ps_y[par][:],
                           wob[m2 % 2][:, k * 128:(k + 1) * 128],
                           attn_sb[:, k, base + tc * 512:
                                   base + (tc + 1) * 512],
                           start=(k == 0), stop=(k == NK - 1),
                           ctr=(cPE if k == NK - 1 else None))
                    y_sl = y_half[:, m2, tc * 512:(tc + 1) * 512]
                    ew = [(sPE, cPE.v)]
                    if half == 1:
                        ew.append(apm_all[m2])
                    em("ac", lambda e, o=y_sl, i=ps_y[par][:]:
                       e.activation(o, i, AF.Copy), waits=ew, ctr=cAC)
                    y_evac[par] = cAC.v
                    sw = [(sAC, y_evac[par])]
                    if stats_mark[par]:
                        sw.append((sPE, stats_mark[par]))
                    em("dv", lambda e, o=yb[par][:], i=y_sl:
                       e.tensor_copy(o, i), waits=sw, ctr=cDV)
                    em("ac", lambda e, o=ysq[par][:], i=y_sl:
                       e.activation(o, i, AF.Square),
                       waits=([(sPE, stats_mark[par])]
                              if stats_mark[par] else ()), ctr=cAC)
                    if pend_st[0] is not None:
                        flush_ystats()
                    extra = ([(sDV, fin0dv)]
                             if half == 1 and m2 == 0 else [])
                    pend_st[0] = (tc, m2, cDV.v, cAC.v, extra)
                wo_mark[gi] = cPE.v

            chain_seq = [(h, m) for h in range(2) for m in range(NK)]
            xf2_ld = {}
            apm4 = [0, 0, 0, 0]
            x1dma4 = [0, 0, 0, 0]

            def emit_xf2_load(ci):
                (half, m2) = chain_seq[ci]
                bp = ci % 4
                rows = slice(m2 * 128, (m2 + 1) * 128)
                cols = slice(half * 1024, half * 1024 + 1024)
                leng, lctr = (("sy", cSY), ("gp", cGP), ("ac", cAC16))[ci % 3]
                lw = [apm4[bp]] if apm4[bp] else []
                dma(leng, lctr, xf2[bp][:], xT[rows, cols], waits=lw)
                xf2_ld[ci] = (lctr.sem, lctr.v)

            def emit_chain(ci):
                (half, m2) = chain_seq[ci]
                bp = ci % 4
                par = m2 % 2
                ceng, cctr = (("gp", cPL) if m2 % 3 == 1 else ("dv", cDV))
                fin, r4a = half_fin[half]
                rows = slice(m2 * 128, (m2 + 1) * 128)
                cols = slice(half * 1024, half * 1024 + 1024)
                dwts = [xf2_ld[ci], (sAC, r4a), (sDV, fin)]
                if x1dma4[bp]:
                    dwts.append(x1dma4[bp])
                wt(ceng, *dwts)
                em(ceng, lambda e, o=y_half[:, m2, :], r=var_b4[:]:
                   e.tensor_tensor(o, o, r, ALU.mult))
                em(ceng, lambda e, o=xf2[bp][:], m=m_b4[:]:
                   e.tensor_tensor(o, o, m, ALU.subtract))
                em(ceng, lambda e, o=x1t[bp][:], a=y_half[:, m2, :],
                   b=xf2[bp][:]: e.tensor_tensor(o, a, b, ALU.add),
                   ctr=cctr)
                apm4[bp] = (cctr.sem, cctr.v)
                apm_all[m2] = apm4[bp]
                seng, sctr = (("ac", cAC16), ("sy", cSY), ("gp", cGP))[ci % 3]
                dma(seng, sctr, x1_scr[rows, cols], x1t[bp][:],
                    waits=[apm4[bp]])
                x1dma4[bp] = (sctr.sem, sctr.v)
                if ci + 4 < len(chain_seq):
                    emit_xf2_load(ci + 4)
                xsw = [apm4[bp]]
                if x1q_mark[par]:
                    xsw.append((sPE, x1q_mark[par]))
                em("ac", lambda e, o=x1sq[par][:], i=x1t[bp][:]:
                   e.activation(o, i, AF.Square), waits=xsw, ctr=cAC)
                if pend_x1q[0] is not None:
                    flush_x1q()
                extra = ([(sDV, finx_mark[0])]
                         if half == 1 and m2 == 0 else [])
                pend_x1q[0] = (m2, cAC.v, extra)

            def finx(half):
                base = half * 1024
                sl_t = slice(base, base + 1024)
                mxs = mx_sb[:, sl_t]
                r5s = r5_all[:, sl_t]
                wt("dv", (sPE, cPE.v))
                for tc2 in range(2):
                    s2 = slice(tc2 * 512, (tc2 + 1) * 512)
                    em("dv", lambda e, o=var_b4x[:, s2], i=ps_x1q[tc2][:]:
                       e.tensor_scalar(o, i, 1.0 / D, None, ALU.mult))
                em("dv", lambda e, o=nmsq4[:], a=mxs:
                   e.scalar_tensor_tensor(o, a, -1.0, a, ALU.mult, ALU.mult))
                em("dv", lambda e, o=var_b4x[:], a=nmsq4[:]:
                   e.tensor_tensor(o, o, a, ALU.add), ctr=cDV)
                wt("ac", (sDV, cDV.v))
                em("ac", lambda e, o=var_b4x[:]:
                   e.activation(o, o, AF.Ln, bias=eps_t[:]))
                em("ac", lambda e, o=r5s, i=var_b4x[:]:
                   e.activation(o, i, AF.Exp, scale=-0.5), ctr=cAC)
                wt("dv", (sAC, cAC.v))
                em("dv", lambda e, o=mrb5_all[:, sl_t], a=mxs, b=r5s:
                   e.tensor_tensor(o, a, b, ALU.mult), ctr=cDV)
                finx_mark[0] = cDV.v

            for ci in range(4):
                emit_xf2_load(ci)
            for m2 in range(NK):
                emit_wo_iter(0, m2, 0)
            flush_ystats()
            fin0 = stats_finish(ps_s1, ps_q1, m_b4, var_b4, nmsq4, 512,
                                cPE.v)
            half_fin[0] = (fin0, cAC.v)
            emit_chain(0)
            emit_chain(1)
            for m2 in range(NK):
                emit_wo_iter(1, m2, fin0)
                if m2 + 2 < NK:
                    emit_chain(m2 + 2)
            flush_ystats()
            flush_x1q()
            fin1 = stats_finish(ps_s1, ps_q1, m_b4, var_b4, nmsq4, 512,
                                cPE.v, extra=[(sPL, cPL.v)])
            half_fin[1] = (fin1, cAC.v)
            finx(0)
            for ci in range(NK, 2 * NK):
                emit_chain(ci)
            flush_x1q()
            finx(1)
            flush()
        barrier()

        if upto == "x1":
            dma("sy", cSY, out[:, :], x1_scr[:, :])
            _finalize(ops, wt, barrier, flush, ALLC)
            return nc

        with contextlib.ExitStack() as p5:
            x1sb = p5.enter_context(nc.sbuf_tensor("x1sb", [128, NK, 512], F32))
            hbuf = p5.enter_context(nc.sbuf_tensor("hbuf", [128, NC, 512], BF16))
            y2 = p5.enter_context(nc.sbuf_tensor("y2", [128, NK, 512], BF16))
            xn2 = p5.enter_context(nc.sbuf_tensor("xn2", [128, NK, 512], BF16))
            xapp = [p5.enter_context(
                nc.sbuf_tensor(f"xapp{i}", [128, 512], F32)) for i in range(2)]
            wgub = [p5.enter_context(
                nc.sbuf_tensor(f"wgub{i}", [128, 2, D], BF16))
                for i in range(2)]
            wdb = [p5.enter_context(nc.sbuf_tensor(f"wdb{i}", [128, 8192], BF16))
                   for i in range(2)]
            sg_t = [p5.enter_context(nc.sbuf_tensor(f"sg_t{i}", [128, 512], BF16))
                    for i in range(2)]
            xq5 = [p5.enter_context(nc.sbuf_tensor(f"xq5{i}", [128, 512], BF16))
                   for i in range(2)]
            m_b5 = p5.enter_context(nc.sbuf_tensor("m_b5", [128, 512], F32))
            var_b5 = p5.enter_context(nc.sbuf_tensor("var_b5", [128, 512], F32))
            ps_a = p5.enter_context(nc.psum_tensor("ps_a", [128, 512], F32))
            ps_b = p5.enter_context(nc.psum_tensor("ps_b", [128, 512], F32))
            ps_g = [p5.enter_context(
                nc.psum_tensor(f"ps_g{i}", [128, 512], F32)) for i in range(2)]
            ps_u = [p5.enter_context(
                nc.psum_tensor(f"ps_u{i}", [128, 512], F32)) for i in range(2)]
            ps_d = [p5.enter_context(
                nc.psum_tensor(f"ps_d{i}", [128, 512], F32)) for i in range(2)]

            x2_all = {}
            gu_mark = [0, 0]
            h_mark = [0, 0]
            d_mark = [0, 0]
            y2_evac = [0, 0]
            st5_mark = [0, 0]
            apm5 = {}
            xld_g = []

            def emit_x1sb_load(dst_tt, gw):
                res = []
                cols2 = slice(dst_tt * 512, (dst_tt + 1) * 512)
                for g in range(4):
                    gsrc = x1_scr[g * 512:(g + 1) * 512, cols2].rearrange(
                        "(k p) t -> p k t", p=128)
                    geng, gctr = (("sy", cSY), ("ac", cAC16))[g % 2]
                    dma(geng, gctr, x1sb[:, 4 * g:4 * (g + 1), :], gsrc,
                        waits=gw)
                    res.append((gctr.sem, gctr.v))
                return res

            xld_g = emit_x1sb_load(0, [])
            ap5 = [0, 0]
            for k in range(NK):
                par = k % 2
                aw = [xld_g[k // 4]]
                if ap5[par]:
                    aw.append((sDV, ap5[par]))
                wt("dv", *aw)
                em("dv", lambda e, o=xapp[par][:], a=x1sb[:, k, :],
                   r=r5_all[:, 0:512]: e.tensor_tensor(o, a, r, ALU.mult))
                em("dv", lambda e, o=xn2[:, k, :], a=xapp[par][:],
                   m=mrb5_all[:, 0:512]: e.tensor_tensor(o, a, m, ALU.subtract),
                   ctr=cDV)
                ap5[par] = cDV.v
                apm5[k] = (sDV, cDV.v)
            xapp_free = (sDV, cDV.v)

            wgu_pre = {}
            gu_early = {}

            def emit_gu_group(m, wld, apm_map):
                par = m % 2
                pw = [wld]
                if m >= 2:
                    pw.append(apm_map[NK - 1])
                if h_mark[par]:
                    pw.append((sDV, h_mark[par]))
                wt("pe", *pw)
                for k in range(NK):
                    kw = [apm_map[k]] if m < 2 else ()
                    mm(ps_g[par][:], wgub[par][:, 0, k * 128:(k + 1) * 128],
                       xn2[:, k, :], start=(k == 0), stop=(k == NK - 1),
                       waits=kw, ctr=(cPE if k == NK - 1 else None))
                gmm = cPE.v
                for k in range(NK):
                    mm(ps_u[par][:], wgub[par][:, 1, k * 128:(k + 1) * 128],
                       xn2[:, k, :], start=(k == 0), stop=(k == NK - 1),
                       ctr=(cPE if k == NK - 1 else None))
                umm = cPE.v
                gu_mark[par] = cPE.v
                return gmm, umm

            def emit_wd_load(m2, dlw):
                par = m2 % 2
                dma("gp", cGP, wdb[par][:], wd_m[m2], waits=dlw)
                return (sGP, cGP.v)

            def emit_wgu_load_gp(m):
                par = m % 2
                lw = []
                if gu_mark[par]:
                    lw.append((sPE, gu_mark[par]))
                dma("gp", cGP, wgub[par][:],
                    wgu_m[m].rearrange("a p f -> p a f"), waits=lw)
                return (sGP, cGP.v)

            def emit_gu_evac(m, gmm, umm, extra):
                par = m % 2
                em("ac", lambda e, o=sg_t[par][:], i=ps_g[par][:],
                   s=sg_sb[:, m:m + 1]:
                   e.activation(o, i, AF.Silu, scale=s),
                   waits=[(sPE, gmm), (sSY, consts_ld)] + list(extra),
                   ctr=cAC)
                em("dv", lambda e, o=hbuf[:, m, :], i=ps_u[par][:],
                   s=su_sb[:, m:m + 1], g=sg_t[par][:]:
                   e.scalar_tensor_tensor(o, i, s, g, ALU.mult, ALU.mult),
                   waits=[(sPE, umm), (sAC, cAC.v)] + list(extra), ctr=cDV)
                h_mark[par] = cDV.v

            for tt in range(4):
                cols = slice(tt * 512, (tt + 1) * 512)

                wd_pre = {}
                for m in range(NC):
                    par = m % 2
                    if m in gu_early:
                        gu_early.pop(m)
                        continue
                    if m in wgu_pre:
                        wld = wgu_pre.pop(m)
                    else:
                        wld = emit_wgu_load_gp(m)
                    gmm, umm = emit_gu_group(m, wld, apm5)
                    if m in (48, 56):
                        m2p = (m - 48) // 8
                        dlw = []
                        if d_mark[m2p % 2]:
                            dlw.append((sPE, d_mark[m2p % 2]))
                        wd_pre[m2p] = emit_wd_load(m2p, dlw)
                    emit_gu_evac(m, gmm, umm, ())
                h_all = cDV.v
                gu_last = cPE.v

                pend_d = None

                def flush_dstats(p):
                    (pm2, dvm, acm) = p
                    ppar = pm2 % 2
                    pw = [(sDV, dvm), (sAC, acm)]
                    if pm2 == 0 and tt > 0:
                        pw.append((sDV, prev_stats_fin5))
                    wt("pe", *pw)
                    mm(ps_a[:], ones_st[:], y2[:, pm2, :],
                       start=(pm2 == 0), stop=(pm2 == NK - 1))
                    mm(ps_b[:], ones_st[:], xq5[ppar][:],
                       start=(pm2 == 0), stop=(pm2 == NK - 1), ctr=cPE)
                    st5_mark[ppar] = cPE.v

                nap = apm5
                if tt < 3:
                    nap = {}
                    ncols = slice((tt + 1) * 512, (tt + 2) * 512)
                for m2 in range(NK):
                    par = m2 % 2
                    if tt < 3 and m2 < 8:
                        for k in (2 * m2, 2 * m2 + 1):
                            kp = k % 2
                            klw = [nap[k - 2]] if k >= 2 else [xapp_free]
                            dma(("sy", "ac")[kp],
                                (cSY, cAC16)[kp], xapp[kp][:],
                                x1_scr[k * 128:(k + 1) * 128, ncols],
                                waits=klw)
                            kld = ((sSY, cSY.v), (sAC16, cAC16.v))[kp]
                            wt("dv", kld, (sPE, gu_last))
                            em("dv", lambda e, o=xapp[kp][:],
                               r=r5_all[:, ncols]:
                               e.tensor_tensor(o, o, r, ALU.mult))
                            em("dv", lambda e, o=xn2[:, k, :], a=xapp[kp][:],
                               m=mrb5_all[:, ncols]:
                               e.tensor_tensor(o, a, m, ALU.subtract),
                               ctr=cDV)
                            nap[k] = (sDV, cDV.v)
                    if tt < 3 and m2 in (6, 7):
                        mpf = m2 - 6
                        wgu_pre[mpf] = emit_wgu_load_gp(mpf)
                    wld = wd_pre.pop(m2)
                    pw = [wld, (sDV, h_all)]
                    if y2_evac[par]:
                        pw.append((sDV, y2_evac[par]))
                    wt("pe", *pw)
                    for c in range(NC):
                        mm(ps_d[par][:], wdb[par][:, c * 128:(c + 1) * 128],
                           hbuf[:, c, :], start=(c == 0), stop=(c == NC - 1),
                           ctr=(cPE if c == NC - 1 else None))
                    d_mark[par] = cPE.v
                    if m2 + 2 < NK:
                        wd_pre[m2 + 2] = emit_wd_load(
                            m2 + 2, [(sPE, d_mark[par])])
                    if tt < 3 and m2 in (9, 10):
                        mpf = m2 - 9
                        gu_early[mpf] = emit_gu_group(
                            mpf, wgu_pre.pop(mpf), nap)
                    if tt < 3 and m2 in (11, 12):
                        wgu_pre[m2 - 9] = emit_wgu_load_gp(m2 - 9)
                    em("dv", lambda e, o=y2[:, m2, :], i=ps_d[par][:],
                       s=sd_sb[:, m2:m2 + 1]:
                       e.tensor_scalar(o, i, s, None, ALU.mult),
                       waits=[(sPE, d_mark[par]), (sSY, consts_ld)], ctr=cDV)
                    y2_evac[par] = cDV.v
                    em("ac", lambda e, o=xq5[par][:], i=y2[:, m2, :]:
                       e.activation(o, i, AF.Square),
                       waits=[(sDV, y2_evac[par])] + ([(sPE, st5_mark[par])]
                                                      if st5_mark[par] else []),
                       ctr=cAC)
                    if pend_d is not None:
                        flush_dstats(pend_d)
                    pend_d = (m2, y2_evac[par], cAC.v)
                flush_dstats(pend_d)
                pend_d = None
                d_last = cPE.v
                for mpf in sorted(gu_early):
                    gmm, umm = gu_early[mpf]
                    emit_gu_evac(mpf, gmm, umm,
                                 [(sPE, d_last), (sDV, h_mark[mpf % 2])])
                prev_stats_fin5 = stats_finish([ps_a], [ps_b], m_b5, var_b5,
                                               xapp[0], 512, cPE.v)
                r5c_ac = cAC.v
                for m2 in range(NK):
                    par = m2 % 2
                    rows = slice(m2 * 128, (m2 + 1) * 128)
                    aw = [(sAC, r5c_ac), (sDV, prev_stats_fin5),
                          xld_g[m2 // 4]]
                    wt("dv", *aw)
                    em("dv", lambda e, o=xapp[par][:], a=y2[:, m2, :],
                       r=var_b5[:]: e.tensor_tensor(o, a, r, ALU.mult))
                    em("dv", lambda e, o=x1sb[:, m2, :], m=m_b5[:]:
                       e.tensor_tensor(o, o, m, ALU.subtract))
                    em("dv", lambda e, o=x1sb[:, m2, :], a=xapp[par][:]:
                       e.tensor_tensor(o, o, a, ALU.add), ctr=cDV)
                    oeng, octr = (("sy", cSY), ("ac", cAC16))[m2 % 2]
                    dma(oeng, octr, out[rows, cols], x1sb[:, m2, :],
                        waits=[(sDV, cDV.v)])
                    x2_all[m2] = (octr.sem, octr.v)
                apm5 = nap
                xapp_free = (sDV, cDV.v)
                if tt < 3:
                    st_done = [(sSY, cSY.v), (sAC16, cAC16.v)]
                    xld_g = emit_x1sb_load(tt + 1, st_done)
            flush()

        _finalize(ops, wt, barrier, flush, ALLC)
    return nc


def _finalize(ops, wt, barrier, flush, ALLC):
    barrier()
    flush()


def _prep_inputs(L0_attn_x, L0_attn_Wq, L0_attn_Wk, L0_attn_Wv, L0_attn_Wo,
                 L0_ffn_w_gate, L0_ffn_w_up, L0_ffn_w_down,
                 L0_ffn_s_gate, L0_ffn_s_up, L0_ffn_s_down):
    c = np.ascontiguousarray
    wq = c(np.asarray(L0_attn_Wq, np.float32)
           .reshape(16, 128, 16, 128).transpose(2, 1, 0, 3)
           .reshape(16, 128, 2048))
    wk = c(np.asarray(L0_attn_Wk, np.float32)
           .reshape(16, 128, 16, 128).transpose(2, 1, 0, 3)
           .reshape(16, 128, 2048))
    wo = c(np.asarray(L0_attn_Wo, np.float32)
           .reshape(16, 128, 16, 128).transpose(2, 1, 0, 3)
           .reshape(16, 128, 2048))
    wv = c(np.asarray(L0_attn_Wv, np.float32))
    wg = (np.asarray(L0_ffn_w_gate, np.int8)
          .reshape(16, 128, 64, 128).transpose(2, 1, 0, 3)
          .reshape(64, 128, 2048))
    wu = (np.asarray(L0_ffn_w_up, np.int8)
          .reshape(16, 128, 64, 128).transpose(2, 1, 0, 3)
          .reshape(64, 128, 2048))
    wgu = c(np.stack([wg, wu], axis=1))
    wd = c(np.asarray(L0_ffn_w_down, np.int8)
           .reshape(64, 128, 16, 128).transpose(2, 1, 0, 3)
           .reshape(16, 128, 8192))
    sg = c(np.asarray(L0_ffn_s_gate, np.float32).reshape(64, 128).T)
    su = c(np.asarray(L0_ffn_s_up, np.float32).reshape(64, 128).T)
    sd = c(np.asarray(L0_ffn_s_down, np.float32).reshape(16, 128).T)
    tri = np.triu(np.ones((128, 128), np.float32))
    x = np.asarray(L0_attn_x, np.float32)
    shared = dict(wq_h=wq, wk_h=wk, wv=wv, wo_m=wo, wgu_m=wgu,
                  wd_m=wd, sg2d=sg, su2d=su, sd2d=sd, tri_in=tri)
    in_maps = []
    for b in range(8):
        m = dict(shared)
        m["xT"] = c(x[b].T)
        in_maps.append(m)
    return in_maps


_NC_CACHE = {}


def kernel(**inputs):
    in_maps = _prep_inputs(**inputs)
    if "full" not in _NC_CACHE:
        _NC_CACHE["full"] = build_nc("full")
    nc = _NC_CACHE["full"]
    res = run_bass_kernel_spmd(nc, in_maps, core_ids=list(range(8)))
    outs = [np.ascontiguousarray(np.asarray(r["out"], np.float32).T)
            for r in res.results]
    return np.stack(outs, axis=0)


# version 25
# speedup vs baseline: 1.2273x; 1.0192x over previous
import contextlib

import numpy as np
import concourse.bass as bass
import concourse.mybir as mybir
from concourse.bass_utils import run_bass_kernel_spmd

F32 = mybir.dt.float32
BF16 = mybir.dt.bfloat16
I8 = mybir.dt.int8
AF = mybir.ActivationFunctionType
ALU = mybir.AluOpType

D = 2048
T = 2048
H = 16
NK = D // 128
NC = 8192 // 128
SM_SCALE = float(1.0 / np.sqrt(2048.0))
EPS = 1e-5


class Ctr:
    def __init__(self, sem, step):
        self.sem = sem
        self.v = 0
        self.step = step


def build_nc(upto="full"):
    nc = bass.Bass("TRN2", target_bir_lowering=False)

    xT = nc.dram_tensor("xT", [D, T], F32, kind="ExternalInput").ap()
    wq_h = nc.dram_tensor("wq_h", [H, 128, D], F32, kind="ExternalInput").ap()
    wk_h = nc.dram_tensor("wk_h", [H, 128, D], F32, kind="ExternalInput").ap()
    wv = nc.dram_tensor("wv", [D, D], F32, kind="ExternalInput").ap()
    wo_m = nc.dram_tensor("wo_m", [NK, 128, D], F32, kind="ExternalInput").ap()
    wgu_m = nc.dram_tensor("wgu_m", [NC, 2, 128, D], I8,
                           kind="ExternalInput").ap()
    wd_m = nc.dram_tensor("wd_m", [NK, 128, 8192], I8, kind="ExternalInput").ap()
    sg2d = nc.dram_tensor("sg2d", [128, NC], F32, kind="ExternalInput").ap()
    su2d = nc.dram_tensor("su2d", [128, NC], F32, kind="ExternalInput").ap()
    sd2d = nc.dram_tensor("sd2d", [128, NK], F32, kind="ExternalInput").ap()
    tri_in = nc.dram_tensor("tri_in", [128, 128], F32, kind="ExternalInput").ap()

    out = nc.dram_tensor("out", [D, T], F32, kind="ExternalOutput").ap()

    x1_scr = nc.dram_tensor("x1_scr", [D, T], F32, kind="Internal").ap()
    wo_scr = nc.dram_tensor("wo_scr", [NK, 128, D], BF16, kind="Internal").ap()
    attn_scr = nc.dram_tensor("attn_scr", [H, 128, T], BF16, kind="Internal").ap()

    with contextlib.ExitStack() as es:
        tri = es.enter_context(nc.sbuf_tensor("tri", [128, 128], F32))
        ones_st = es.enter_context(nc.sbuf_tensor("ones_st", [128, 128], BF16))
        ones1 = es.enter_context(nc.sbuf_tensor("ones1", [128, 1], BF16))
        onesk1 = es.enter_context(nc.sbuf_tensor("onesk1", [1, 128], BF16))
        eps_t = es.enter_context(nc.sbuf_tensor("eps_t", [128, 1], F32))
        sg_sb = es.enter_context(nc.sbuf_tensor("sg_sb", [128, NC], F32))
        su_sb = es.enter_context(nc.sbuf_tensor("su_sb", [128, NC], F32))
        sd_sb = es.enter_context(nc.sbuf_tensor("sd_sb", [128, NK], F32))
        mx_sb = es.enter_context(nc.sbuf_tensor("mx_sb", [128, T], BF16))

        sGP = es.enter_context(nc.semaphore("sGP"))
        sSY = es.enter_context(nc.semaphore("sSY"))
        sPE = es.enter_context(nc.semaphore("sPE"))
        sDV = es.enter_context(nc.semaphore("sDV"))
        sAC = es.enter_context(nc.semaphore("sAC"))
        sAC16 = es.enter_context(nc.semaphore("sAC16"))
        sPL = es.enter_context(nc.semaphore("sPL"))
        cGP, cSY = Ctr(sGP, 16), Ctr(sSY, 16)
        cPE, cDV, cAC = Ctr(sPE, 1), Ctr(sDV, 1), Ctr(sAC, 1)
        cAC16, cPL = Ctr(sAC16, 16), Ctr(sPL, 1)
        ALLC = [cGP, cSY, cPE, cDV, cAC, cAC16, cPL]

        block = es.enter_context(nc.Block())

        ops = {"gp": [], "sy": [], "pe": [], "dv": [], "ac": []}

        def flush():
            if ops["gp"]:
                block.gpsimd(lambda e, fns=tuple(ops["gp"]): [f(e) for f in fns])
            if ops["sy"]:
                block.sync(lambda e, fns=tuple(ops["sy"]): [f(e) for f in fns])
            if ops["pe"]:
                block.tensor(lambda e, fns=tuple(ops["pe"]): [f(e) for f in fns])
            if ops["dv"]:
                block.vector(lambda e, fns=tuple(ops["dv"]): [f(e) for f in fns])
            if ops["ac"]:
                block.scalar(lambda e, fns=tuple(ops["ac"]): [f(e) for f in fns])
            for k in ops:
                ops[k] = []

        def _run(eng, waits, make, inc):
            for (sem, v) in waits:
                if v > 0:
                    eng.wait_ge(sem, v)
            ins = make(eng)
            if inc is not None:
                ins.then_inc(inc[0], inc[1])

        def em(engname, make, waits=(), ctr=None):
            inc = None
            if ctr is not None:
                ctr.v += ctr.step
                inc = (ctr.sem, ctr.step)
            ops[engname].append(
                lambda eng, w=tuple(waits), m=make, i=inc: _run(eng, w, m, i))

        def wt(engname, *waits):
            ops[engname].append(
                lambda eng, w=tuple(waits): [eng.wait_ge(s, v)
                                             for (s, v) in w if v > 0])

        def barrier():
            snap = [(c.sem, c.v) for c in ALLC]
            for e in ("gp", "sy", "pe", "dv", "ac"):
                wt(e, *snap)

        def dma(engname, ctr, dst, src, waits=()):
            em(engname, lambda e, d=dst, s=src: e.dma_start(out=d, in_=s),
               waits=waits, ctr=ctr)

        def warm(psum_t, rhs_ap, *waits):
            em("pe", lambda e, p=psum_t[0:64, 0:64], r=rhs_ap:
               e.matmul(p, ones_st[:, 0:64], r, start=True, stop=True,
                        skip_group_check=True), waits=list(waits))

        def mm(psum, lhsT, rhs, start, stop, waits=(), ctr=None):
            em("pe", lambda e, p=psum, l=lhsT, r=rhs, s0=start, s1=stop:
               e.matmul(p, l, r, start=s0, stop=s1, skip_group_check=True),
               waits=waits, ctr=ctr)

        def stats_finish(ps_sum, ps_sq, m_b, var_b, nmsq, w_each, pe_done,
                         save_m=None, extra=()):
            wt("dv", (sPE, pe_done), *extra)
            for tc in range(len(ps_sum)):
                sl = slice(tc * w_each, (tc + 1) * w_each)
                em("dv", lambda e, o=m_b[:, sl], i=ps_sum[tc][:]:
                   e.tensor_scalar(o, i, 1.0 / D, None, ALU.mult))
                em("dv", lambda e, o=var_b[:, sl], i=ps_sq[tc][:]:
                   e.tensor_scalar(o, i, 1.0 / D, None, ALU.mult))
            if save_m is not None:
                em("dv", lambda e, o=save_m, i=m_b[:]: e.tensor_copy(o, i))
            em("dv", lambda e, o=nmsq[:], a=m_b[:]:
               e.scalar_tensor_tensor(o, a, -1.0, a, ALU.mult, ALU.mult))
            em("dv", lambda e, o=var_b[:], a=nmsq[:]:
               e.tensor_tensor(o, o, a, ALU.add), ctr=cDV)
            wt("ac", (sDV, cDV.v))
            em("ac", lambda e, o=var_b[:]:
               e.activation(o, o, AF.Ln, bias=eps_t[:]))
            em("ac", lambda e, o=var_b[:]:
               e.activation(o, o, AF.Exp, scale=-0.5), ctr=cAC)
            wt("dv", (sAC, cAC.v))
            em("dv", lambda e, o=m_b[:], b=var_b[:]:
               e.tensor_tensor(o, o, b, ALU.mult), ctr=cDV)
            return cDV.v

        dma("sy", cSY, tri[:], tri_in)
        dma("sy", cSY, sg_sb[:], sg2d)
        dma("sy", cSY, su_sb[:], su2d)
        dma("sy", cSY, sd_sb[:], sd2d)
        consts_ld = cSY.v
        em("dv", lambda e: e.memset(ones_st[:], 1.0))
        em("dv", lambda e: e.memset(ones1[:], 1.0))
        em("dv", lambda e: e.memset(onesk1[:], 1.0))
        em("dv", lambda e: e.memset(eps_t[:], EPS), ctr=cDV)
        consts_dv = cDV.v

        p123 = es.enter_context(contextlib.ExitStack())
        xn1 = p123.enter_context(nc.sbuf_tensor("xn1", [128, NK, T], BF16))
        v_tm = p123.enter_context(nc.sbuf_tensor("v_tm", [128, NK, D], BF16))
        wqb = [p123.enter_context(nc.sbuf_tensor(f"wqb{i}", [128, D], BF16))
               for i in range(2)]
        wkb = [p123.enter_context(nc.sbuf_tensor(f"wkb{i}", [128, D], BF16))
               for i in range(2)]
        wostg = [p123.enter_context(
            nc.sbuf_tensor(f"wostg{i}", [128, D], BF16)) for i in range(2)]
        wv_ld = {}
        qk_ld = {}

        with contextlib.ExitStack() as p1:
            xsq = [p1.enter_context(nc.sbuf_tensor(f"xsq{i}", [128, T], BF16))
                   for i in range(2)]
            xh = [p1.enter_context(nc.sbuf_tensor(f"xh{i}", [128, 1024], F32))
                  for i in range(5)]
            m_b = p1.enter_context(nc.sbuf_tensor("m_b", [128, T], F32))
            var_b = p1.enter_context(nc.sbuf_tensor("var_b", [128, T], F32))
            ps_sum = [p1.enter_context(
                nc.psum_tensor(f"ps_sum{i}", [128, 512], F32)) for i in range(4)]
            ps_sq = [p1.enter_context(
                nc.psum_tensor(f"ps_sq{i}", [128, 512], F32)) for i in range(4)]

            ready = {}
            cp_mark = {i: 0 for i in range(5)}
            nhalf = {"sy": 0, "ac": 0}

            def ld_chunk(k):
                qn = "sy" if k < 9 else "ac"
                qc2, qs = (cSY, sSY) if qn == "sy" else (cAC16, sAC16)
                boff, nb = (0, 2) if qn == "sy" else (2, 3)
                for h2 in range(2):
                    b = boff + nhalf[qn] % nb
                    nhalf[qn] += 1
                    lw = [(sAC, cp_mark[b])] if cp_mark[b] else []
                    dma(qn, qc2, xh[b][:],
                        xT[k * 128:(k + 1) * 128, h2 * 1024:(h2 + 1) * 1024],
                        waits=lw)
                    em("ac", lambda e,
                       o=xn1[:, k, h2 * 1024:(h2 + 1) * 1024], i=xh[b][:]:
                       e.activation(o, i, AF.Copy),
                       waits=[(qs, qc2.v)], ctr=cAC)
                    cp_mark[b] = cAC.v
                ready[k] = (sAC, cAC.v)

            order = [0, 9, 1, 10, 2, 11, 3, 12, 4, 13, 5, 14, 6, 15, 7, 8]
            pe_k = {}
            for pos, k in enumerate(order):
                ld_chunk(k)
                wts = [ready[k], (sDV, consts_dv)]
                if pos >= 2:
                    wts.append((sPE, pe_k[order[pos - 2]]))
                em("dv", lambda e, o=xsq[pos % 2][:], i=xn1[:, k, :]:
                   e.tensor_tensor(o, i, i, ALU.mult), waits=wts, ctr=cDV)
                wt("pe", ready[k], (sDV, cDV.v))
                for tc in range(4):
                    sl = slice(tc * 512, (tc + 1) * 512)
                    mm(ps_sum[tc][:], ones_st[:], xn1[:, k, sl],
                       start=(pos == 0), stop=(pos == NK - 1))
                    mm(ps_sq[tc][:], ones_st[:], xsq[pos % 2][:, sl],
                       start=(pos == 0), stop=(pos == NK - 1),
                       ctr=(cPE if tc == 3 else None))
                pe_k[k] = cPE.v

            ln1_ready = stats_finish(ps_sum, ps_sq, m_b, var_b, xsq[0], 512,
                                     cPE.v, save_m=mx_sb[:])
            r_ready_ac = cAC.v

            ap_mark = []
            GPK = {2, 5, 8, 11, 14}
            for k in range(NK):
                isgp = k in GPK
                eng, ctr = (("gp", cPL) if isgp else ("dv", cDV))
                boff = 2 if isgp else 0
                wt(eng, (sAC, r_ready_ac), (sDV, ln1_ready),
                   (sPE, pe_k[k]))
                for h2 in range(2):
                    sl = slice(h2 * 1024, (h2 + 1) * 1024)
                    em(eng, lambda e, b=xh[boff + h2][:], a=xn1[:, k, sl],
                       r=var_b[:, sl]: e.tensor_tensor(b, a, r, ALU.mult))
                    em(eng, lambda e, o=xn1[:, k, sl], b=xh[boff + h2][:],
                       m=m_b[:, sl]: e.tensor_tensor(o, b, m, ALU.subtract),
                       ctr=(ctr if h2 == 1 else None))
                ap_mark.append((ctr.sem, ctr.v))
            flush()
        barrier()

        if upto == "ln1":
            for k in range(NK):
                dma("gp", cGP, out[k * 128:(k + 1) * 128, :], xn1[:, k, :])
            _finalize(ops, wt, barrier, flush, ALLC)
            return nc

        with contextlib.ExitStack() as p2:
            wvb = [p2.enter_context(
                nc.sbuf_tensor(f"wvb{i}", [128, NK, 512], BF16))
                for i in range(2)]
            ps_v = [p2.enter_context(
                nc.psum_tensor(f"ps_v{i}", [128, 512], F32)) for i in range(2)]
            evac = []
            vq_mark = []
            for q in range(4):
                cols = slice(q * 512, (q + 1) * 512)
                src2 = wv[:, cols].rearrange("(k p) n -> p k n", p=128)
                lw = [] if q < 2 else [(sPE, vq_mark[q - 2])]
                dma("gp", cGP, wvb[q % 2][:], src2, waits=lw)
                wld = cGP.v
                for st in range(NK):
                    j = q * NK + st
                    par = j % 2
                    wts = [(sGP, wld)]
                    if j >= 2:
                        wts.append((sAC, evac[j - 2]))
                    wt("pe", *wts)
                    for k in range(NK):
                        mm(ps_v[par][:],
                           xn1[:, k, st * 128:(st + 1) * 128],
                           wvb[q % 2][:, k, :],
                           start=(k == 0), stop=(k == NK - 1),
                           ctr=(cPE if k == NK - 1 else None))
                    dst = v_tm[:, st, q * 512:(q + 1) * 512]
                    em("ac", lambda e, o=dst, i=ps_v[par][:]:
                       e.activation(o, i, AF.Copy),
                       waits=[(sPE, cPE.v)], ctr=cAC)
                    evac.append(cAC.v)
                vq_mark.append(cPE.v)
            for h2 in range(2):
                dma("gp", cGP, wqb[h2][:], wq_h[h2])
                dma("gp", cGP, wkb[h2][:], wk_h[h2])
                qk_ld[h2] = cGP.v
            flush()
        barrier()

        with contextlib.ExitStack() as p3:
            qh = p3.enter_context(nc.sbuf_tensor("qh", [128, T], BF16))
            kh = p3.enter_context(nc.sbuf_tensor("kh", [128, T], BF16))
            at = p3.enter_context(nc.sbuf_tensor("at", [128, 16, 512], BF16))
            attn_h = [p3.enter_context(
                nc.sbuf_tensor(f"attn_h{i}", [128, T], BF16)) for i in range(2)]
            zt = p3.enter_context(nc.sbuf_tensor("zt", [1, 512], F32))
            cnt4 = p3.enter_context(nc.sbuf_tensor("cnt4", [1, 4], F32))
            for qc0 in range(4):
                em("dv", lambda e, o=cnt4[:, qc0:qc0 + 1],
                   v=float(2048 - 4 * (qc0 + 1) * 128): e.memset(o, v),
                   ctr=(cDV if qc0 == 3 else None))
            cnt_dv = cDV.v
            suf_sb = p3.enter_context(nc.sbuf_tensor("suf_sb", [128, 4], F32))
            zinv_bf = p3.enter_context(nc.sbuf_tensor("zinv_bf", [1, 512], BF16))
            zb_sb = p3.enter_context(nc.sbuf_tensor("zb_sb", [128, 512], BF16))
            ps_qk = [p3.enter_context(
                nc.psum_tensor(f"ps_qk{i}", [128, 512], F32)) for i in range(2)]
            ps_s = [p3.enter_context(
                nc.psum_tensor(f"ps_s{i}", [128, 512], F32)) for i in range(2)]
            ps_o = [p3.enter_context(
                nc.psum_tensor(f"ps_o{i}", [128, 512], F32)) for i in range(2)]
            ps_z = p3.enter_context(nc.psum_tensor("ps_z", [1, 512], F32))
            ps_zb = p3.enter_context(nc.psum_tensor("ps_zb", [128, 512], F32))

            qk_mm_done = {}
            scores_last = {-1: 0}
            exp_all = [0, 0]
            qk_evac = [0, 0]
            at_free = 0
            o_evac = [0, 0]
            z_ts_prev = 0
            zbmm_prev = 0
            attn_store = [0, 0]
            suf_read = 0
            suf_dv_prev = 0

            def emit_tail(pend):
                nonlocal zbmm_prev
                (o_par_p, q_sl_p, h_p, zinv_mark, qc_p, sufdv) = pend
                wt("pe", (sAC, zinv_mark), (sDV, o_evac[1 - o_par_p]))
                mm(ps_zb[:], onesk1[:], zinv_bf[:], start=True, stop=True,
                   ctr=cPE)
                zbmm_prev = cPE.v
                wts = [(sPE, zbmm_prev), (sSY, attn_store[h_p % 2])]
                if sufdv:
                    wts.append((sDV, sufdv))
                wt("dv", *wts)
                em("dv", lambda e, o=zb_sb[:], i=ps_zb[:]: e.tensor_copy(o, i))
                sufop = suf_sb[:, qc_p:qc_p + 1] if qc_p < 3 else 0.0
                em("dv", lambda e, o=attn_h[h_p % 2][:, q_sl_p],
                   i=ps_o[o_par_p][:], z2=zb_sb[:], s3=sufop:
                   e.scalar_tensor_tensor(o, i, s3, z2, ALU.add, ALU.mult),
                   ctr=cDV)
                o_evac[o_par_p] = cDV.v

            wo_stg_mark = [0, 0]
            for h in range(H):
                if h in qk_ld:
                    wld = qk_ld[h]
                else:
                    hw = [(sPE, qk_mm_done[h - 2])]
                    dma("gp", cGP, wqb[h % 2][:], wq_h[h], waits=hw)
                    dma("gp", cGP, wkb[h % 2][:], wk_h[h])
                    wld = cGP.v
                sw = [] if wo_stg_mark[h % 2] == 0 else                     [(sSY, wo_stg_mark[h % 2])]
                dma("gp", cGP, wostg[h % 2][:], wo_m[h], waits=sw)
                dma("sy", cSY, wo_scr[h], wostg[h % 2][:],
                    waits=[(sGP, cGP.v)])
                wo_stg_mark[h % 2] = cSY.v

                for which, (wbuf, dst) in enumerate(
                        ((wqb, qh), (wkb, kh))):
                    for tc in range(4):
                        j = which * 4 + tc
                        par = j % 2
                        wts = [(sGP, wld)]
                        if qk_evac[par]:
                            wts.append((sAC, qk_evac[par]))
                        if which == 0 and tc == 0 and suf_dv_prev:
                            wts.append((sDV, suf_dv_prev))
                        wt("pe", *wts)
                        for k in range(NK):
                            mm(ps_qk[par][:],
                               wqb[h % 2][:, k * 128:(k + 1) * 128] if which == 0
                               else wkb[h % 2][:, k * 128:(k + 1) * 128],
                               xn1[:, k, tc * 512:(tc + 1) * 512],
                               start=(k == 0), stop=(k == NK - 1),
                               ctr=(cPE if k == NK - 1 else None))
                        ewts = [(sPE, cPE.v), (sPE, scores_last[h - 1])]
                        em("ac", lambda e,
                           o=dst[:, tc * 512:(tc + 1) * 512],
                           i=ps_qk[par][:]: e.activation(o, i, AF.Copy),
                           waits=ewts, ctr=cAC)
                        qk_evac[par] = cAC.v
                qk_mm_done[h] = cPE.v
                qk_ready = cAC.v

                hs0 = slice(h * 128, (h + 1) * 128)
                wt("pe", (sAC, qk_evac[0]), (sAC, qk_evac[1]),
                   (sDV, suf_read))
                for c in range(4, 16):
                    mm(ps_qk[0][:, c - 4:c - 3], v_tm[:, c, hs0], ones1[:],
                       start=True, stop=True,
                       ctr=(cPE if c == 15 else None))
                wt("dv", (sPE, cPE.v))
                for j in range(3):
                    em("dv", lambda e, o=suf_sb[:, j:j + 1],
                       i=ps_qk[0][:, 4 * j:12]:
                       e.reduce_sum(o, i, axis=mybir.AxisListType.X),
                       ctr=(cDV if j == 2 else None))
                suf_dv = cDV.v
                suf_dv_prev = suf_dv

                pend = None
                for qc in range(4):
                    c_hi = 4 * (qc + 1)
                    q_sl = slice(qc * 512, (qc + 1) * 512)
                    o_par = qc % 2
                    hs = slice(h * 128, (h + 1) * 128)
                    expm = {}
                    msm = {}

                    def emit_av(cav):
                        avw = [(sAC, expm[cav])] if cav in expm else []
                        if cav in msm:
                            avw.append((sDV, msm[cav]))
                        if cav == 0:
                            avw.append((sDV, o_evac[o_par]))
                        if avw:
                            wt("pe", *avw)
                        mm(ps_o[o_par][:], v_tm[:, cav, hs], at[:, cav, :],
                           start=(cav == 0), stop=(cav == c_hi - 1),
                           ctr=(cPE if cav == c_hi - 1 else None))

                    def emit_z(cz):
                        if cz == 0:
                            wt("pe", (sAC, z_ts_prev))
                        mm(ps_z[:], ones1[:], at[:, cz, :], start=(cz == 0),
                           stop=(cz == c_hi - 1),
                           ctr=(cPE if cz == c_hi - 1 else None))

                    for c in range(c_hi):
                        par = c % 2
                        wts = [(sAC, exp_all[par])]
                        if qc == 0 and c == 0:
                            wts.append((sAC, qk_ready))
                        wt("pe", *wts)
                        mm(ps_s[par][:], kh[:, c * 128:(c + 1) * 128],
                           qh[:, q_sl], start=True, stop=True, ctr=cPE)
                        sc = cPE.v
                        if c == 2 and pend is not None:
                            emit_tail(pend)
                            pend = None
                        band = c >= 4 * qc
                        dmark = 0
                        j0 = 0
                        if band:
                            j0 = c * 128 - qc * 512
                            em("dv", lambda e,
                               p=ps_s[par][:, j0:j0 + 128], t2=tri[:]:
                               e.tensor_tensor(p, p, t2, ALU.mult),
                               waits=[(sPE, sc), (sSY, consts_ld)], ctr=cDV)
                            dmark = cDV.v
                            if j0 > 0:
                                em("dv", lambda e, o=at[:, c, 0:j0]:
                                   e.memset(o, 1.0),
                                   waits=[(sPE, at_free)], ctr=cDV)
                                msm[c] = cDV.v
                        ewts = [(sPE, sc), (sPE, at_free)]
                        if dmark:
                            ewts.append((sDV, dmark))
                        em("ac", lambda e, o=at[:, c, j0:512],
                           i=ps_s[par][:, j0:512]:
                           e.activation(o, i, AF.Exp, scale=SM_SCALE),
                           waits=ewts, ctr=cAC)
                        exp_all[par] = cAC.v
                        expm[c] = cAC.v
                        if c >= 2:
                            emit_av(c - 2)
                            emit_z(c - 2)
                    for cav in range(c_hi - 2, c_hi):
                        emit_av(cav)
                        emit_z(cav)
                    z_done = cPE.v
                    at_free = z_done
                    wt("ac", (sPE, z_done), (sPE, zbmm_prev),
                       (sDV, cnt_dv))
                    em("ac", lambda e, o=zt[:], i=ps_z[:],
                       b=cnt4[:, qc:qc + 1]:
                       e.activation(o, i, AF.Ln, bias=b))
                    em("ac", lambda e, o=zinv_bf[:], i=zt[:]:
                       e.activation(o, i, AF.Exp, scale=-1.0), ctr=cAC)
                    z_ts_prev = cAC.v
                    pend = (o_par, q_sl, h, cAC.v, qc, suf_dv)
                emit_tail(pend)
                pend = None
                suf_read = o_evac[1]
                scores_last[h] = cPE.v
                dma("sy", cSY, attn_scr[h], attn_h[h % 2][:],
                    waits=[(sDV, cDV.v)])
                attn_store[h % 2] = cSY.v
            flush()
        barrier()

        if upto == "attn":
            for hh in range(H):
                dma("gp", cGP, out[hh * 128:(hh + 1) * 128, :], attn_scr[hh])
            _finalize(ops, wt, barrier, flush, ALLC)
            return nc

        p123.close()

        p45 = es.enter_context(contextlib.ExitStack())
        r5_all = p45.enter_context(nc.sbuf_tensor("r5_all", [128, T], BF16))
        mrb5_all = p45.enter_context(nc.sbuf_tensor("mrb5_all", [128, T], BF16))

        with contextlib.ExitStack() as p4:
            attn_sb = p4.enter_context(
                nc.sbuf_tensor("attn_sb", [128, NK, T], BF16))
            y_half = p4.enter_context(
                nc.sbuf_tensor("y_half", [128, NK, 1024], F32))
            wob = [p4.enter_context(nc.sbuf_tensor(f"wob{i}", [128, D], BF16))
                   for i in range(2)]
            yb = [p4.enter_context(nc.sbuf_tensor(f"yb{i}", [128, 512], BF16))
                  for i in range(2)]
            ysq = [p4.enter_context(nc.sbuf_tensor(f"ysq{i}", [128, 512], BF16))
                   for i in range(2)]
            xf2 = [p4.enter_context(nc.sbuf_tensor(f"xf2{i}", [128, 1024], F32))
                   for i in range(4)]
            x1sq = [p4.enter_context(
                nc.sbuf_tensor(f"x1sq{i}", [128, 1024], BF16)) for i in range(2)]
            x1t = [p4.enter_context(nc.sbuf_tensor(f"x1t{i}", [128, 1024], F32))
                   for i in range(4)]
            m_b4 = p4.enter_context(nc.sbuf_tensor("m_b4", [128, 1024], F32))
            var_b4 = p4.enter_context(nc.sbuf_tensor("var_b4", [128, 1024], F32))
            var_b4x = p4.enter_context(
                nc.sbuf_tensor("var_b4x", [128, 1024], F32))
            nmsq4 = p4.enter_context(nc.sbuf_tensor("nmsq4", [128, 1024], F32))
            ps_y = [p4.enter_context(
                nc.psum_tensor(f"ps_y{i}", [128, 512], F32)) for i in range(2)]
            ps_s1 = [p4.enter_context(
                nc.psum_tensor(f"ps_s1{i}", [128, 512], F32)) for i in range(2)]
            ps_q1 = [p4.enter_context(
                nc.psum_tensor(f"ps_q1{i}", [128, 512], F32)) for i in range(2)]
            ps_x1q = [p4.enter_context(
                nc.psum_tensor(f"ps_x1q{i}", [128, 512], F32))
                for i in range(2)]


            half_src = attn_scr.rearrange("h p t -> p h t")

            def attn_ld(cb, hh, eng, ctr2):
                cs = slice(cb * 512, (cb + 1) * 512)
                dma(eng, ctr2, attn_sb[:, 8 * hh:8 * (hh + 1), cs],
                    half_src[:, 8 * hh:8 * (hh + 1), cs])
                return (ctr2.sem, ctr2.v)

            h0m = [attn_ld(cb, 0, "sy", cSY) for cb in range(4)]
            wo_pre = {}
            for m2p in range(2):
                dma("ac", cAC16, wob[m2p % 2][:], wo_scr[m2p])
                wo_pre[m2p] = (sAC16, cAC16.v)
            ald_cb = [[h0m[0], attn_ld(0, 1, "ac", cAC16)],
                      [h0m[1], attn_ld(1, 1, "ac", cAC16)], None, None]

            y_evac = [0, 0]
            stats_mark = [0, 0]
            wo_mark = {}
            apm_all = {}
            apm = [0, 0]
            x1dma = [(None, 0), (None, 0)]
            x1q_mark = [0, 0]
            half_fin = {}
            finx_mark = [0]
            pend_st = [None]
            pend_x1q = [None]

            def flush_ystats():
                (ptc, pm2, dvm, extra) = pend_st[0]
                pend_st[0] = None
                wt("pe", (sDV, dvm), *extra)
                mm(ps_s1[ptc][:], ones_st[:], yb[ptc][:],
                   start=(pm2 == 0), stop=(pm2 == NK - 1))
                mm(ps_q1[ptc][:], ones_st[:], ysq[ptc][:],
                   start=(pm2 == 0), stop=(pm2 == NK - 1), ctr=cPE)
                stats_mark[ptc] = cPE.v

            def flush_x1q():
                (pm2, dvm, extra) = pend_x1q[0]
                pend_x1q[0] = None
                ppar = pm2 % 2
                wt("pe", (sDV, dvm), *extra)
                for tc2 in range(2):
                    s2 = slice(tc2 * 512, (tc2 + 1) * 512)
                    mm(ps_x1q[tc2][:], ones_st[:], x1sq[ppar][:, s2],
                       start=(pm2 == 0), stop=(pm2 == NK - 1),
                       ctr=(cPE if tc2 == 1 else None))
                x1q_mark[ppar] = cPE.v

            def emit_wo_iter(half, m2, fin0dv):
                base = half * 1024
                gi = half * NK + m2
                if gi in wo_pre:
                    wldt = wo_pre.pop(gi)
                else:
                    hw = [(sPE, wo_mark.get(gi - 2, 0))]
                    dma("ac", cAC16, wob[m2 % 2][:], wo_scr[m2], waits=hw)
                    wldt = (sAC16, cAC16.v)
                for tc in range(2):
                    par = tc
                    wts = [wldt] + ald_cb[half * 2 + tc]
                    if y_evac[par]:
                        wts.append((sAC, y_evac[par]))
                    wt("pe", *wts)
                    for k in range(NK):
                        mm(ps_y[par][:],
                           wob[m2 % 2][:, k * 128:(k + 1) * 128],
                           attn_sb[:, k, base + tc * 512:
                                   base + (tc + 1) * 512],
                           start=(k == 0), stop=(k == NK - 1),
                           ctr=(cPE if k == NK - 1 else None))
                    y_sl = y_half[:, m2, tc * 512:(tc + 1) * 512]
                    ew = [(sPE, cPE.v)]
                    if half == 1:
                        ew.append(apm_all[m2])
                    em("ac", lambda e, o=y_sl, i=ps_y[par][:]:
                       e.activation(o, i, AF.Copy), waits=ew, ctr=cAC)
                    y_evac[par] = cAC.v
                    sw = [(sAC, y_evac[par])]
                    if stats_mark[par]:
                        sw.append((sPE, stats_mark[par]))
                    em("dv", lambda e, o=yb[par][:], i=y_sl:
                       e.tensor_copy(o, i), waits=sw, ctr=cDV)
                    em("dv", lambda e, o=ysq[par][:], i=y_sl:
                       e.tensor_tensor(o, i, i, ALU.mult), ctr=cDV)
                    if pend_st[0] is not None:
                        flush_ystats()
                    extra = ([(sDV, fin0dv)]
                             if half == 1 and m2 == 0 else [])
                    pend_st[0] = (tc, m2, cDV.v, extra)
                wo_mark[gi] = cPE.v

            chain_seq = [(h, m) for h in range(2) for m in range(NK)]
            xf2_ld = {}
            apm4 = [0, 0, 0, 0]
            x1dma4 = [0, 0, 0, 0]

            def emit_xf2_load(ci):
                (half, m2) = chain_seq[ci]
                bp = ci % 4
                rows = slice(m2 * 128, (m2 + 1) * 128)
                cols = slice(half * 1024, half * 1024 + 1024)
                lctr = cSY
                lw = [apm4[bp]] if apm4[bp] else []
                dma("sy", lctr, xf2[bp][:], xT[rows, cols], waits=lw)
                xf2_ld[ci] = (lctr.sem, lctr.v)

            def emit_chain(ci):
                (half, m2) = chain_seq[ci]
                bp = ci % 4
                par = m2 % 2
                ceng, cctr = (("gp", cPL) if m2 % 3 == 1 else ("dv", cDV))
                fin, r4a = half_fin[half]
                rows = slice(m2 * 128, (m2 + 1) * 128)
                cols = slice(half * 1024, half * 1024 + 1024)
                dwts = [xf2_ld[ci], (sAC, r4a), (sDV, fin)]
                if x1dma4[bp]:
                    dwts.append(x1dma4[bp])
                wt(ceng, *dwts)
                em(ceng, lambda e, o=y_half[:, m2, :], r=var_b4[:]:
                   e.tensor_tensor(o, o, r, ALU.mult))
                em(ceng, lambda e, o=xf2[bp][:], m=m_b4[:]:
                   e.tensor_tensor(o, o, m, ALU.subtract))
                em(ceng, lambda e, o=x1t[bp][:], a=y_half[:, m2, :],
                   b=xf2[bp][:]: e.tensor_tensor(o, a, b, ALU.add),
                   ctr=cctr)
                apm4[bp] = (cctr.sem, cctr.v)
                apm_all[m2] = apm4[bp]
                dma("sy", cSY, x1_scr[rows, cols], x1t[bp][:],
                    waits=[apm4[bp]])
                x1dma4[bp] = (sSY, cSY.v)
                if ci + 4 < len(chain_seq):
                    emit_xf2_load(ci + 4)
                xsw = [apm4[bp]]
                if x1q_mark[par]:
                    xsw.append((sPE, x1q_mark[par]))
                em("dv", lambda e, o=x1sq[par][:], i=x1t[bp][:]:
                   e.tensor_tensor(o, i, i, ALU.mult), waits=xsw, ctr=cDV)
                if pend_x1q[0] is not None:
                    flush_x1q()
                extra = ([(sDV, finx_mark[0])]
                         if half == 1 and m2 == 0 else [])
                pend_x1q[0] = (m2, cDV.v, extra)

            def finx(half):
                base = half * 1024
                sl_t = slice(base, base + 1024)
                mxs = mx_sb[:, sl_t]
                r5s = r5_all[:, sl_t]
                wt("dv", (sPE, cPE.v))
                for tc2 in range(2):
                    s2 = slice(tc2 * 512, (tc2 + 1) * 512)
                    em("dv", lambda e, o=var_b4x[:, s2], i=ps_x1q[tc2][:]:
                       e.tensor_scalar(o, i, 1.0 / D, None, ALU.mult))
                em("dv", lambda e, o=nmsq4[:], a=mxs:
                   e.scalar_tensor_tensor(o, a, -1.0, a, ALU.mult, ALU.mult))
                em("dv", lambda e, o=var_b4x[:], a=nmsq4[:]:
                   e.tensor_tensor(o, o, a, ALU.add), ctr=cDV)
                wt("ac", (sDV, cDV.v))
                em("ac", lambda e, o=var_b4x[:]:
                   e.activation(o, o, AF.Ln, bias=eps_t[:]))
                em("ac", lambda e, o=r5s, i=var_b4x[:]:
                   e.activation(o, i, AF.Exp, scale=-0.5), ctr=cAC)
                wt("dv", (sAC, cAC.v))
                em("dv", lambda e, o=mrb5_all[:, sl_t], a=mxs, b=r5s:
                   e.tensor_tensor(o, a, b, ALU.mult), ctr=cDV)
                finx_mark[0] = cDV.v

            for ci in range(4):
                emit_xf2_load(ci)
            for m2 in range(NK):
                emit_wo_iter(0, m2, 0)
                if m2 == 10:
                    ald_cb[2] = [h0m[2], attn_ld(2, 1, "ac", cAC16)]
                if m2 == 12:
                    ald_cb[3] = [h0m[3], attn_ld(3, 1, "ac", cAC16)]
            flush_ystats()
            fin0 = stats_finish(ps_s1, ps_q1, m_b4, var_b4, nmsq4, 512,
                                cPE.v)
            half_fin[0] = (fin0, cAC.v)
            emit_chain(0)
            emit_chain(1)
            for m2 in range(NK):
                emit_wo_iter(1, m2, fin0)
                if m2 + 2 < NK:
                    emit_chain(m2 + 2)
            flush_ystats()
            flush_x1q()
            fin1 = stats_finish(ps_s1, ps_q1, m_b4, var_b4, nmsq4, 512,
                                cPE.v, extra=[(sPL, cPL.v)])
            half_fin[1] = (fin1, cAC.v)
            finx(0)
            for ci in range(NK, 2 * NK):
                emit_chain(ci)
            flush_x1q()
            finx(1)
            flush()
        barrier()

        if upto == "x1":
            dma("sy", cSY, out[:, :], x1_scr[:, :])
            _finalize(ops, wt, barrier, flush, ALLC)
            return nc

        with contextlib.ExitStack() as p5:
            x1sb = p5.enter_context(nc.sbuf_tensor("x1sb", [128, NK, 512], F32))
            hbuf = p5.enter_context(nc.sbuf_tensor("hbuf", [128, NC, 512], BF16))
            y2 = p5.enter_context(nc.sbuf_tensor("y2", [128, NK, 512], BF16))
            xn2 = p5.enter_context(nc.sbuf_tensor("xn2", [128, NK, 512], BF16))
            xapp = [p5.enter_context(
                nc.sbuf_tensor(f"xapp{i}", [128, 512], F32)) for i in range(2)]
            wgub = [p5.enter_context(
                nc.sbuf_tensor(f"wgub{i}", [128, 2, D], BF16))
                for i in range(2)]
            wdb = [p5.enter_context(nc.sbuf_tensor(f"wdb{i}", [128, 8192], BF16))
                   for i in range(2)]
            sg_t = [p5.enter_context(nc.sbuf_tensor(f"sg_t{i}", [128, 512], BF16))
                    for i in range(2)]
            xq5 = [p5.enter_context(nc.sbuf_tensor(f"xq5{i}", [128, 512], BF16))
                   for i in range(2)]
            m_b5 = p5.enter_context(nc.sbuf_tensor("m_b5", [128, 512], F32))
            var_b5 = p5.enter_context(nc.sbuf_tensor("var_b5", [128, 512], F32))
            ps_a = p5.enter_context(nc.psum_tensor("ps_a", [128, 512], F32))
            ps_b = p5.enter_context(nc.psum_tensor("ps_b", [128, 512], F32))
            ps_g = [p5.enter_context(
                nc.psum_tensor(f"ps_g{i}", [128, 512], F32)) for i in range(2)]
            ps_u = [p5.enter_context(
                nc.psum_tensor(f"ps_u{i}", [128, 512], F32)) for i in range(2)]
            ps_d = [p5.enter_context(
                nc.psum_tensor(f"ps_d{i}", [128, 512], F32)) for i in range(2)]

            x2_all = {}
            pend_x2 = [None]
            gu_mark = [0, 0]
            h_mark = [0, 0]
            d_mark = [0, 0]
            y2_evac = [0, 0]
            st5_mark = [0, 0]
            apm5 = {}
            xld_g = []

            def emit_x1sb_load(dst_tt, gw):
                res = []
                cols2 = slice(dst_tt * 512, (dst_tt + 1) * 512)
                for g in range(4):
                    gsrc = x1_scr[g * 512:(g + 1) * 512, cols2].rearrange(
                        "(k p) t -> p k t", p=128)
                    dma("sy", cSY, x1sb[:, 4 * g:4 * (g + 1), :], gsrc,
                        waits=gw)
                    res.append((sSY, cSY.v))
                return res

            xld_g = emit_x1sb_load(0, [])
            ap5 = [0, 0]
            for k in range(NK):
                par = k % 2
                aw = [xld_g[k // 4]]
                if ap5[par]:
                    aw.append((sDV, ap5[par]))
                wt("dv", *aw)
                em("dv", lambda e, o=xapp[par][:], a=x1sb[:, k, :],
                   r=r5_all[:, 0:512]: e.tensor_tensor(o, a, r, ALU.mult))
                em("dv", lambda e, o=xn2[:, k, :], a=xapp[par][:],
                   m=mrb5_all[:, 0:512]: e.tensor_tensor(o, a, m, ALU.subtract),
                   ctr=cDV)
                ap5[par] = cDV.v
                apm5[k] = (sDV, cDV.v)
            xapp_free = [(sDV, cDV.v)]

            wgu_pre = {}
            gu_early = {}

            def emit_gu_group(m, wld, apm_map):
                par = m % 2
                pw = [wld]
                if m >= 2:
                    pw.append(apm_map[NK - 1])
                if h_mark[par]:
                    pw.append((sDV, h_mark[par]))
                wt("pe", *pw)
                for k in range(NK):
                    kw = [apm_map[k]] if m < 2 else ()
                    mm(ps_g[par][:], wgub[par][:, 0, k * 128:(k + 1) * 128],
                       xn2[:, k, :], start=(k == 0), stop=(k == NK - 1),
                       waits=kw, ctr=(cPE if k == NK - 1 else None))
                gmm = cPE.v
                for k in range(NK):
                    mm(ps_u[par][:], wgub[par][:, 1, k * 128:(k + 1) * 128],
                       xn2[:, k, :], start=(k == 0), stop=(k == NK - 1),
                       ctr=(cPE if k == NK - 1 else None))
                umm = cPE.v
                gu_mark[par] = cPE.v
                return gmm, umm

            def emit_wd_load(m2, dlw):
                par = m2 % 2
                dma("gp", cGP, wdb[par][:], wd_m[m2], waits=dlw)
                return (sGP, cGP.v)

            def emit_wgu_load_gp(m):
                par = m % 2
                lw = []
                if gu_mark[par]:
                    lw.append((sPE, gu_mark[par]))
                dma("gp", cGP, wgub[par][:],
                    wgu_m[m].rearrange("a p f -> p a f"), waits=lw)
                return (sGP, cGP.v)

            def emit_gu_evac(m, gmm, umm, extra):
                par = m % 2
                em("ac", lambda e, o=sg_t[par][:], i=ps_g[par][:],
                   s=sg_sb[:, m:m + 1]:
                   e.activation(o, i, AF.Silu, scale=s),
                   waits=[(sPE, gmm), (sSY, consts_ld)] + list(extra),
                   ctr=cAC)
                em("dv", lambda e, o=hbuf[:, m, :], i=ps_u[par][:],
                   s=su_sb[:, m:m + 1], g=sg_t[par][:]:
                   e.scalar_tensor_tensor(o, i, s, g, ALU.mult, ALU.mult),
                   waits=[(sPE, umm), (sAC, cAC.v)] + list(extra), ctr=cDV)
                h_mark[par] = cDV.v

            def emit_x2_step(m2):
                (src_tt, fin5, r5c, xld) = pend_x2[0]
                scols = slice(src_tt * 512, (src_tt + 1) * 512)
                par = m2 % 2
                rows = slice(m2 * 128, (m2 + 1) * 128)
                wt("dv", (sAC, r5c), (sDV, fin5), xld[m2 // 4])
                em("dv", lambda e, o=xapp[par][:], a=y2[:, m2, :],
                   r=var_b5[:]: e.tensor_tensor(o, a, r, ALU.mult))
                em("dv", lambda e, o=x1sb[:, m2, :], m=m_b5[:]:
                   e.tensor_tensor(o, o, m, ALU.subtract))
                em("dv", lambda e, o=x1sb[:, m2, :], a=xapp[par][:]:
                   e.tensor_tensor(o, o, a, ALU.add), ctr=cDV)
                if src_tt == 3 and m2 % 2 == 1:
                    dma("gp", cGP, out[rows, scols], x1sb[:, m2, :],
                        waits=[(sDV, cDV.v)])
                    x2_all[m2] = (sGP, cGP.v)
                else:
                    dma("sy", cSY, out[rows, scols], x1sb[:, m2, :],
                        waits=[(sDV, cDV.v)])
                    x2_all[m2] = (sSY, cSY.v)

            for tt in range(4):
                cols = slice(tt * 512, (tt + 1) * 512)

                wd_pre = {}
                for m in range(NC):
                    par = m % 2
                    if m in gu_early:
                        gu_early.pop(m)
                        continue
                    if m in wgu_pre:
                        wld = wgu_pre.pop(m)
                    else:
                        wld = emit_wgu_load_gp(m)
                    gmm, umm = emit_gu_group(m, wld, apm5)
                    if m in (48, 56):
                        m2p = (m - 48) // 8
                        dlw = []
                        if d_mark[m2p % 2]:
                            dlw.append((sPE, d_mark[m2p % 2]))
                        wd_pre[m2p] = emit_wd_load(m2p, dlw)
                    emit_gu_evac(m, gmm, umm, ())
                    if pend_x2[0] is not None and 2 <= m <= 17:
                        emit_x2_step(m - 2)
                        if m == 17:
                            pend_x2[0] = None
                            xapp_free = [(sDV, cDV.v)]
                    if m == 18 and tt >= 1:
                        xld_g = emit_x1sb_load(tt, [(sSY, cSY.v)])
                h_all = cDV.v
                gu_last = cPE.v

                pend_d = None

                def flush_dstats(p):
                    (pm2, dvm, acm) = p
                    ppar = pm2 % 2
                    pw = [(sDV, dvm), (sAC, acm)]
                    if pm2 == 0 and tt > 0:
                        pw.append((sDV, prev_stats_fin5))
                    wt("pe", *pw)
                    mm(ps_a[:], ones_st[:], y2[:, pm2, :],
                       start=(pm2 == 0), stop=(pm2 == NK - 1))
                    mm(ps_b[:], ones_st[:], xq5[ppar][:],
                       start=(pm2 == 0), stop=(pm2 == NK - 1), ctr=cPE)
                    st5_mark[ppar] = cPE.v

                nap = apm5
                if tt < 3:
                    nap = {}
                    ncols = slice((tt + 1) * 512, (tt + 2) * 512)
                for m2 in range(NK):
                    par = m2 % 2
                    if tt < 3 and m2 < 8:
                        for k in (2 * m2, 2 * m2 + 1):
                            kp = k % 2
                            klw = [nap[k - 2]] if k >= 2 else list(xapp_free)
                            dma("sy", cSY, xapp[kp][:],
                                x1_scr[k * 128:(k + 1) * 128, ncols],
                                waits=klw)
                            kld = (sSY, cSY.v)
                            wt("dv", kld, (sPE, gu_last))
                            em("dv", lambda e, o=xapp[kp][:],
                               r=r5_all[:, ncols]:
                               e.tensor_tensor(o, o, r, ALU.mult))
                            em("dv", lambda e, o=xn2[:, k, :], a=xapp[kp][:],
                               m=mrb5_all[:, ncols]:
                               e.tensor_tensor(o, a, m, ALU.subtract),
                               ctr=cDV)
                            nap[k] = (sDV, cDV.v)
                    if tt < 3 and m2 in (6, 7):
                        mpf = m2 - 6
                        wgu_pre[mpf] = emit_wgu_load_gp(mpf)
                    wld = wd_pre.pop(m2)
                    pw = [wld, (sDV, h_all)]
                    if y2_evac[par]:
                        pw.append((sDV, y2_evac[par]))
                    wt("pe", *pw)
                    for c in range(NC):
                        mm(ps_d[par][:], wdb[par][:, c * 128:(c + 1) * 128],
                           hbuf[:, c, :], start=(c == 0), stop=(c == NC - 1),
                           ctr=(cPE if c == NC - 1 else None))
                    d_mark[par] = cPE.v
                    if m2 + 2 < NK:
                        wd_pre[m2 + 2] = emit_wd_load(
                            m2 + 2, [(sPE, d_mark[par])])
                    if tt < 3 and m2 in (9, 10):
                        mpf = m2 - 9
                        gu_early[mpf] = emit_gu_group(
                            mpf, wgu_pre.pop(mpf), nap)
                    if tt < 3 and m2 in (11, 12):
                        wgu_pre[m2 - 9] = emit_wgu_load_gp(m2 - 9)
                    em("dv", lambda e, o=y2[:, m2, :], i=ps_d[par][:],
                       s=sd_sb[:, m2:m2 + 1]:
                       e.tensor_scalar(o, i, s, None, ALU.mult),
                       waits=[(sPE, d_mark[par]), (sSY, consts_ld)], ctr=cDV)
                    y2_evac[par] = cDV.v
                    em("ac", lambda e, o=xq5[par][:], i=y2[:, m2, :]:
                       e.activation(o, i, AF.Square),
                       waits=[(sDV, y2_evac[par])] + ([(sPE, st5_mark[par])]
                                                      if st5_mark[par] else []),
                       ctr=cAC)
                    if pend_d is not None:
                        flush_dstats(pend_d)
                    pend_d = (m2, y2_evac[par], cAC.v)
                flush_dstats(pend_d)
                pend_d = None
                d_last = cPE.v
                for mpf in sorted(gu_early):
                    gmm, umm = gu_early[mpf]
                    emit_gu_evac(mpf, gmm, umm,
                                 [(sPE, d_last), (sDV, h_mark[mpf % 2])])
                prev_stats_fin5 = stats_finish([ps_a], [ps_b], m_b5, var_b5,
                                               xapp[0], 512, cPE.v)
                pend_x2[0] = (tt, prev_stats_fin5, cAC.v, xld_g)
                apm5 = nap
                if tt == 3:
                    for m2 in range(NK):
                        emit_x2_step(m2)
                    pend_x2[0] = None
            flush()

        _finalize(ops, wt, barrier, flush, ALLC)
    return nc


def _finalize(ops, wt, barrier, flush, ALLC):
    barrier()
    flush()


def _prep_inputs(L0_attn_x, L0_attn_Wq, L0_attn_Wk, L0_attn_Wv, L0_attn_Wo,
                 L0_ffn_w_gate, L0_ffn_w_up, L0_ffn_w_down,
                 L0_ffn_s_gate, L0_ffn_s_up, L0_ffn_s_down):
    c = np.ascontiguousarray
    wq = c(np.asarray(L0_attn_Wq, np.float32)
           .reshape(16, 128, 16, 128).transpose(2, 1, 0, 3)
           .reshape(16, 128, 2048))
    wk = c(np.asarray(L0_attn_Wk, np.float32)
           .reshape(16, 128, 16, 128).transpose(2, 1, 0, 3)
           .reshape(16, 128, 2048))
    wo = c(np.asarray(L0_attn_Wo, np.float32)
           .reshape(16, 128, 16, 128).transpose(2, 1, 0, 3)
           .reshape(16, 128, 2048))
    wv = c(np.asarray(L0_attn_Wv, np.float32))
    wg = (np.asarray(L0_ffn_w_gate, np.int8)
          .reshape(16, 128, 64, 128).transpose(2, 1, 0, 3)
          .reshape(64, 128, 2048))
    wu = (np.asarray(L0_ffn_w_up, np.int8)
          .reshape(16, 128, 64, 128).transpose(2, 1, 0, 3)
          .reshape(64, 128, 2048))
    wgu = c(np.stack([wg, wu], axis=1))
    wd = c(np.asarray(L0_ffn_w_down, np.int8)
           .reshape(64, 128, 16, 128).transpose(2, 1, 0, 3)
           .reshape(16, 128, 8192))
    sg = c(np.asarray(L0_ffn_s_gate, np.float32).reshape(64, 128).T)
    su = c(np.asarray(L0_ffn_s_up, np.float32).reshape(64, 128).T)
    sd = c(np.asarray(L0_ffn_s_down, np.float32).reshape(16, 128).T)
    tri = np.triu(np.ones((128, 128), np.float32))
    x = np.asarray(L0_attn_x, np.float32)
    shared = dict(wq_h=wq, wk_h=wk, wv=wv, wo_m=wo, wgu_m=wgu,
                  wd_m=wd, sg2d=sg, su2d=su, sd2d=sd, tri_in=tri)
    in_maps = []
    for b in range(8):
        m = dict(shared)
        m["xT"] = c(x[b].T)
        in_maps.append(m)
    return in_maps


_NC_CACHE = {}


def kernel(**inputs):
    in_maps = _prep_inputs(**inputs)
    if "full" not in _NC_CACHE:
        _NC_CACHE["full"] = build_nc("full")
    nc = _NC_CACHE["full"]
    res = run_bass_kernel_spmd(nc, in_maps, core_ids=list(range(8)))
    outs = [np.ascontiguousarray(np.asarray(r["out"], np.float32).T)
            for r in res.results]
    return np.stack(outs, axis=0)
